# version 08 (final)
# baseline (speedup 1.0000x reference)
import contextlib
import math
import numpy as np
import ml_dtypes
import concourse.bass as bass
import concourse.mybir as mybir
from concourse.bass_utils import run_bass_kernel_spmd

F32 = mybir.dt.float32
BF16 = mybir.dt.bfloat16
I32 = mybir.dt.int32
U8 = mybir.dt.uint8
AF = mybir.ActivationFunctionType
ALU = mybir.AluOpType
AXX = mybir.AxisListType.X
BF = ml_dtypes.bfloat16

NCORES = 8
B = 2
T = 16384
D = 1024


ENGINES = ("tensor", "vector", "scalar", "gpsimd", "sync")


class Prog:

    def __init__(self, name="prog"):
        self.nc = bass.Bass("TRN2", target_bir_lowering=False)
        self.es = contextlib.ExitStack()
        self.block = None
        self.semh = {}
        self.semc = {}
        for s in ("pe", "act", "dve", "pool", "sp"):
            self.semh[s] = self.es.enter_context(self.nc.semaphore(s))
            self.semc[s] = 0
        self.dma_pool = {"sync": [], "gpsimd": []}
        self.dma_rr = {"sync": 0, "gpsimd": 0}
        for q, pfx in (("sync", "sq"), ("gpsimd", "gq")):
            for i in range(4):
                s = f"{pfx}{i}"
                self.semh[s] = self.es.enter_context(self.nc.semaphore(s))
                self.semc[s] = 0
                self.dma_pool[q].append([s, None])
        self.floor = {e: {} for e in ENGINES}
        self.eng_sem = {"tensor": "pe", "vector": "dve", "scalar": "act",
                        "gpsimd": "pool", "sync": "sp"}
        self._sb_bytes = 0
        self._cmap = {}
        self._clist = []
        self._scope_es = None
        self.cst_sb = None
        self.cst_ticket = None
        self.pfx = ""

    def start_block(self):
        self.cst_dram = self.dram("cst", [128, 128], F32,
                                  kind="ExternalInput")
        self.cst_sb = self.sbuf("cst_sb", [128, 128], F32)
        self.block = self.es.enter_context(self.nc.Block())
        self.cst_ticket = self.dma(
            "sync", lambda eng: [eng.dma_start(out=self.cst_sb[:, :],
                                               in_=self.cst_dram)],
            tag="cst_ld")

    def cval(self, v):
        v = float(v)
        if v not in self._cmap:
            assert len(self._clist) < 128
            self._cmap[v] = len(self._clist)
            self._clist.append(v)
        return self.cst_sb[:, self._cmap[v]:self._cmap[v] + 1]

    def cst_array(self):
        a = np.zeros((128, 128), np.float32)
        for i, v in enumerate(self._clist):
            a[:, i] = v
        return a

    def close(self):
        self.es.close()

    def sbuf(self, name, shape, dt):
        es = self._scope_es if self._scope_es is not None else self.es
        return es.enter_context(
            self.nc.sbuf_tensor(self.pfx + name, shape, dt))

    @contextlib.contextmanager
    def scope(self):
        prev = self._scope_es
        self._scope_es = contextlib.ExitStack()
        try:
            yield
        finally:
            self._scope_es.close()
            self._scope_es = prev

    def fence(self):
        return [(s, v) for s, v in self.semc.items() if v > 0]

    def psum(self, name, shape, dt=F32):
        es = self._scope_es if self._scope_es is not None else self.es
        return es.enter_context(
            self.nc.psum_tensor(self.pfx + name, shape, dt))

    def dram(self, name, shape, dt, kind="Internal", addr_space="Local"):
        return self.nc.dram_tensor(self.pfx + name, shape, dt, kind=kind,
                                   addr_space=addr_space).ap()

    def _emit_waits(self, eng, engname, deps):
        own = self.eng_sem[engname]
        best = {}
        for (s, v) in deps:
            if s == own and engname == "tensor":
                continue
            if self.floor[engname].get(s, -1) >= v:
                continue
            if best.get(s, -1) < v:
                best[s] = v
        for s, v in best.items():
            eng.wait_ge(self.semh[s], v)
            self.floor[engname][s] = v

    def step(self, engname, fn, deps=(), tag=None):
        sem = self.eng_sem[engname]
        deps = tuple(deps) + ((self.cst_ticket,) if self.cst_ticket else ())
        self.semc[sem] += 1
        val = self.semc[sem]

        def section(eng):
            self._emit_waits(eng, engname, deps)
            last = fn(eng)
            assert last is not None, f"step {tag}: fn returned None"
            last.then_inc(self.semh[sem], 1)
        getattr(self.block, engname)(section)
        return (sem, val)

    def dma(self, queue, fn, deps=(), tag=None):
        slot = self.dma_pool[queue][self.dma_rr[queue]]
        self.dma_rr[queue] = (self.dma_rr[queue] + 1) % len(self.dma_pool[queue])
        qsem, pending = slot
        deps = tuple(deps) + ((pending,) if pending else ()) \
            + ((self.cst_ticket,) if self.cst_ticket else ())
        holder = {}

        def section(eng):
            self._emit_waits(eng, queue, deps)
            insts = fn(eng)
            assert insts, f"dma step {tag}: no instructions"
            for i in insts:
                i.then_inc(self.semh[qsem], 16)
            holder["n"] = len(insts)
        getattr(self.block, queue)(section)
        self.semc[qsem] += 16 * holder["n"]
        ticket = (qsem, self.semc[qsem])
        slot[1] = ticket
        return ticket

    def red(self, tickets):
        best = {}
        for (s, v) in tickets:
            if best.get(s, -1) < v:
                best[s] = v
        return [(s, v) for s, v in best.items()]

    def chain(self, engname, fns, deps=(), tag=None):
        t = None
        for i, fn in enumerate(fns):
            d = list(deps) if i == 0 else [t]
            t = self.step(engname, fn, deps=d, tag=f"{tag}_{i}")
        return t

    def barrier_all(self, deps):
        out = []
        for e in ENGINES:
            out.append(self.step(e, lambda eng: eng.nop(), deps=deps,
                                 tag="barrier"))
        return out


def ceil_div(a, b):
    return (a + b - 1) // b



def bcast_row(P, row_ap_bf16, out_sbuf_bf16, ones1_bf, psbank, nfree, deps,
              evac_engine="scalar"):
    t_mm = P.step("tensor", lambda t: t.matmul(
        psbank[:, 0:nfree], ones1_bf[0:1, :], row_ap_bf16,
        start=True, stop=True), deps=deps, tag="bcast_mm")
    if evac_engine == "scalar":
        t_ev = P.step("scalar", lambda sc: sc.copy(
            out=out_sbuf_bf16[:, 0:nfree], in_=psbank[:, 0:nfree]),
            deps=[t_mm], tag="bcast_ev")
    else:
        t_ev = P.step("vector", lambda v: v.tensor_copy(
            out_sbuf_bf16[:, 0:nfree], psbank[:, 0:nfree]),
            deps=[t_mm], tag="bcast_ev")
    return t_ev



def fm(x):
    tkn, feat = x.shape
    return np.ascontiguousarray(x.T.reshape(feat // 128, 128, tkn))


def wblk(w):
    K, M = w.shape
    if M >= 128:
        assert M % 128 == 0
        return np.ascontiguousarray(
            w.reshape(K // 128, 128, M // 128, 128).transpose(2, 0, 1, 3))
    return np.ascontiguousarray(
        w.reshape(K // 128, 128, M).transpose(1, 0, 2).reshape(128, -1))


def col128(v):
    return np.ascontiguousarray(v.reshape(-1, 128, 1))



def stream_matmul(P, *, tag, x_hbm, w_hbm, out_hbm, KB, M, TLOC, TP=None,
                  psb, act=AF.Identity, bias_sb=None, scale_sb=None,
                  out_dt=BF16, deps=(), x_sb=None, extra_evac=None,
                  store_out=True):
    if TP is None:
        TP = TLOC
    TP = min(TP, 1024)
    NPAN = TLOC // TP
    NT = TP // 512
    MB = ceil_div(M, 128)
    NG = max(1, len(psb) // NT)
    groups = [psb[g * NT:(g + 1) * NT] for g in range(NG)]
    assert len(psb) >= NT, (tag, len(psb), NT)
    own_x = x_sb is None
    stream_x = own_x and (KB * TP * 2 > 100000)
    if own_x and not stream_x:
        x_sb = P.sbuf(f"{tag}_x", [128, KB * TP], BF16)
    elif stream_x:
        xr = [P.sbuf(f"{tag}_xr{i}", [128, TP], BF16) for i in range(2)]
    w_sb = [P.sbuf(f"{tag}_w{i}", [128, KB * 128], BF16) for i in range(2)]
    ev_sb = [P.sbuf(f"{tag}_ev{i}", [128, 512], out_dt) for i in range(6)]
    stores = []
    deps = list(deps)

    iters = [(pan, m) for pan in range(NPAN) for m in range(MB)]
    w_tk = [None, None]
    w_gate = [list(deps), list(deps)]
    ev_gate = [list(deps)] * 6
    ps_gates = [list(deps) for _ in range(NG)]
    x_tk = None
    pan_last_mm = None
    prev_pan = -1

    for i in range(min(2, len(iters))):
        pm, mm_ = iters[i]
        ms0 = min(128, M - mm_ * 128)
        def wload(eng, m=mm_, buf=i, ms0=ms0):
            if M < 128:
                return [eng.dma_start(out=w_sb[buf][:, 0:KB * ms0],
                                      in_=w_hbm)]
            out = []
            for k0 in range(0, KB, 32):
                k1 = min(KB, k0 + 32)
                out.append(eng.dma_start(
                    out=w_sb[buf][:, k0 * ms0:k1 * ms0].rearrange(
                        "p (k m) -> p k m", k=k1 - k0),
                    in_=w_hbm[m, k0:k1].rearrange("k p m -> p k m")))
            return out
        w_tk[i] = P.dma("gpsimd", wload, deps=deps, tag=f"{tag}_wld_i{i}")

    for it, (pan, m) in enumerate(iters):
        ms = min(128, M - m * 128)
        buf = it % 2
        t_base = pan * TP
        if own_x and not stream_x and pan != prev_pan:
            xdeps = deps if pan == 0 else [pan_last_mm]
            def xload(eng, t_base=t_base):
                return [eng.dma_start(
                    out=x_sb[:, k * TP:(k + 1) * TP],
                    in_=x_hbm[k, :, t_base:t_base + TP]) for k in range(KB)]
            x_tk = P.dma("sync", xload, deps=xdeps, tag=f"{tag}_xld{pan}")
            prev_pan = pan
        grp = it % NG
        psbg = groups[grp]
        mdeps = [w_tk[buf]] + ps_gates[grp]
        if x_tk is not None:
            mdeps.append(x_tk)

        if stream_x:
            xg = [list(deps), list(deps)]
            mm_t = None
            xtks = [None, None]
            for k in range(KB):
                kb = k % 2
                xtks[kb] = P.dma(
                    "sync", lambda eng, k=k, kb=kb, t_base=t_base: [
                        eng.dma_start(out=xr[kb][:, :],
                                      in_=x_hbm[k, :, t_base:t_base + TP])],
                    deps=xg[kb], tag=f"{tag}_xs{it}_{k}")

                def mmk(eng, k=k, kb=kb, ms=ms, buf=buf, psbg=psbg):
                    last = None
                    wv = w_sb[buf][:, k * ms:(k + 1) * ms]
                    for t in range(NT):
                        xv = xr[kb][:, t * 512:(t + 1) * 512]
                        last = eng.matmul(psbg[t][0:ms, :], wv, xv,
                                          start=(k == 0), stop=(k == KB - 1))
                    return last
                mm_t = P.step("tensor", mmk,
                              deps=mdeps + [xtks[kb]], tag=f"{tag}_mk{it}_{k}")
                xg[kb] = [mm_t]
        else:
            def mms(eng, m=m, ms=ms, buf=buf, psbg=psbg):
                last = None
                for k in range(KB):
                    wv = w_sb[buf][:, k * ms:(k + 1) * ms]
                    for t in range(NT):
                        xv = x_sb[:, k * TP + t * 512: k * TP + (t + 1) * 512]
                        last = eng.matmul(psbg[t][0:ms, :], wv, xv,
                                          start=(k == 0), stop=(k == KB - 1))
                return last
            mm_t = P.step("tensor", mms, deps=mdeps, tag=f"{tag}_mm{pan}_{m}")
        pan_last_mm = mm_t
        w_gate[buf] = [mm_t]
        if it + 2 < len(iters):
            nm = iters[it + 2][1]
            nms = min(128, M - nm * 128)
            def wload2(eng, nm=nm, buf=buf, nms=nms):
                if M < 128:
                    return [eng.dma_start(out=w_sb[buf][:, 0:KB * nms],
                                          in_=w_hbm)]
                out = []
                for k0 in range(0, KB, 32):
                    k1 = min(KB, k0 + 32)
                    out.append(eng.dma_start(
                        out=w_sb[buf][:, k0 * nms:k1 * nms].rearrange(
                            "p (k m) -> p k m", k=k1 - k0),
                        in_=w_hbm[nm, k0:k1].rearrange("k p m -> p k m")))
                return out
            w_tk[buf] = P.dma("gpsimd", wload2, deps=w_gate[buf],
                              tag=f"{tag}_wld{it+2}")
        cur_evacs = []
        for t in range(NT):
            si = (it * NT + t) % 6
            bias_ap = bias_sb[0:ms, m:m + 1] if bias_sb is not None else 0.0
            scale_ap = scale_sb[0:ms, m:m + 1] if scale_sb is not None else 1.0

            def ev(eng, m=m, ms=ms, t=t, si=si, bias_ap=bias_ap,
                   scale_ap=scale_ap, psbg=psbg):
                return eng.activation(out=ev_sb[si][0:ms, :],
                                      in_=psbg[t][0:ms, :], func=act,
                                      bias=bias_ap, scale=scale_ap)
            ev_t = P.step("scalar", ev, deps=[mm_t] + list(ev_gate[si]),
                          tag=f"{tag}_ev{it}_{t}")
            cur_evacs.append(ev_t)
            gates = []
            if store_out:
                def st(eng, m=m, ms=ms, t=t, si=si, t_base=t_base):
                    return [eng.dma_start(
                        out=out_hbm[m, 0:ms,
                                    t_base + t * 512:t_base + (t + 1) * 512],
                        in_=ev_sb[si][0:ms, :])]
                st_t = P.dma("sync", st, deps=[ev_t], tag=f"{tag}_st")
                stores.append(st_t)
                gates.append(st_t)
            if extra_evac is not None:
                gates += extra_evac(P, t_base, m, t, ev_sb[si], ms, [ev_t])
            ev_gate[si] = gates if gates else [ev_t]
        ps_gates[grp] = cur_evacs
    return P.red(stores)



def ln_phase(P, *, tag, x_hbm, KB, TLOC, out_bf=None, out_f32=None,
             kind="plain", gamma_sb=None, beta_sb=None, eps=1e-5,
             psb, ones_col_bf, ones1_f32, sum_acc_sb=None, deps=()):
    NT = TLOC // 512
    D_ = KB * 128
    xf = [P.sbuf(f"{tag}_xf{i}", [128, KB * 512], F32) for i in range(2)]
    xbf = [P.sbuf(f"{tag}_xbf{i}", [128, KB * 512], BF16) for i in range(2)]
    xsq = P.sbuf(f"{tag}_xsq", [128, KB * 512], BF16)
    xn = [P.sbuf(f"{tag}_xn{i}", [128, KB * 512], F32) for i in range(2)]
    rows = P.sbuf(f"{tag}_rows", [1, 512 * 4], F32)
    nmu_r = rows[0:1, 0:512]
    rstd_r = rows[0:1, 512:1024]
    t1_r = rows[0:1, 1024:1536]
    t2_r = rows[0:1, 1536:2048]
    pS, pQ, pMU, pRS = psb[0], psb[1], psb[2], psb[3]
    lbs, lfs = [], []
    gate = [list(deps), list(deps)]
    sq_gate = list(deps)
    misc = []
    if sum_acc_sb is not None:
        tz = P.step("vector", lambda v: v.memset(sum_acc_sb[:, :], 0.0),
                    deps=deps, tag=f"{tag}_acc0")
        misc.append(tz)
    red_tmp = P.sbuf(f"{tag}_red", [128, 1], F32)
    prev_stats = list(deps)
    for t in range(NT):
        bi = t % 2
        t0 = t * 512

        def xload(eng, t0=t0, bi=bi):
            return [eng.dma_start(out=xf[bi][:, k * 512:(k + 1) * 512],
                                  in_=x_hbm[k, :, t0:t0 + 512])
                    for k in range(KB)]
        tx = P.dma("sync", xload, deps=gate[bi], tag=f"{tag}_xf{t}")

        def xloadb(eng, t0=t0, bi=bi):
            return [eng.dma_start(out=xbf[bi][:, k * 512:(k + 1) * 512],
                                  in_=x_hbm[k, :, t0:t0 + 512])
                    for k in range(KB)]
        txb = P.dma("gpsimd", xloadb, deps=gate[bi], tag=f"{tag}_xbf{t}")

        def sq(eng, bi=bi):
            last = None
            for k in range(KB):
                last = eng.activation(
                    out=xsq[:, k * 512:(k + 1) * 512],
                    in_=xf[bi][:, k * 512:(k + 1) * 512], func=AF.Square)
            return last
        tsq = P.step("scalar", sq, deps=[tx] + sq_gate, tag=f"{tag}_sq{t}")

        def mmstats(eng, bi=bi):
            last = None
            for k in range(KB):
                if kind != "rms":
                    eng.matmul(pS[0:1, :], ones_col_bf[:, 0:1],
                               xbf[bi][:, k * 512:(k + 1) * 512],
                               start=(k == 0), stop=(k == KB - 1))
                last = eng.matmul(pQ[0:1, :], ones_col_bf[:, 0:1],
                                  xsq[:, k * 512:(k + 1) * 512],
                                  start=(k == 0), stop=(k == KB - 1))
            return last
        tmm = P.step("tensor", mmstats, deps=[txb, tsq] + prev_stats,
                     tag=f"{tag}_mm{t}")

        if kind == "rms":
            def rowcalc(eng):
                return eng.activation(out=t1_r, in_=pQ[0:1, :], func=AF.Sqrt)
            tr1 = P.step("scalar", rowcalc, deps=[tmm], tag=f"{tag}_row{t}")

            trow = P.chain("vector", [
                lambda v: v.tensor_scalar_max(t2_r, t1_r, 1e-12),
                lambda v: v.reciprocal(t1_r, t2_r),
                lambda v: v.tensor_scalar_mul(rstd_r, t1_r,
                                              float(math.sqrt(D_))),
            ], deps=[tr1], tag=f"{tag}_rowv{t}")
        else:
            tr0 = P.step("scalar", lambda eng: eng.activation(
                out=nmu_r, in_=pS[0:1, :], func=AF.Identity,
                scale=P.cval(-1.0 / D_)[0:1, :]), deps=[tmm],
                tag=f"{tag}_rowa{t}")
            tr0b = P.step("scalar", lambda eng: eng.activation(
                out=t1_r, in_=pQ[0:1, :], func=AF.Identity,
                scale=P.cval(1.0 / D_)[0:1, :]), deps=[tmm],
                tag=f"{tag}_rowb{t}")
            tr1 = P.step("scalar", lambda eng: eng.activation(
                out=t2_r, in_=nmu_r, func=AF.Square), deps=[tr0, tr0b],
                tag=f"{tag}_row{t}")

            def rowv(eng):
                return eng.tensor_sub(t1_r, t1_r, t2_r)
            trv = P.step("vector", rowv, deps=[tr1, tr0b], tag=f"{tag}_rowv{t}")

            def rowsd(eng):
                return eng.activation(out=t2_r, in_=t1_r, func=AF.Sqrt,
                                      bias=P.cval(float(eps))[0:1, :])
            trs = P.step("scalar", rowsd, deps=[trv], tag=f"{tag}_rowsd{t}")
            trow = P.step("vector", lambda v: v.reciprocal(rstd_r, t2_r),
                          deps=[trs], tag=f"{tag}_rowrc{t}")
        prev_stats = [trow]

        def bc(eng):
            if kind != "rms":
                eng.matmul(pMU[:, :], ones1_f32[0:1, :], nmu_r,
                           start=True, stop=True)
            return eng.matmul(pRS[:, :], ones1_f32[0:1, :], rstd_r,
                              start=True, stop=True)
        tbc = P.step("tensor", bc, deps=[trow], tag=f"{tag}_bc{t}")

        tns = []
        for k in range(KB):
            xi = xf[bi][:, k * 512:(k + 1) * 512]
            xo = xn[bi][:, k * 512:(k + 1) * 512]
            fns = []
            if kind == "rms":
                fns.append(lambda v, xi=xi, xo=xo: v.tensor_tensor(
                    out=xo, in0=xi, in1=pRS[:, :], op=ALU.mult))
            else:
                fns.append(lambda v, xi=xi, xo=xo: v.tensor_tensor(
                    out=xo, in0=xi, in1=pMU[:, :], op=ALU.add))
                fns.append(lambda v, xo=xo: v.tensor_tensor(
                    out=xo, in0=xo, in1=pRS[:, :], op=ALU.mult))
            if kind == "affine":
                fns.append(lambda v, xo=xo, k=k: v.tensor_scalar(
                    out=xo, in0=xo, scalar1=gamma_sb[:, k:k + 1],
                    scalar2=beta_sb[:, k:k + 1], op0=ALU.mult, op1=ALU.add))
            tns.append(P.chain("vector", fns, deps=[tbc, tx],
                               tag=f"{tag}_n{t}_{k}"))
        tn = tns[-1]
        sq_gate = tns

        if sum_acc_sb is not None:
            tacc = []
            for k in range(KB):
                ta = P.chain("vector", [
                    lambda v, k=k, bi=bi: v.reduce_sum(
                        red_tmp[:, 0:1], xn[bi][:, k * 512:(k + 1) * 512],
                        axis=AXX),
                    lambda v, k=k: v.tensor_add(
                        sum_acc_sb[:, k:k + 1], sum_acc_sb[:, k:k + 1],
                        red_tmp[:, 0:1]),
                ], deps=tns + misc, tag=f"{tag}_acc{t}_{k}")
                tacc.append(ta)
                misc = [ta]
            misc = [tacc[-1]]

        st_gates = []
        if out_f32 is not None:
            def stf(eng, t0=t0, bi=bi):
                return [eng.dma_start(out=out_f32[k, :, t0:t0 + 512],
                                      in_=xn[bi][:, k * 512:(k + 1) * 512])
                        for k in range(KB)]
            lf = P.dma("sync", stf, deps=tns, tag=f"{tag}_stf{t}")
            lfs.append(lf)
            st_gates.append(lf)
        if out_bf is not None:
            def stb(eng, t0=t0, bi=bi):
                return [eng.dma_start(out=out_bf[k, :, t0:t0 + 512],
                                      in_=xn[bi][:, k * 512:(k + 1) * 512])
                        for k in range(KB)]
            lb = P.dma("gpsimd", stb, deps=tns, tag=f"{tag}_stb{t}")
            lbs.append(lb)
            st_gates.append(lb)
        gate[bi] = tns + st_gates + (misc if sum_acc_sb is not None else [])
    return P.red(lbs), P.red(lfs), misc



def rope_phase(P, *, tag, src_hbm, src_tiles, out_hbm, TLOC, ropec_sb,
               ropes_sb, deps):
    q0 = [P.sbuf(f"{tag}_q{i}", [128, TLOC], BF16) for i in range(2)]
    qs = P.sbuf(f"{tag}_qs", [128, TLOC], BF16)
    t1 = P.sbuf(f"{tag}_t1", [128, TLOC], BF16)
    perm = []
    for i in range(0, 32, 2):
        perm += [i + 1, i]
    gate = [list(deps), list(deps)]
    g_qs = list(deps)
    g_t1 = list(deps)
    all_st = []
    for i, ti in enumerate(src_tiles):
        bi = i % 2
        tl = P.dma("sync", lambda eng, ti=ti, bi=bi: [eng.dma_start(
            out=q0[bi][:, :], in_=src_hbm[ti])], deps=gate[bi],
            tag=f"{tag}_ld{i}")
        tsh = P.step("vector", lambda v, bi=bi: v.stream_shuffle(
            qs[:, :], q0[bi][:, :], perm), deps=[tl] + g_qs,
            tag=f"{tag}_sh{i}")
        tm1 = P.step("gpsimd", lambda g, bi=bi: g.tensor_mul(
            t1[:, :], q0[bi][:, :], ropec_sb[:, :]), deps=[tl] + g_t1,
            tag=f"{tag}_m1{i}")
        tm2 = P.step("vector", lambda v, bi=bi: v.tensor_mul(
            qs[:, :], qs[:, :], ropes_sb[:, :]), deps=[tsh],
            tag=f"{tag}_m2{i}")
        tad = P.step("vector", lambda v, bi=bi: v.tensor_add(
            q0[bi][:, :], t1[:, :], qs[:, :]), deps=[tm1, tm2],
            tag=f"{tag}_ad{i}")
        st = P.dma("sync", lambda eng, i=i, bi=bi: [eng.dma_start(
            out=out_hbm[i], in_=q0[bi][:, :])], deps=[tad],
            tag=f"{tag}_st{i}")
        gate[bi] = [st]
        g_qs = [tm2]
        g_t1 = [tad]
        all_st.append(st)
    return P.red(all_st)



def l0_scan_A(P, *, tag, qr_hbm, a_hbm, v_hbm, y_hbm, send_hbm, TLOC,
              logitsT_sb, ramps_sb, ident_bf, maskAT_sb, psf, psbf16,
              deps):
    NCH = TLOC // 128
    qr_h = P.sbuf(f"{tag}_qr", [16, TLOC], BF16)
    a_h = P.sbuf(f"{tag}_a", [16, TLOC], BF16)
    v_h = P.sbuf(f"{tag}_v", [64, TLOC], BF16)
    y_h = P.sbuf(f"{tag}_y", [64, TLOC], F32)
    M = P.sbuf(f"{tag}_M", [16, 64], F32)
    M_bf = P.sbuf(f"{tag}_Mbf", [16, 64], BF16)
    Tq = P.sbuf(f"{tag}_Tq", [16, 128], BF16)
    Tk = P.sbuf(f"{tag}_Tk", [16, 128], BF16)
    Td = P.sbuf(f"{tag}_Td", [16, 128], BF16)
    Thp = P.sbuf(f"{tag}_Thp", [16, 128], BF16)
    ThT = P.sbuf(f"{tag}_ThT", [128, 16], BF16)
    lamC = P.sbuf(f"{tag}_lamC", [16, 1], F32)
    lnl = P.sbuf(f"{tag}_lnl", [16, 4], F32)
    qt = P.sbuf(f"{tag}_qt", [16, 128], BF16)
    kt = P.sbuf(f"{tag}_kt", [16, 128], BF16)
    qh = P.sbuf(f"{tag}_qh", [16, 128], BF16)
    aT = P.sbuf(f"{tag}_aT", [128, 16], BF16)
    ahT = P.sbuf(f"{tag}_ahT", [128, 16], BF16)
    vT = P.sbuf(f"{tag}_vT", [128, 64], BF16)
    Am = P.sbuf(f"{tag}_Am", [128, 128], BF16)
    pA = psf[0]
    pY = psf[1]
    pDM = psf[2]
    pTv = psbf16[:, 0:64]
    pTa = psbf16[:, 128:144]
    head_gate = list(deps)
    out_tickets = []
    for h in range(16):
        tl = P.dma("sync", lambda eng, h=h: [
            eng.dma_start(out=qr_h[:, :],
                          in_=qr_hbm[h // 8, (h % 8) * 16:(h % 8) * 16 + 16, :]),
            eng.dma_start(out=a_h[:, :],
                          in_=a_hbm[h // 8, (h % 8) * 16:(h % 8) * 16 + 16, :]),
            eng.dma_start(out=v_h[:, :],
                          in_=v_hbm[h // 2, (h % 2) * 64:(h % 2) * 64 + 64, :]),
        ], deps=head_gate, tag=f"{tag}_ld{h}")
        tl0 = P.step("scalar", lambda sc, h=h: sc.activation(
            out=lnl[:, 0:1], in_=logitsT_sb[:, h:h + 1], func=AF.Exp,
            scale=P.cval(-1.0)[0:16, :]), deps=head_gate, tag=f"{tag}_l0{h}")
        tl1 = P.step("scalar", lambda sc: sc.activation(
            out=lnl[:, 1:2], in_=lnl[:, 0:1], func=AF.Ln, bias=1.0),
            deps=[tl0], tag=f"{tag}_l1{h}")
        tl2 = P.step("scalar", lambda sc: sc.mul(
            out=lnl[:, 2:3], in_=lnl[:, 1:2], mul=-1.0), deps=[tl1],
            tag=f"{tag}_l2{h}")

        def tables(sc):
            sc.activation(out=Tq[:, :], in_=ramps_sb[:, 0:128], func=AF.Exp,
                          scale=lnl[:, 2:3])
            sc.activation(out=Tk[:, :], in_=ramps_sb[:, 128:256], func=AF.Exp,
                          scale=lnl[:, 2:3], bias=P.cval(np.log(0.125))[0:16, :])
            sc.activation(out=Td[:, :], in_=ramps_sb[:, 256:384], func=AF.Exp,
                          scale=lnl[:, 2:3])
            sc.activation(out=Thp[:, :], in_=ramps_sb[:, 384:512],
                          func=AF.Exp, scale=lnl[:, 2:3],
                          bias=P.cval(np.log(0.125))[0:16, :])
            return sc.activation(out=lamC[:, :], in_=lnl[:, 2:3],
                                 func=AF.Exp, scale=P.cval(128.0)[0:16, :])
        ttab = P.step("scalar", tables, deps=[tl2], tag=f"{tag}_tab{h}")
        tth0 = P.step("tensor", lambda t: t.transpose(
            pTa, Thp[:, :], ident_bf[0:16, 0:16]), deps=[ttab],
            tag=f"{tag}_thT{h}")
        tth1 = P.step("vector", lambda v: v.tensor_copy(ThT[:, :], pTa),
                      deps=[tth0], tag=f"{tag}_thTe{h}")
        tmz = P.step("vector", lambda v: v.memset(M[:, :], 0.0),
                     deps=head_gate, tag=f"{tag}_m0{h}")
        tmzb = P.step("vector", lambda v: v.memset(M_bf[:, :], 0.0),
                      deps=head_gate, tag=f"{tag}_m0b{h}")
        g_qt = [ttab]
        g_kt = [ttab]
        g_qh = [ttab]
        g_vT = [tth1]
        g_aT = [tth1]
        g_Am = list(deps)
        g_Mbf = [tmzb]
        t_mup = tmz
        last_evY = None
        for i in range(NCH):
            sl = slice(i * 128, (i + 1) * 128)
            t_qt = P.step("gpsimd", lambda g, sl=sl: g.tensor_mul(
                qt[:, :], qr_h[:, sl], Tq[:, :]), deps=[tl] + g_qt,
                tag=f"{tag}_qt")
            t_kt = P.step("gpsimd", lambda g, sl=sl: g.tensor_mul(
                kt[:, :], a_h[:, sl], Tk[:, :]), deps=[tl] + g_kt,
                tag=f"{tag}_kt")
            t_qh = P.step("gpsimd", lambda g, sl=sl: g.tensor_mul(
                qh[:, :], qr_h[:, sl], Td[:, :]), deps=[tl] + g_qh,
                tag=f"{tag}_qh")
            t_tv = P.step("tensor", lambda t, sl=sl: t.transpose(
                pTv, v_h[:, sl], ident_bf[0:64, 0:64]), deps=[tl] + g_vT,
                tag=f"{tag}_tv")
            t_ve = P.step("vector", lambda v: v.tensor_copy(vT[:, :], pTv),
                          deps=[t_tv], tag=f"{tag}_ve")
            t_ta = P.step("tensor", lambda t, sl=sl: t.transpose(
                pTa, a_h[:, sl], ident_bf[0:16, 0:16]), deps=[tl, t_ve, tth1],
                tag=f"{tag}_ta")
            t_ae = P.step("vector", lambda v: v.tensor_copy(aT[:, :], pTa),
                          deps=[t_ta], tag=f"{tag}_ae")
            t_ah = P.step("vector", lambda v: v.tensor_mul(
                ahT[:, :], aT[:, :], ThT[:, :]), deps=[t_ae],
                tag=f"{tag}_ah")
            t_A = P.step("tensor", lambda t: t.matmul(
                pA[0:128, 0:128], kt[:, :], qt[:, :], start=True, stop=True),
                deps=[t_qt, t_kt] + g_Am, tag=f"{tag}_A")
            t_msk0 = P.step("vector", lambda v: v.memset(Am[:, :], 0.0),
                            deps=g_Am, tag=f"{tag}_mz")
            t_msk = P.step("vector", lambda v: v.copy_predicated(
                Am[:, :], maskAT_sb[:, :], pA[0:128, 0:128]),
                deps=[t_A, t_msk0], tag=f"{tag}_mk")
            t_y1 = P.step("tensor", lambda t: t.matmul(
                pY[0:64, 0:128], M_bf[:, :], qh[:, :], start=True,
                stop=False), deps=[t_qh, t_msk] + g_Mbf + ([last_evY] if last_evY else []),
                tag=f"{tag}_y1")
            t_y2 = P.step("tensor", lambda t: t.matmul(
                pY[0:64, 0:128], vT[:, :], Am[:, :], start=False, stop=True),
                deps=[t_msk, t_ve], tag=f"{tag}_y2")
            t_ev = P.step("vector", lambda v, sl=sl: v.tensor_copy(
                y_h[:, sl], pY[0:64, 0:128]), deps=[t_y2], tag=f"{tag}_ye")
            last_evY = t_ev
            t_dm = P.step("tensor", lambda t: t.matmul(
                pDM[0:16, 0:64], ahT[:, :], vT[:, :], start=True, stop=True),
                deps=[t_ah, t_ve, t_mup], tag=f"{tag}_dm")
            t_mu1 = P.step("vector", lambda v: v.tensor_scalar_mul(
                M[:, :], M[:, :], lamC[:, 0:1]), deps=[t_mup, ttab, t_y1],
                tag=f"{tag}_mu1")
            t_mu2 = P.step("vector", lambda v: v.tensor_add(
                M[:, :], M[:, :], pDM[0:16, 0:64]), deps=[t_mu1, t_dm],
                tag=f"{tag}_mu2")
            t_mbf = P.step("vector", lambda v: v.tensor_copy(
                M_bf[:, :], M[:, :]), deps=[t_mu2, t_y1], tag=f"{tag}_mbf")
            t_mup = t_mu2
            g_Mbf = [t_mbf]
            g_qt = [t_A]
            g_kt = [t_A]
            g_qh = [t_y1]
            g_vT = [t_y2, t_dm]
            g_aT = [t_ah]
            g_Am = [t_y2]
        tsy = P.dma("sync", lambda eng, h=h: [eng.dma_start(
            out=y_hbm[h // 2, (h % 2) * 64:(h % 2) * 64 + 64, :],
            in_=y_h[:, :])], deps=[last_evY], tag=f"{tag}_sy{h}")
        tsm = P.dma("sync", lambda eng, h=h: [eng.dma_start(
            out=send_hbm[h], in_=M[:, :])], deps=[t_mup], tag=f"{tag}_sm{h}")
        out_tickets += [tsy, tsm]
        head_gate = P.red([tsy, tsm, last_evY, t_mup] + g_Mbf + g_qt + g_kt
                          + g_qh + g_vT + g_aT + g_Am)
    return P.red(out_tickets)



TL1 = 4096


def emit_l0a(P, TLOC):
    TPX = min(2048, TLOC)
    x = P.dram("x", [8, 128, TLOC], F32, kind="ExternalInput")
    w_qkv = P.dram("w_qkv", [24, 8, 128, 128], BF16, kind="ExternalInput")
    w_g = P.dram("w_g", [2, 8, 128, 128], BF16, kind="ExternalInput")
    gb = P.dram("gb", [128, 2], F32, kind="ExternalInput")
    w_qr = P.dram("w_qr", [2, 8, 128, 128], BF16, kind="ExternalInput")
    w_kr = P.dram("w_kr", [2, 8, 128, 128], BF16, kind="ExternalInput")
    logitsT = P.dram("logitsT", [16, 16], F32, kind="ExternalInput")
    ropec = P.dram("ropec", [128, TLOC], F32, kind="ExternalInput")
    ropes = P.dram("ropes", [128, TLOC], F32, kind="ExternalInput")
    ident = P.dram("ident", [128, 128], BF16, kind="ExternalInput")
    maskAT = P.dram("maskAT", [128, 128], U8, kind="ExternalInput")
    ramps = P.dram("ramps", [16, 512], F32, kind="ExternalInput")
    xn_bf = P.dram("xn_bf", [8, 128, TLOC], BF16)
    xn_f = P.dram("xn_f", [8, 128, TLOC], F32)
    qr_o = P.dram("qr", [2, 128, TLOC], BF16)
    y_loc = P.dram("y_loc", [8, 128, TLOC], F32)
    send = P.dram("send", [16, 16, 64], F32)
    meanp = P.dram("meanp", [128, 8], F32)
    qkvg = P.dram("qkvg", [26, 128, TLOC], BF16)
    qrope = P.dram("qrope", [8, 128, TLOC], BF16)
    krope = P.dram("krope", [8, 128, TLOC], BF16)
    kr_i = P.dram("kr_i", [2, 128, TLOC], BF16)
    a_i = P.dram("a_i", [2, 128, TLOC], BF16)

    ones_col_bf = P.sbuf("ones_col_bf", [128, 1], BF16)
    ones1_f = P.sbuf("ones1_f", [1, 128], F32)
    ident_sb = P.sbuf("ident_sb", [128, 128], BF16)
    maskAT_sb = P.sbuf("maskAT_sb", [128, 128], U8)
    ramps_sb = P.sbuf("ramps_sb", [16, 512], F32)
    logitsT_sb = P.sbuf("logitsT_sb", [16, 16], F32)
    gb_sb = P.sbuf("gb_sb", [128, 2], F32)
    acc_sb = P.sbuf("acc_sb", [128, 8], F32)

    psf = [P.psum(f"ps{i}", [128, 512], F32) for i in range(6)]
    psbf = P.psum("psbf", [128, 512], BF16)

    tco = P.step("vector", lambda v: v.memset(ones_col_bf[:, :], 1.0),
                 tag="c0")
    tco2 = P.step("vector", lambda v: v.memset(ones1_f[:, :], 1.0), tag="c1")
    tcl = P.dma("sync", lambda eng: [
        eng.dma_start(out=ident_sb[:, :], in_=ident),
        eng.dma_start(out=maskAT_sb[:, :], in_=maskAT),
        eng.dma_start(out=ramps_sb[:, :], in_=ramps),
        eng.dma_start(out=logitsT_sb[:, :], in_=logitsT),
        eng.dma_start(out=gb_sb[:, :], in_=gb),
    ], tag="consts")
    c0 = [tco, tco2, tcl]

    with P.scope():
        lnb, lnf, lnmisc = ln_phase(
            P, tag="ln", x_hbm=x, KB=8, TLOC=TLOC, out_bf=xn_bf, out_f32=xn_f,
            kind="plain", psb=psf[0:4], ones_col_bf=ones_col_bf,
            ones1_f32=ones1_f, sum_acc_sb=acc_sb, deps=c0)
        tma = P.dma("sync", lambda eng: [eng.dma_start(out=meanp,
                                                       in_=acc_sb[:, :])],
                    deps=lnmisc, tag="meanp_st")
    f1 = P.fence()
    with P.scope():
        st1 = stream_matmul(P, tag="qkv", x_hbm=xn_bf, w_hbm=w_qkv,
                            out_hbm=qkvg[0:24], KB=8, M=3072, TLOC=TLOC,
                            TP=TPX, psb=psf[0:6], act=AF.Identity, deps=f1)
    f2 = P.fence()
    with P.scope():
        st2 = stream_matmul(P, tag="gproj", x_hbm=xn_bf, w_hbm=w_g,
                            out_hbm=qkvg[24:26], KB=8, M=256, TLOC=TLOC,
                            TP=TPX, psb=psf[0:6], act=AF.Sigmoid,
                            bias_sb=gb_sb, deps=f2)
    f3 = P.fence()
    with P.scope():
        ropec_sb = P.sbuf("ropec_sb", [128, TLOC], BF16)
        ropes_sb = P.sbuf("ropes_sb", [128, TLOC], BF16)
        tcr = P.dma("gpsimd", lambda eng: [
            eng.dma_start(out=ropec_sb[:, :], in_=ropec),
            eng.dma_start(out=ropes_sb[:, :], in_=ropes),
        ], deps=f3, tag="ropetab")
        str1 = rope_phase(P, tag="ropeq", src_hbm=qkvg,
                          src_tiles=list(range(8)), out_hbm=qrope,
                          TLOC=TLOC, ropec_sb=ropec_sb, ropes_sb=ropes_sb,
                          deps=f3 + [tcr])
        str2 = rope_phase(P, tag="ropek", src_hbm=qkvg,
                          src_tiles=list(range(8, 16)), out_hbm=krope,
                          TLOC=TLOC, ropec_sb=ropec_sb, ropes_sb=ropes_sb,
                          deps=str1)
    f4 = P.fence()
    with P.scope():
        st3 = stream_matmul(P, tag="qr", x_hbm=qrope, w_hbm=w_qr,
                            out_hbm=qr_o, KB=8, M=256, TLOC=TLOC, TP=TPX,
                            psb=psf[0:6], deps=f4)
    f5 = P.fence()
    with P.scope():
        st4 = stream_matmul(P, tag="kr", x_hbm=krope, w_hbm=w_kr,
                            out_hbm=kr_i, KB=8, M=256, TLOC=TLOC, TP=TPX,
                            psb=psf[0:6], deps=f5)
    f6 = P.fence()
    with P.scope():
        ab = [P.sbuf(f"ab{i}", [128, TLOC], BF16) for i in range(3)]
        prev = f6
        for ti in range(2):
            tld = P.dma("sync", lambda eng, ti=ti: [
                eng.dma_start(out=ab[0][:, :], in_=qkvg[24 + ti]),
                eng.dma_start(out=ab[1][:, :], in_=kr_i[ti]),
            ], deps=prev, tag=f"a_ld{ti}")
            tm = P.step("vector", lambda v: v.tensor_mul(
                ab[2][:, :], ab[0][:, :], ab[1][:, :]), deps=[tld],
                tag=f"a_m{ti}")
            tst = P.dma("sync", lambda eng, ti=ti: [
                eng.dma_start(out=a_i[ti], in_=ab[2][:, :])], deps=[tm],
                tag=f"a_st{ti}")
            prev = [tm, tst]
    f7 = P.fence()
    with P.scope():
        sc = l0_scan_A(P, tag="sc", qr_hbm=qr_o, a_hbm=a_i,
                       v_hbm=qkvg[16:24], y_hbm=y_loc, send_hbm=send,
                       TLOC=TLOC, logitsT_sb=logitsT_sb, ramps_sb=ramps_sb,
                       ident_bf=ident_sb, maskAT_sb=maskAT_sb,
                       psf=psf[0:3], psbf16=psbf, deps=f7)
    return dict(x=x, xn_bf=xn_bf, xn_f=xn_f, qr=qr_o, y_loc=y_loc,
                send=send, meanp=meanp)


def host_prep_consts_l0a(TLOC, q0):
    ident = np.eye(128).astype(BF)
    maskAT = (np.tril(np.ones((128, 128)))).T.astype(np.uint8)
    j = np.arange(128, dtype=np.float32)
    ramps = np.concatenate([
        np.tile(j - 64.0, (16, 1)),
        np.tile(64.0 - j, (16, 1)),
        np.tile(j + 1.0, (16, 1)),
        np.tile(127.0 - j, (16, 1)),
    ], axis=1).astype(np.float32)
    half = 32
    inv_freq = (1.0 / 10000.0 ** (np.arange(half, dtype=np.float64) / half))
    tpos = np.arange(q0, q0 + TLOC, dtype=np.float64)
    fr = tpos[:, None] * inv_freq[None, :]
    c = np.cos(fr)
    s = np.sin(fr)
    C64 = np.empty((64, TLOC), np.float32)
    S64 = np.empty((64, TLOC), np.float32)
    C64[0::2] = c.T
    C64[1::2] = c.T
    S64[0::2] = -s.T
    S64[1::2] = s.T
    ropec = np.tile(C64, (2, 1)).astype(np.float32)
    ropes = np.tile(S64, (2, 1)).astype(np.float32)
    return dict(ident=ident, maskAT=np.ascontiguousarray(maskAT),
                ramps=ramps, ropec=ropec, ropes=ropes)


def host_prep_l0a_core(inputs, core, TLOC=TL1):
    b = core // 4
    q0 = (core % 4) * TLOC
    x = np.asarray(inputs["L0_attn_x"])[b, q0:q0 + TLOC]
    c = host_prep_consts_l0a(TLOC, q0)
    return dict(x=fm(x), ropec=c["ropec"], ropes=c["ropes"])


def host_prep_l0a_w(inputs, TLOC=TL1):
    c = host_prep_consts_l0a(TLOC, 0)
    d = dict(ident=c["ident"], maskAT=c["maskAT"], ramps=c["ramps"])
    qw = np.asarray(inputs["L0_attn_q_w"])
    kw = np.asarray(inputs["L0_attn_k_w"])
    vw = np.asarray(inputs["L0_attn_v_w"])
    gw = np.asarray(inputs["L0_attn_g_w"])
    wcat = np.concatenate([qw.T, kw.T, vw.T], axis=1)
    d["w_qkv"] = wblk(wcat).astype(BF)
    d["w_g"] = wblk(gw.T).astype(BF)
    d["gb"] = np.ascontiguousarray(
        np.asarray(inputs["L0_attn_g_b"]).reshape(2, 128).T)
    Wq = np.asarray(inputs["L0_attn_Wq"])
    Wk = np.asarray(inputs["L0_attn_Wk"])

    def blockdiag(W):
        bd = np.zeros((1024, 256), np.float32)
        for h in range(16):
            bd[h * 64:(h + 1) * 64, h * 16:(h + 1) * 16] = W[h].T
        return bd
    d["w_qr"] = wblk(blockdiag(Wq)).astype(BF)
    d["w_kr"] = wblk(blockdiag(Wk)).astype(BF)
    d["logitsT"] = np.ascontiguousarray(
        np.asarray(inputs["L0_attn_base_decay_logits"]).T)
    return d



def ew_phase(P, *, tag, srcs, KB, TLOC, emit, out_specs, deps, CH=512,
             extra_sb=()):
    NT = TLOC // CH
    bufs = [[P.sbuf(f"{tag}_s{si}_{i}", [128, KB * CH],
                    dt) for i in range(2)] for si, (_, dt) in enumerate(srcs)]
    obuf = [P.sbuf(f"{tag}_o{i}", [128, KB * CH], F32) for i in range(2)]
    gate = [list(deps), list(deps)]
    stores = []
    for t in range(NT):
        bi = t % 2
        t0 = t * CH
        tl = []
        for si, (hbm, dt) in enumerate(srcs):
            q = "gpsimd" if si % 2 else "sync"
            tl.append(P.dma(q, lambda eng, si=si, bi=bi, t0=t0, hbm=hbm: [
                eng.dma_start(out=bufs[si][bi][:, k * CH:(k + 1) * CH],
                              in_=hbm[k, :, t0:t0 + CH]) for k in range(KB)],
                deps=gate[bi], tag=f"{tag}_ld{t}_{si}"))
        ks = []
        for k in range(KB):
            bb = [bufs[si][bi][:, k * CH:(k + 1) * CH]
                  for si in range(len(srcs))]
            ob = obuf[bi][:, k * CH:(k + 1) * CH]
            ks += emit(P, k, bb, ob, tl + gate[bi])
        sts = []
        for oi, (ohbm, q) in enumerate(out_specs):
            st = P.dma(q, lambda eng, t0=t0, bi=bi, ohbm=ohbm: [
                eng.dma_start(out=ohbm[k, :, t0:t0 + CH],
                              in_=obuf[bi][:, k * CH:(k + 1) * CH])
                for k in range(KB)], deps=ks, tag=f"{tag}_st{t}_{oi}")
            sts.append(st)
            stores.append(st)
        gate[bi] = ks + sts
    return P.red(stores)



def emit_l0b(P, TLOC, ext):
    TPX = min(2048, TLOC)
    x = ext["x"]
    xn_bf = ext["xn_bf"]
    xn_f = ext["xn_f"]
    qr = ext["qr"]
    y_loc = ext["y_loc"]
    s_all = ext["s_all"]
    meanp_all = ext["meanp_all"]
    rjs = P.dram("rjs", [128, 4], F32, kind="ExternalInput")
    rjm = P.dram("rjm", [128, 4], F32, kind="ExternalInput")
    logitsT = P.dram("logitsT", [16, 16], F32, kind="ExternalInput")
    ramp_g = P.dram("ramp_g", [16, TLOC], F32, kind="ExternalInput")
    w_out = P.dram("w_out", [8, 8, 128, 128], BF16, kind="ExternalInput")
    outb = P.dram("outb", [128, 8], F32, kind="ExternalInput")
    lng = P.dram("lng", [128, 8], F32, kind="ExternalInput")
    lnb_i = P.dram("lnb", [128, 8], F32, kind="ExternalInput")
    w_cn1 = P.dram("w_cn1", [8, 8, 128, 128], BF16, kind="ExternalInput")
    w_cn2 = P.dram("w_cn2", [4, 8, 128, 128], BF16, kind="ExternalInput")
    w_cn3 = P.dram("w_cn3", [2, 4, 128, 128], BF16, kind="ExternalInput")
    w_cn4 = P.dram("w_cn4", [128, 2], F32, kind="ExternalInput")
    b_cn1 = P.dram("b_cn1", [128, 8], F32, kind="ExternalInput")
    b_cn2 = P.dram("b_cn2", [128, 4], F32, kind="ExternalInput")
    b_cn3 = P.dram("b_cn3", [128, 2], F32, kind="ExternalInput")
    b_cn4 = P.dram("b_cn4", [128, 1], F32, kind="ExternalInput")
    w_gc1 = P.dram("w_gc1", [4, 8, 128, 128], BF16, kind="ExternalInput")
    w_gc2 = P.dram("w_gc2", [2, 4, 128, 128], BF16, kind="ExternalInput")
    w_gc3 = P.dram("w_gc3", [128, 2], F32, kind="ExternalInput")
    b_gc1 = P.dram("b_gc1", [128, 4], F32, kind="ExternalInput")
    b_gc2 = P.dram("b_gc2", [128, 2], F32, kind="ExternalInput")
    b_gc3 = P.dram("b_gc3", [128, 1], F32, kind="ExternalInput")
    ar_in = P.dram("ar_in", [1, 1], F32, kind="ExternalInput")
    x1 = P.dram("x1", [8, 128, TLOC], F32)
    ybf = P.dram("ybf", [8, 128, TLOC], BF16)
    s1f = P.dram("s1f", [8, 128, TLOC], F32)
    hres = P.dram("hres", [8, 128, TLOC], F32)
    s_f = P.dram("s_f", [8, 128, TLOC], F32)
    h1b = P.dram("h1b", [8, 128, TLOC], BF16)
    h2b = P.dram("h2b", [4, 128, TLOC], BF16)
    h3b = P.dram("h3b", [2, 128, TLOC], BF16)
    tfr = P.dram("tfr", [1, 1, TLOC], F32)
    outk_f = [P.dram(f"outk_f{i}", [8, 128, TLOC], F32) for i in range(2)]
    outk_b = [P.dram(f"outk_b{i}", [8, 128, TLOC], BF16) for i in range(2)]

    ones_col_bf = None
    psf = [P.psum(f"ps{i}", [128, 512], F32) for i in range(6)]

    onesb = P.sbuf("onesb", [128, 1], BF16)
    ones1f = P.sbuf("ones1f", [1, 128], F32)
    logitsT_sb = P.sbuf("logitsT_sb", [16, 16], F32)
    rjs_sb = P.sbuf("rjs_sb", [128, 4], F32)
    rjm_sb = P.sbuf("rjm_sb", [128, 4], F32)
    outb_sb = P.sbuf("outb_sb", [128, 8], F32)
    lng_sb = P.sbuf("lng_sb", [128, 8], F32)
    lnbb_sb = P.sbuf("lnbb_sb", [128, 8], F32)
    bcn1_sb = P.sbuf("bcn1_sb", [128, 8], F32)
    bcn2_sb = P.sbuf("bcn2_sb", [128, 4], F32)
    bcn3_sb = P.sbuf("bcn3_sb", [128, 2], F32)
    bcn4_sb = P.sbuf("bcn4_sb", [128, 1], F32)
    gfrow = P.sbuf("gfrow", [1, 8], F32)
    krow = P.sbuf("krow", [1, 4], F32)
    ar_sb = P.sbuf("ar_sb", [1, 1], F32)
    tco = P.step("vector", lambda v: v.memset(onesb[:, :], 1.0), tag="c0")
    tco2 = P.step("vector", lambda v: v.memset(ones1f[:, :], 1.0), tag="c1")
    tcl = P.dma("sync", lambda eng: [
        eng.dma_start(out=logitsT_sb[:, :], in_=logitsT),
        eng.dma_start(out=rjs_sb[:, :], in_=rjs),
        eng.dma_start(out=rjm_sb[:, :], in_=rjm),
        eng.dma_start(out=outb_sb[:, :], in_=outb),
        eng.dma_start(out=lng_sb[:, :], in_=lng),
        eng.dma_start(out=lnbb_sb[:, :], in_=lnb_i),
        eng.dma_start(out=bcn1_sb[:, :], in_=b_cn1),
        eng.dma_start(out=bcn2_sb[:, :], in_=b_cn2),
        eng.dma_start(out=bcn3_sb[:, :], in_=b_cn3),
        eng.dma_start(out=bcn4_sb[:, :], in_=b_cn4),
        eng.dma_start(out=ar_sb[:, :], in_=ar_in[0:1, 0:1]),
    ], tag="consts")
    c0 = [tco, tco2, tcl]

    with P.scope():
        qr_h = P.sbuf("pb_qr", [16, TLOC], BF16)
        lamg = P.sbuf("pb_lamg", [16, TLOC], BF16)
        rampg_sb = P.sbuf("pb_ramp", [16, TLOC], F32)
        y_h = P.sbuf("pb_y", [64, TLOC], F32)
        lnl = P.sbuf("pb_lnl", [16, 4], F32)
        Sj = P.sbuf("pb_Sj", [16, 64], F32)
        coef = P.sbuf("pb_coef", [16, 4], F32)
        Mi = P.sbuf("pb_Mi", [16, 64], F32)
        Mib = P.sbuf("pb_Mib", [16, 64], BF16)
        trmp = P.dma("sync", lambda eng: [
            eng.dma_start(out=rampg_sb[:, :], in_=ramp_g)], deps=c0,
            tag="rampg")
        hg = list(c0) + [trmp]
        for h in range(16):
            tld = P.dma("sync", lambda eng, h=h: [
                eng.dma_start(out=qr_h[:, :],
                              in_=qr[h // 8, (h % 8) * 16:(h % 8) * 16 + 16, :]),
                eng.dma_start(out=y_h[:, :],
                              in_=y_loc[h // 2, (h % 2) * 64:(h % 2) * 64 + 64, :]),
            ], deps=hg, tag=f"pb_ld{h}")
            tl0 = P.step("scalar", lambda sc, h=h: sc.activation(
                out=lnl[:, 0:1], in_=logitsT_sb[:, h:h + 1], func=AF.Exp,
                scale=P.cval(-1.0)[0:16, :]), deps=hg, tag=f"pb_l0{h}")
            tl1 = P.step("scalar", lambda sc: sc.activation(
                out=lnl[:, 1:2], in_=lnl[:, 0:1], func=AF.Ln, bias=1.0),
                deps=[tl0], tag=f"pb_l1{h}")
            tl2 = P.step("scalar", lambda sc: sc.mul(
                out=lnl[:, 2:3], in_=lnl[:, 1:2], mul=-1.0), deps=[tl1],
                tag=f"pb_l2{h}")
            tlg = P.step("scalar", lambda sc: sc.activation(
                out=lamg[:, :], in_=rampg_sb[:, :], func=AF.Exp,
                scale=lnl[:, 2:3]), deps=[tl2, trmp] + hg, tag=f"pb_lg{h}")
            tqh = P.step("vector", lambda v: v.tensor_mul(
                qr_h[:, :], qr_h[:, :], lamg[:, :]), deps=[tlg, tld],
                tag=f"pb_qh{h}")
            tmz = P.step("vector", lambda v: v.memset(Mi[:, :], 0.0),
                         deps=hg, tag=f"pb_mz{h}")
            tacc = tmz
            for j in range(4):
                tsl = P.dma("gpsimd", lambda eng, j=j, h=h: [
                    eng.dma_start(out=Sj[:, :], in_=s_all[j, h])],
                    deps=[tacc], tag=f"pb_sl{h}_{j}")
                tc0 = P.step("scalar", lambda sc, j=j: sc.activation(
                    out=coef[:, 0:1], in_=lnl[:, 2:3], func=AF.Exp,
                    scale=rjs_sb[0:16, j:j + 1]), deps=[tl2, tacc],
                    tag=f"pb_c0{h}_{j}")
                tc1 = P.step("vector", lambda v, j=j: v.tensor_scalar_mul(
                    coef[:, 1:2], coef[:, 0:1], rjm_sb[0:16, j:j + 1]),
                    deps=[tc0], tag=f"pb_c1{h}_{j}")
                tc2 = P.chain("vector", [
                    lambda v: v.tensor_scalar_mul(Sj[:, :], Sj[:, :],
                                                  coef[:, 1:2]),
                    lambda v: v.tensor_add(Mi[:, :], Mi[:, :], Sj[:, :]),
                ], deps=[tc1, tsl], tag=f"pb_c2{h}_{j}")
                tacc = tc2
            tmb = P.step("vector", lambda v: v.tensor_copy(
                Mib[:, :], Mi[:, :]), deps=[tacc], tag=f"pb_mb{h}")
            tys = []
            for tch in range(TLOC // 512):
                sl = slice(tch * 512, (tch + 1) * 512)
                tmm = P.step("tensor", lambda t, sl=sl: t.matmul(
                    psf[0][0:64, :], Mib[:, :], qr_h[:, sl], start=True,
                    stop=True), deps=[tmb, tqh] + (tys[-1:] if tys else []),
                    tag=f"pb_mm{h}_{tch}")
                tad = P.step("vector", lambda v, sl=sl: v.tensor_tensor(
                    out=y_h[:, sl], in0=y_h[:, sl], in1=psf[0][0:64, :],
                    op=ALU.add), deps=[tmm, tld], tag=f"pb_ad{h}_{tch}")
                tys.append(tad)
            tst = P.dma("gpsimd", lambda eng, h=h: [eng.dma_start(
                out=ybf[h // 2, (h % 2) * 64:(h % 2) * 64 + 64, :],
                in_=y_h[:, :])], deps=tys, tag=f"pb_st{h}")
            hg = P.red([tst] + tys + [tqh, tlg, tacc, tmb])
    fB = P.fence()
    with P.scope():
        stp = stream_matmul(P, tag="oproj", x_hbm=ybf, w_hbm=w_out,
                            out_hbm=s1f, KB=8, M=1024, TLOC=TLOC, TP=TPX,
                            psb=psf[0:6], act=AF.Identity, bias_sb=outb_sb,
                            out_dt=F32, deps=fB)
    fC = P.fence()
    with P.scope():
        def em_add(P_, k, bb, ob, dd):
            t = P_.step("vector", lambda v: v.tensor_add(ob, bb[0], bb[1]),
                        deps=dd, tag=f"hres_{k}")
            return [t]
        sth = ew_phase(P, tag="hres", srcs=[(xn_f, F32), (s1f, F32)], KB=8,
                       TLOC=TLOC, emit=em_add, out_specs=[(hres, "sync")],
                       deps=fC)
    fD = P.fence()
    with P.scope():
        _, slf, _ = ln_phase(P, tag="iln", x_hbm=hres, KB=8, TLOC=TLOC,
                             out_bf=None, out_f32=s_f, kind="affine",
                             gamma_sb=lng_sb, beta_sb=lnbb_sb, eps=1e-5,
                             psb=psf[0:4], ones_col_bf=onesb,
                             ones1_f32=ones1f, deps=fD)
    fE = P.fence()
    with P.scope():
        bgc1_sb = P.sbuf("bgc1_sb", [128, 4], F32)
        bgc2_sb = P.sbuf("bgc2_sb", [128, 2], F32)
        bgc3_sb = P.sbuf("bgc3_sb", [128, 1], F32)
        mp = P.sbuf("gf_mp", [128, 8], F32)
        mp2 = P.sbuf("gf_mp2", [128, 8], F32)
        mbf = P.sbuf("gf_mbf", [128, 8], BF16)
        h1c = P.sbuf("gf_h1", [128, 4], BF16)
        h2c = P.sbuf("gf_h2", [128, 2], BF16)
        wgc = P.sbuf("gf_w", [128, 4096], BF16)
        tbl = P.dma("sync", lambda eng: [
            eng.dma_start(out=bgc1_sb[:, :], in_=b_gc1),
            eng.dma_start(out=bgc2_sb[:, :], in_=b_gc2),
            eng.dma_start(out=bgc3_sb[:, :], in_=b_gc3),
            eng.dma_start(out=mp[:, :], in_=meanp_all[0]),
        ], deps=fE, tag="gf_c")
        tac = tbl
        for j in range(1, 4):
            tl_ = P.dma("gpsimd", lambda eng, j=j: [
                eng.dma_start(out=mp2[:, :], in_=meanp_all[j])], deps=[tac],
                tag=f"gf_m{j}")
            tac = P.step("vector", lambda v: v.tensor_add(
                mp[:, :], mp[:, :], mp2[:, :]), deps=[tl_, tac],
                tag=f"gf_a{j}")
        tsc = P.step("vector", lambda v: v.tensor_scalar_mul(
            mp[:, :], mp[:, :], 1.0 / (4 * TLOC)), deps=[tac], tag="gf_sc")
        tbf = P.step("vector", lambda v: v.tensor_copy(mbf[:, :], mp[:, :]),
                     deps=[tsc], tag="gf_bf")
        twl = P.dma("gpsimd", lambda eng: [eng.dma_start(
            out=wgc[:, 0:4096].rearrange("p (m k x) -> p m k x", m=4, k=8),
            in_=w_gc1.rearrange("m k p x -> p (m k) x" if False
                                else "m k p x -> p m k x"))],
            deps=fE, tag="gf_w1")
        g1 = []
        for m in range(4):
            def mm1(t, m=m):
                last = None
                for k in range(8):
                    last = t.matmul(
                        psf[m][0:128, 0:1],
                        wgc[:, (m * 8 + k) * 128:(m * 8 + k) * 128 + 128],
                        mbf[:, k:k + 1], start=(k == 0), stop=(k == 7))
                return last
            tmm = P.step("tensor", mm1, deps=[twl, tbf], tag=f"gf_mm1_{m}")
            tev = P.step("scalar", lambda sc, m=m: sc.activation(
                out=h1c[:, m:m + 1], in_=psf[m][0:128, 0:1], func=AF.Gelu,
                bias=bgc1_sb[:, m:m + 1], scale=1.0), deps=[tmm, tbl],
                tag=f"gf_ev1_{m}")
            g1.append(tev)
        twl2 = P.dma("gpsimd", lambda eng: [eng.dma_start(
            out=wgc[:, 0:1024].rearrange("p (m k x) -> p m k x", m=2, k=4),
            in_=w_gc2.rearrange("m k p x -> p m k x"))], deps=g1,
            tag="gf_w2")
        g2 = []
        for m in range(2):
            def mm2(t, m=m):
                last = None
                for k in range(4):
                    last = t.matmul(
                        psf[4][0:128, m:m + 1],
                        wgc[:, (m * 4 + k) * 128:(m * 4 + k) * 128 + 128],
                        h1c[:, k:k + 1], start=(k == 0), stop=(k == 3))
                return last
            tmm = P.step("tensor", mm2, deps=[twl2] + g1, tag=f"gf_mm2_{m}")
            tev = P.step("scalar", lambda sc, m=m: sc.activation(
                out=h2c[:, m:m + 1], in_=psf[4][0:128, m:m + 1],
                func=AF.Gelu, bias=bgc2_sb[:, m:m + 1], scale=1.0),
                deps=[tmm], tag=f"gf_ev2_{m}")
            g2.append(tev)
        twl3 = P.dma("gpsimd", lambda eng: [eng.dma_start(
            out=wgc[:, 0:2], in_=w_gc3)], deps=g2, tag="gf_w3")

        def mm3(t):
            last = None
            for k in range(2):
                last = t.matmul(psf[5][0:1, 0:1], wgc[:, k:k + 1],
                                h2c[:, k:k + 1], start=(k == 0),
                                stop=(k == 1))
            return last
        tmm3 = P.step("tensor", mm3, deps=[twl3] + g2, tag="gf_mm3")
        tgf = P.step("scalar", lambda sc: sc.activation(
            out=gfrow[0:1, 0:1], in_=psf[5][0:1, 0:1], func=AF.Sigmoid,
            bias=bgc3_sb[0:1, 0:1], scale=1.0), deps=[tmm3, tbl],
            tag="gf_sig")
        tk = P.chain("vector", [
            lambda v: v.tensor_scalar_mul(krow[0:1, 0:1], ar_sb[0:1, 0:1],
                                          0.6),
            lambda v: v.tensor_mul(krow[0:1, 1:2], ar_sb[0:1, 0:1],
                                   gfrow[0:1, 0:1]),
            lambda v: v.tensor_scalar_mul(krow[0:1, 1:2], krow[0:1, 1:2],
                                          1.4),
            lambda v: v.tensor_sub(krow[0:1, 1:2], krow[0:1, 1:2],
                                   ar_sb[0:1, 0:1]),
            lambda v: v.tensor_scalar_add(krow[0:1, 1:2], krow[0:1, 1:2],
                                          0.1),
        ], deps=[tgf], tag="gf_k")
    fF = P.fence()

    twist_src = [4, 5, 6, 7, 0, 1, 2, 3]
    twist_sgn = [+1, -1, -1, +1, -1, -1, +1, +1]
    twist_src = [4, 5, 6, 7, 0, 1, 2, 3]
    twist_sgn = [+1, +1, -1, -1, -1, -1, +1, +1]
    cur_f, cur_b = xn_f, xn_bf
    fcy = fF
    for cyc in range(3):
        with P.scope():
            sm1 = stream_matmul(P, tag=f"cn1_{cyc}", x_hbm=cur_b,
                                w_hbm=w_cn1, out_hbm=h1b, KB=8, M=1024,
                                TLOC=TLOC, TP=TPX, psb=psf[0:6],
                                act=AF.Gelu, bias_sb=bcn1_sb, deps=fcy)
        f_1 = P.fence()
        with P.scope():
            sm2 = stream_matmul(P, tag=f"cn2_{cyc}", x_hbm=h1b, w_hbm=w_cn2,
                                out_hbm=h2b, KB=8, M=512, TLOC=TLOC, TP=TPX,
                                psb=psf[0:6], act=AF.Gelu, bias_sb=bcn2_sb,
                                deps=f_1)
        f_2 = P.fence()
        with P.scope():
            sm3 = stream_matmul(P, tag=f"cn3_{cyc}", x_hbm=h2b, w_hbm=w_cn3,
                                out_hbm=h3b, KB=4, M=256, TLOC=TLOC, TP=TPX,
                                psb=psf[0:6], act=AF.Gelu, bias_sb=bcn3_sb,
                                deps=f_2)
        f_3 = P.fence()
        with P.scope():
            sm4 = stream_matmul(P, tag=f"cn4_{cyc}", x_hbm=h3b, w_hbm=w_cn4,
                                out_hbm=tfr, KB=2, M=1, TLOC=TLOC, TP=TPX,
                                psb=psf[0:6], act=AF.Sigmoid,
                                bias_sb=bcn4_sb, out_dt=F32, deps=f_3)
        f_4 = P.fence()
        with P.scope():
            tf_sb = P.sbuf(f"tw_tf{cyc}", [1, TLOC], F32)
            crow = P.sbuf(f"tw_cr{cyc}", [1, TLOC], F32)
            ot = [P.sbuf(f"tw_o{cyc}_{i}", [128, 512], F32) for i in range(8)]
            nt_ = [P.sbuf(f"tw_n{cyc}_{i}", [128, 512], F32)
                   for i in range(8)]
            ttf = P.dma("sync", lambda eng: [
                eng.dma_start(out=tf_sb[:, :], in_=tfr[0])], deps=f_4,
                tag=f"tw_tf{cyc}")
            tcr = P.step("vector", lambda v: v.tensor_scalar(
                out=crow[:, :], in0=tf_sb[:, :], scalar1=krow[0:1, 0:1],
                scalar2=krow[0:1, 1:2], op0=ALU.mult, op1=ALU.add),
                deps=[ttf, tk], tag=f"tw_cr{cyc}")
            gate = list(f_4) + [tcr]
            sts = []
            for tch in range(TLOC // 512):
                sl = slice(tch * 512, (tch + 1) * 512)
                tld = P.dma("sync", lambda eng, sl=sl: [
                    eng.dma_start(out=ot[k][:, :], in_=cur_f[k, :, sl])
                    for k in range(8)], deps=gate, tag=f"tw_ld{cyc}_{tch}")
                tbc = P.step("tensor", lambda t, sl=sl: t.matmul(
                    psf[4][:, :], ones1f[0:1, :], crow[0:1, sl],
                    start=True, stop=True), deps=[tcr] + gate,
                    tag=f"tw_bc{cyc}_{tch}")
                ks = []
                for k in range(8):
                    sgn = twist_sgn[k]
                    srk = twist_src[k]
                    ts1 = P.step("vector", lambda v, k=k, srk=srk: v.tensor_tensor(
                        out=nt_[k][:, :], in0=ot[srk][:, :], in1=psf[4][:, :],
                        op=ALU.mult), deps=[tld, tbc], tag=f"tw_m{cyc}_{tch}_{k}")
                    ts2 = P.step("vector", lambda v, k=k, sgn=sgn: v.tensor_tensor(
                        out=nt_[k][:, :], in0=ot[k][:, :], in1=nt_[k][:, :],
                        op=(ALU.add if sgn > 0 else ALU.subtract)),
                        deps=[ts1], tag=f"tw_a{cyc}_{tch}_{k}")
                    ks.append(ts2)
                nf, nb = outk_f[cyc % 2], outk_b[cyc % 2]
                st1_ = P.dma("sync", lambda eng, sl=sl, nf=nf: [
                    eng.dma_start(out=nf[k, :, sl], in_=nt_[k][:, :])
                    for k in range(8)], deps=ks, tag=f"tw_st{cyc}_{tch}")
                st2_ = P.dma("gpsimd", lambda eng, sl=sl, nb=nb: [
                    eng.dma_start(out=nb[k, :, sl], in_=nt_[k][:, :])
                    for k in range(8)], deps=ks, tag=f"tw_sb{cyc}_{tch}")
                sts += [st1_, st2_]
                gate = ks + [st1_, st2_]
        cur_f, cur_b = outk_f[cyc % 2], outk_b[cyc % 2]
        fcy = P.fence()

    with P.scope():
        def em_x1(P_, k, bb, ob, dd):
            tg = P_.step("scalar", lambda sc: sc.activation(
                out=ob, in_=bb[2], func=AF.Silu), deps=dd, tag=f"x1_g{k}")
            tm = P_.step("vector", lambda v: v.tensor_mul(ob, ob, bb[1]),
                         deps=[tg], tag=f"x1_m{k}")
            ta = P_.step("vector", lambda v: v.tensor_add(ob, ob, bb[0]),
                         deps=[tm], tag=f"x1_a{k}")
            return [ta]
        stx = ew_phase(P, tag="x1", srcs=[(x, F32), (s_f, F32),
                                          (cur_f, F32)], KB=8, TLOC=TLOC,
                       emit=em_x1, out_specs=[(x1, "sync")], deps=fcy)
    return dict(x1=x1)


def host_prep_l0b(inputs, TLOC=TL1):
    d = {}
    d["logitsT"] = np.ascontiguousarray(
        np.asarray(inputs["L0_attn_base_decay_logits"]).T)
    d["ramp_g"] = np.tile(np.arange(1, TLOC + 1, dtype=np.float32),
                          (16, 1))
    d["w_out"] = wblk(np.asarray(inputs["L0_attn_out_w"]).T).astype(BF)
    d["outb"] = np.ascontiguousarray(
        np.asarray(inputs["L0_attn_out_b"]).reshape(8, 128).T)
    d["lng"] = np.ascontiguousarray(
        np.asarray(inputs["L0_attn_ln_g"]).reshape(8, 128).T)
    d["lnb"] = np.ascontiguousarray(
        np.asarray(inputs["L0_attn_ln_b"]).reshape(8, 128).T)
    for i in (1, 2, 3, 4):
        w = np.asarray(inputs[f"L0_ffn_cn_w{i}"])
        bb_ = np.asarray(inputs[f"L0_ffn_cn_b{i}"])
        d[f"w_cn{i}"] = wblk(w).astype(BF) if i < 4 else wblk(w)
        nb = max(1, bb_.shape[0] // 128)
        d[f"b_cn{i}"] = np.ascontiguousarray(
            bb_.reshape(nb, -1).T) if bb_.shape[0] >= 128 else np.tile(
            bb_, (128, 1)).astype(np.float32)
    for i in (1, 2, 3):
        w = np.asarray(inputs[f"L0_ffn_gc_w{i}"])
        bb_ = np.asarray(inputs[f"L0_ffn_gc_b{i}"])
        d[f"w_gc{i}"] = wblk(w).astype(BF) if i < 3 else wblk(w)
        d[f"b_gc{i}"] = np.ascontiguousarray(
            bb_.reshape(-1, 128).T) if bb_.shape[0] >= 128 else np.tile(
            bb_, (128, 1)).astype(np.float32)
    d["ar_in"] = np.asarray(
        inputs["L0_ffn_adaptive_range"]).reshape(1, 1).astype(np.float32)
    return d



def emit_l1a(P, TLOC, ext):
    TPX = min(2048, TLOC)
    x1 = ext["x1"]
    x1bnd = ext["x1bnd"]
    w_k = P.dram("w_k", [8, 8, 128, 128], BF16, kind="ExternalInput")
    w_v = P.dram("w_v", [8, 8, 128, 128], BF16, kind="ExternalInput")
    w_r = P.dram("w_r", [8, 8, 128, 128], BF16, kind="ExternalInput")
    tmk = P.dram("tmk", [128, 8], F32, kind="ExternalInput")
    tmv = P.dram("tmv", [128, 8], F32, kind="ExternalInput")
    tmr = P.dram("tmr", [128, 8], F32, kind="ExternalInput")
    tdec = P.dram("tdec", [128, 8], F32, kind="ExternalInput")
    xn1_bf = P.dram("xn1_bf", [8, 128, TLOC], BF16)
    v_f = P.dram("v_f", [8, 128, TLOC], F32)
    r_f = P.dram("r_f", [8, 128, TLOC], F32)
    S_loc = P.dram("S_loc", [8, 128, TLOC], F32)
    Z_loc = P.dram("Z_loc", [8, 128, TLOC], F32)
    P_loc = P.dram("P_loc", [8, 128, TLOC], F32)
    carry = P.dram("carry", [8, 128, 4], F32)
    xn1_ext = P.dram("xn1_ext", [8, 128, TLOC + 1], F32)
    kin_b = P.dram("kin_b", [8, 128, TLOC], BF16)
    vin_b = P.dram("vin_b", [8, 128, TLOC], BF16)
    rin_b = P.dram("rin_b", [8, 128, TLOC], BF16)
    k_f = P.dram("k_f", [8, 128, TLOC], F32)

    psf = [P.psum(f"ps{i}", [128, 512], F32) for i in range(6)]

    onesb = P.sbuf("onesb", [128, 1], BF16)
    ones1f = P.sbuf("ones1f", [1, 128], F32)
    tmk_sb = P.sbuf("tmk_sb", [128, 8], F32)
    tmv_sb = P.sbuf("tmv_sb", [128, 8], F32)
    tmr_sb = P.sbuf("tmr_sb", [128, 8], F32)
    tdec_sb = P.sbuf("tdec_sb", [128, 8], F32)
    tco = P.step("vector", lambda v: v.memset(onesb[:, :], 1.0), tag="c0")
    tco2 = P.step("vector", lambda v: v.memset(ones1f[:, :], 1.0), tag="c1")
    tcl = P.dma("sync", lambda eng: [
        eng.dma_start(out=tmk_sb[:, :], in_=tmk),
        eng.dma_start(out=tmv_sb[:, :], in_=tmv),
        eng.dma_start(out=tmr_sb[:, :], in_=tmr),
        eng.dma_start(out=tdec_sb[:, :], in_=tdec),
    ], tag="consts")
    c0 = [tco, tco2, tcl]

    with P.scope():
        lnb_, lnf_, _ = ln_phase(
            P, tag="ln1", x_hbm=x1, KB=8, TLOC=TLOC, out_bf=xn1_bf,
            out_f32=xn1_ext[:, :, 1:], kind="plain", psb=psf[0:4],
            ones_col_bf=onesb, ones1_f32=ones1f, deps=c0)
    fA = P.fence()
    with P.scope():
        bcol = P.sbuf("bln_c", [128, 8], F32)
        bbf = P.sbuf("bln_b", [128, 8], BF16)
        bsq = P.sbuf("bln_q", [128, 8], BF16)
        brow = P.sbuf("bln_r", [1, 8], F32)
        bout = P.sbuf("bln_o", [128, 8], F32)
        tld = P.dma("sync", lambda eng: [eng.dma_start(out=bcol[:, :],
                                                       in_=x1bnd)],
                    deps=fA, tag="bln_ld")
        tbf = P.step("vector", lambda v: v.tensor_copy(bbf[:, :],
                                                       bcol[:, :]),
                     deps=[tld], tag="bln_bf")
        tsq = P.step("scalar", lambda sc: sc.activation(
            out=bsq[:, :], in_=bcol[:, :], func=AF.Square), deps=[tld],
            tag="bln_sq")
        tmm = P.step("tensor", lambda t: [
            t.matmul(psf[0][0:1, 0:8], onesb[:, 0:1], bbf[:, :],
                     start=True, stop=True),
            t.matmul(psf[1][0:1, 0:8], onesb[:, 0:1], bsq[:, :],
                     start=True, stop=True)][-1], deps=[tbf, tsq] + fA,
            tag="bln_mm")
        trd = P.chain("vector", [
            lambda v: v.reduce_sum(brow[0:1, 0:1], psf[0][0:1, 0:8],
                                   axis=AXX),
            lambda v: v.reduce_sum(brow[0:1, 1:2], psf[1][0:1, 0:8],
                                   axis=AXX),
        ], deps=[tmm], tag="bln_rd")
        tmu = P.chain("scalar", [
            lambda sc: sc.activation(out=brow[0:1, 2:3], in_=brow[0:1, 0:1],
                                     func=AF.Identity,
                                     scale=P.cval(-1.0 / 1024)[0:1, :]),
            lambda sc: sc.activation(out=brow[0:1, 3:4], in_=brow[0:1, 1:2],
                                     func=AF.Identity,
                                     scale=P.cval(1.0 / 1024)[0:1, :]),
            lambda sc: sc.activation(out=brow[0:1, 4:5], in_=brow[0:1, 2:3],
                                     func=AF.Square),
        ], deps=[trd], tag="bln_mu")
        tvar = P.chain("vector", [
            lambda v: v.tensor_sub(brow[0:1, 3:4], brow[0:1, 3:4],
                                   brow[0:1, 4:5]),
        ], deps=[tmu], tag="bln_v")
        tsd = P.step("scalar", lambda sc: sc.activation(
            out=brow[0:1, 5:6], in_=brow[0:1, 3:4], func=AF.Sqrt,
            bias=P.cval(1e-5)[0:1, :]), deps=[tvar], tag="bln_sd")
        trc = P.step("vector", lambda v: v.reciprocal(
            brow[0:1, 6:7], brow[0:1, 5:6]), deps=[tsd], tag="bln_rc")
        tbc = P.step("tensor", lambda t: [
            t.matmul(psf[2][:, 0:1], ones1f[0:1, :], brow[0:1, 2:3],
                     start=True, stop=True),
            t.matmul(psf[3][:, 0:1], ones1f[0:1, :], brow[0:1, 6:7],
                     start=True, stop=True)][-1], deps=[trc, tmu],
            tag="bln_bc")
        tns = []
        for k in range(8):
            tn = P.chain("vector", [
                lambda v, k=k: v.tensor_tensor(
                    out=bout[:, k:k + 1], in0=bcol[:, k:k + 1],
                    in1=psf[2][:, 0:1], op=ALU.add),
                lambda v, k=k: v.tensor_tensor(
                    out=bout[:, k:k + 1], in0=bout[:, k:k + 1],
                    in1=psf[3][:, 0:1], op=ALU.mult),
            ], deps=[tbc, tld], tag=f"bln_n{k}")
            tns.append(tn)
        tst = P.dma("sync", lambda eng: [
            eng.dma_start(out=xn1_ext[k, :, 0:1], in_=bout[:, k:k + 1],
                          allow_slow_non_contiguous=True)
            for k in range(8)], deps=tns, tag="bln_st")
    fB = P.fence()
    with P.scope():
        CH = 512
        NT = TLOC // CH
        sx = [P.sbuf(f"tm_x{i}", [128, 8 * (CH + 1)], F32) for i in range(2)]
        sd = P.sbuf("tm_d", [128, 8 * CH], F32)
        so = [P.sbuf(f"tm_o{j}", [128, 8 * CH], F32) for j in range(3)]
        gate = [list(fB), list(fB)]
        sts = []
        for t in range(NT):
            bi = t % 2
            t0 = t * CH
            tl = P.dma("sync", lambda eng, bi=bi, t0=t0: [
                eng.dma_start(out=sx[bi][:, k * (CH + 1):(k + 1) * (CH + 1)],
                              in_=xn1_ext[k, :, t0:t0 + CH + 1])
                for k in range(8)], deps=gate[bi], tag=f"tm_ld{t}")
            ks = []
            for k in range(8):
                cur = sx[bi][:, k * (CH + 1) + 1:(k + 1) * (CH + 1)]
                sh = sx[bi][:, k * (CH + 1):(k + 1) * (CH + 1) - 1]
                dv_ = sd[:, k * CH:(k + 1) * CH]
                td_ = P.step("vector", lambda v, cur=cur, sh=sh, dv_=dv_:
                             v.tensor_sub(dv_, cur, sh), deps=[tl] + gate[bi],
                             tag=f"tm_d{t}_{k}")
                for j, (tmx, ob) in enumerate([(tmk_sb, so[0]),
                                               (tmv_sb, so[1]),
                                               (tmr_sb, so[2])]):
                    tj = P.step("vector", lambda v, dv_=dv_, sh=sh, k=k,
                                tmx=tmx, ob=ob: v.scalar_tensor_tensor(
                        out=ob[:, k * CH:(k + 1) * CH], in0=dv_,
                        scalar=tmx[:, k:k + 1], in1=sh, op0=ALU.mult,
                        op1=ALU.add), deps=[td_], tag=f"tm_o{t}_{k}_{j}")
                    ks.append(tj)
            st0 = P.dma("gpsimd", lambda eng, t0=t0: [
                eng.dma_start(out=kin_b[k, :, t0:t0 + CH],
                              in_=so[0][:, k * CH:(k + 1) * CH])
                for k in range(8)], deps=ks, tag=f"tm_s0{t}")
            st1 = P.dma("gpsimd", lambda eng, t0=t0: [
                eng.dma_start(out=vin_b[k, :, t0:t0 + CH],
                              in_=so[1][:, k * CH:(k + 1) * CH])
                for k in range(8)], deps=ks, tag=f"tm_s1{t}")
            st2 = P.dma("gpsimd", lambda eng, t0=t0: [
                eng.dma_start(out=rin_b[k, :, t0:t0 + CH],
                              in_=so[2][:, k * CH:(k + 1) * CH])
                for k in range(8)], deps=ks, tag=f"tm_s2{t}")
            sts += [st0, st1, st2]
            gate[bi] = ks + [st0, st1, st2]
    fC = P.fence()
    with P.scope():
        s1 = stream_matmul(P, tag="kp", x_hbm=kin_b, w_hbm=w_k, out_hbm=k_f,
                           KB=8, M=1024, TLOC=TLOC, TP=TPX, psb=psf[0:6],
                           out_dt=F32, deps=fC)
    fD = P.fence()
    with P.scope():
        s2 = stream_matmul(P, tag="vp", x_hbm=vin_b, w_hbm=w_v, out_hbm=v_f,
                           KB=8, M=1024, TLOC=TLOC, TP=TPX, psb=psf[0:6],
                           out_dt=F32, deps=fD)
    fE = P.fence()
    with P.scope():
        s3 = stream_matmul(P, tag="rp", x_hbm=rin_b, w_hbm=w_r, out_hbm=r_f,
                           KB=8, M=1024, TLOC=TLOC, TP=TPX, psb=psf[0:6],
                           act=AF.Sigmoid, out_dt=F32, deps=fE)
    fF = P.fence()
    with P.scope():
        kt = P.sbuf("wk_k", [128, TLOC], F32)
        vt = P.sbuf("wk_v", [128, TLOC], F32)
        Pt = P.sbuf("wk_P", [128, TLOC], F32)
        bt = P.sbuf("wk_b", [128, TLOC], F32)
        St = P.sbuf("wk_S", [128, TLOC], F32)
        Zt = P.sbuf("wk_Z", [128, TLOC], F32)
        Dt = P.sbuf("wk_D", [128, TLOC], F32)
        mcol = P.sbuf("wk_m", [128, 4], F32)
        dcy = P.sbuf("wk_dc", [128, 2], F32)
        zcol = P.sbuf("wk_z", [128, 1], F32)
        tz = P.step("vector", lambda v: v.memset(zcol[:, :], 0.0), deps=fF,
                    tag="wk_z0")
        hg = list(fF) + [tz]
        for k in range(8):
            tl = P.dma("sync", lambda eng, k=k: [
                eng.dma_start(out=kt[:, :], in_=k_f[k]),
                eng.dma_start(out=vt[:, :], in_=v_f[k]),
            ], deps=hg, tag=f"wk_ld{k}")
            tdc = P.chain("scalar", [
                lambda sc, k=k: sc.activation(out=dcy[:, 0:1],
                                              in_=tdec_sb[:, k:k + 1],
                                              func=AF.Exp),
                lambda sc: sc.activation(out=dcy[:, 1:2], in_=dcy[:, 0:1],
                                         func=AF.Exp,
                                         scale=P.cval(-1.0)),
            ], deps=hg, tag=f"wk_dc{k}")
            tdt = P.step("vector", lambda v: v.tensor_copy(
                Dt[:, :], dcy[:, 1:2].broadcast_to([128, TLOC])),
                deps=[tdc], tag=f"wk_dt{k}")
            tmx = P.step("vector", lambda v: v.reduce_max(
                mcol[:, 0:1], kt[:, :], axis=AXX), deps=[tl] + hg,
                tag=f"wk_mx{k}")
            tmn = P.step("vector", lambda v: v.tensor_scalar_mul(
                mcol[:, 1:2], mcol[:, 0:1], -1.0), deps=[tmx],
                tag=f"wk_mn{k}")
            tP = P.step("scalar", lambda sc: sc.activation(
                out=Pt[:, :], in_=kt[:, :], func=AF.Exp,
                bias=mcol[:, 1:2]), deps=[tmn, tl], tag=f"wk_P{k}")
            tb = P.step("vector", lambda v: v.tensor_mul(
                bt[:, :], Pt[:, :], vt[:, :]), deps=[tP, tl],
                tag=f"wk_b{k}")
            tS = P.step("vector", lambda v: v.tensor_tensor_scan(
                out=St[:, :], data0=Dt[:, :], data1=bt[:, :],
                initial=zcol[:, 0:1], op0=ALU.mult, op1=ALU.add),
                deps=[tb, tdt], tag=f"wk_S{k}")
            tZ = P.step("vector", lambda v: v.tensor_tensor_scan(
                out=Zt[:, :], data0=Dt[:, :], data1=Pt[:, :],
                initial=zcol[:, 0:1], op0=ALU.mult, op1=ALU.add),
                deps=[tS, tP], tag=f"wk_Z{k}")
            sts = [
                P.dma("sync", lambda eng, k=k: [
                    eng.dma_start(out=S_loc[k], in_=St[:, :]),
                    eng.dma_start(out=Z_loc[k], in_=Zt[:, :]),
                ], deps=[tZ], tag=f"wk_sS{k}"),
                P.dma("gpsimd", lambda eng, k=k: [
                    eng.dma_start(out=P_loc[k], in_=Pt[:, :]),
                    eng.dma_start(out=carry[k, :, 0:1],
                                  in_=St[:, TLOC - 1:TLOC],
                                  allow_slow_non_contiguous=True),
                    eng.dma_start(out=carry[k, :, 1:2],
                                  in_=Zt[:, TLOC - 1:TLOC],
                                  allow_slow_non_contiguous=True),
                    eng.dma_start(out=carry[k, :, 2:3], in_=mcol[:, 0:1],
                                  allow_slow_non_contiguous=True),
                ], deps=[tZ], tag=f"wk_sP{k}"),
            ]
            hg = P.red(sts + [tZ, tb])
    return dict(xn1_bf=xn1_bf, v_f=v_f, r_f=r_f, S_loc=S_loc, Z_loc=Z_loc,
                P_loc=P_loc, carry=carry)


def host_prep_l1a(inputs, TLOC=TL1):
    d = {}
    d["w_k"] = wblk(np.asarray(inputs["L1_attn_Wk"])).astype(BF)
    d["w_v"] = wblk(np.asarray(inputs["L1_attn_Wv"])).astype(BF)
    d["w_r"] = wblk(np.asarray(inputs["L1_attn_Wr"])).astype(BF)
    for nm, key in [("tmk", "L1_attn_time_mix_key"),
                    ("tmv", "L1_attn_time_mix_value"),
                    ("tmr", "L1_attn_time_mix_receptance"),
                    ("tdec", "L1_attn_time_decay")]:
        d[nm] = np.ascontiguousarray(
            np.asarray(inputs[key]).reshape(8, 128).T)
    return d



def emit_l1b(P, TLOC, ext):
    TPX = min(2048, TLOC)
    x1 = ext["x1"]
    xn1_bf = ext["xn1_bf"]
    v_f = ext["v_f"]
    r_f = ext["r_f"]
    S_loc = ext["S_loc"]
    Z_loc = ext["Z_loc"]
    P_loc = ext["P_loc"]
    carr = ext["carr"]
    mui = ext["mui"]
    rjs = P.dram("rjs", [128, 4], F32, kind="ExternalInput")
    rjm = P.dram("rjm", [128, 4], F32, kind="ExternalInput")
    rampT = P.dram("rampT", [128, TLOC], F32, kind="ExternalInput")
    tdec = P.dram("tdec", [128, 8], F32, kind="ExternalInput")
    tfir = P.dram("tfir", [128, 8], F32, kind="ExternalInput")
    w_o = P.dram("w_o", [8, 8, 128, 128], BF16, kind="ExternalInput")
    w_1 = P.dram("w_1", [32, 8, 128, 128], BF16, kind="ExternalInput")
    w_2 = P.dram("w_2", [32, 8, 128, 128], BF16, kind="ExternalInput")
    w_h1 = P.dram("w_h1", [32, 32, 128, 128], BF16, kind="ExternalInput")
    w_h2 = P.dram("w_h2", [32, 32, 128, 128], BF16, kind="ExternalInput")
    w_3 = P.dram("w_3", [8, 32, 128, 128], BF16, kind="ExternalInput")
    s_1 = P.dram("s_1", [128, 32], F32, kind="ExternalInput")
    s_2 = P.dram("s_2", [128, 32], F32, kind="ExternalInput")
    s_h1 = P.dram("s_h1", [128, 32], F32, kind="ExternalInput")
    s_h2 = P.dram("s_h2", [128, 32], F32, kind="ExternalInput")
    s_3 = P.dram("s_3", [128, 8], F32, kind="ExternalInput")
    x2 = P.dram("x2", [8, 128, TLOC], F32)
    rwkv_b = P.dram("rwkv_b", [8, 128, TLOC], BF16)
    s1f = P.dram("s1f", [8, 128, TLOC], F32)
    t1b = P.dram("t1b", [32, 128, TLOC], BF16)
    t2b = P.dram("t2b", [32, 128, TLOC], BF16)
    hb = P.dram("hb", [32, 128, TLOC], BF16)
    h2b_ = P.dram("h2b_", [32, 128, TLOC], BF16)
    h3b_ = P.dram("h3b_", [32, 128, TLOC], BF16)
    f_f = P.dram("f_f", [8, 128, TLOC], F32)

    psf = [P.psum(f"ps{i}", [128, 512], F32) for i in range(6)]
    rjs_sb = P.sbuf("rjs_sb", [128, 4], F32)
    rjm_sb = P.sbuf("rjm_sb", [128, 4], F32)
    tdec_sb = P.sbuf("tdec_sb", [128, 8], F32)
    tfir_sb = P.sbuf("tfir_sb", [128, 8], F32)
    mui_sb = P.sbuf("mui_sb", [128, 8], F32)
    sc1_sb = P.sbuf("sc1_sb", [128, 32], F32)
    sc2_sb = P.sbuf("sc2_sb", [128, 32], F32)
    sch1_sb = P.sbuf("sch1_sb", [128, 32], F32)
    sch2_sb = P.sbuf("sch2_sb", [128, 32], F32)
    sc3_sb = P.sbuf("sc3_sb", [128, 8], F32)
    tcl = P.dma("sync", lambda eng: [
        eng.dma_start(out=rjs_sb[:, :], in_=rjs),
        eng.dma_start(out=rjm_sb[:, :], in_=rjm),
        eng.dma_start(out=tdec_sb[:, :], in_=tdec),
        eng.dma_start(out=tfir_sb[:, :], in_=tfir),
        eng.dma_start(out=mui_sb[:, :], in_=mui),
        eng.dma_start(out=sc1_sb[:, :], in_=s_1),
        eng.dma_start(out=sc2_sb[:, :], in_=s_2),
        eng.dma_start(out=sch1_sb[:, :], in_=s_h1),
        eng.dma_start(out=sch2_sb[:, :], in_=s_h2),
        eng.dma_start(out=sc3_sb[:, :], in_=s_3),
    ], tag="consts")
    c0 = [tcl]

    with P.scope():
        Se = P.sbuf("pb_S", [128, TLOC + 1], F32)
        Ze = P.sbuf("pb_Z", [128, TLOC + 1], F32)
        Pt = P.sbuf("pb_P", [128, TLOC], F32)
        vt = P.sbuf("pb_v", [128, TLOC], F32)
        rt = P.sbuf("pb_r", [128, TLOC], F32)
        Et = P.sbuf("pb_E", [128, TLOC], F32)
        rmp = P.sbuf("pb_rmp", [128, TLOC], F32)
        num = P.sbuf("pb_num", [128, TLOC], F32)
        den = P.sbuf("pb_den", [128, TLOC], F32)
        wcol = P.sbuf("pb_w", [128, 12], F32)
        crr = P.sbuf("pb_crr", [128, 16], F32)
        trmp = P.dma("sync", lambda eng: [eng.dma_start(out=rmp[:, :],
                                                        in_=rampT)],
                     deps=c0, tag="pb_rmp")
        hg = list(c0) + [trmp]
        for k in range(8):
            tl = P.dma("sync", lambda eng, k=k: [
                eng.dma_start(out=Se[:, 1:], in_=S_loc[k]),
                eng.dma_start(out=Ze[:, 1:], in_=Z_loc[k]),
                eng.dma_start(out=Pt[:, :], in_=P_loc[k]),
                eng.dma_start(out=vt[:, :], in_=v_f[k]),
                eng.dma_start(out=rt[:, :], in_=r_f[k]),
            ], deps=hg, tag=f"pb_ld{k}")
            tlc = P.dma("gpsimd", lambda eng, k=k: [
                eng.dma_start(out=crr[:, 4 * j:4 * j + 4], in_=carr[j, k])
                for j in range(4)], deps=hg, tag=f"pb_lc{k}")
            tz = P.chain("vector", [
                lambda v: v.memset(Se[:, 0:1], 0.0),
                lambda v: v.memset(Ze[:, 0:1], 0.0),
                lambda v: v.memset(wcol[:, 4:5], 0.0),
                lambda v: v.memset(wcol[:, 5:6], 0.0),
            ], deps=[tl], tag=f"pb_z{k}")
            ttd = P.chain("scalar", [
                lambda sc, k=k: sc.activation(out=wcol[:, 6:7],
                                              in_=tdec_sb[:, k:k + 1],
                                              func=AF.Exp),
                lambda sc: sc.mul(out=wcol[:, 7:8], in_=wcol[:, 6:7],
                                  mul=-1.0),
                lambda sc, k=k: sc.activation(out=wcol[:, 8:9],
                                              in_=tfir_sb[:, k:k + 1],
                                              func=AF.Exp),
            ], deps=hg + [tl, tz], tag=f"pb_td{k}")
            tacc = tz
            for j in range(4):
                tc = P.chain("vector", [
                    lambda v, j=j: v.tensor_mul(
                        wcol[:, 0:1], wcol[:, 7:8],
                        rjs_sb[:, j:j + 1]),
                    lambda v, j=j: v.tensor_add(
                        wcol[:, 0:1], wcol[:, 0:1],
                        crr[:, 4 * j + 2:4 * j + 3]),
                    lambda v, k=k: v.tensor_sub(
                        wcol[:, 0:1], wcol[:, 0:1], mui_sb[:, k:k + 1]),
                ], deps=[tacc, tlc, ttd], tag=f"pb_a{k}_{j}")
                te = P.step("scalar", lambda sc, j=j: sc.activation(
                    out=wcol[:, 1:2], in_=wcol[:, 0:1], func=AF.Exp),
                    deps=[tc], tag=f"pb_e{k}_{j}")
                tm = P.chain("vector", [
                    lambda v, j=j: v.tensor_mul(
                        wcol[:, 1:2], wcol[:, 1:2], rjm_sb[:, j:j + 1]),
                    lambda v, j=j: v.tensor_mul(
                        wcol[:, 2:3], wcol[:, 1:2],
                        crr[:, 4 * j:4 * j + 1]),
                    lambda v: v.tensor_add(wcol[:, 4:5], wcol[:, 4:5],
                                           wcol[:, 2:3]),
                    lambda v, j=j: v.tensor_mul(
                        wcol[:, 3:4], wcol[:, 1:2],
                        crr[:, 4 * j + 1:4 * j + 2]),
                    lambda v: v.tensor_add(wcol[:, 5:6], wcol[:, 5:6],
                                           wcol[:, 3:4]),
                ], deps=[te], tag=f"pb_m{k}_{j}")
                tacc = tm
            tE = P.step("scalar", lambda sc: sc.activation(
                out=Et[:, :], in_=rmp[:, :], func=AF.Exp,
                scale=wcol[:, 7:8]), deps=[ttd, trmp], tag=f"pb_E{k}")
            tn1 = P.step("vector", lambda v: v.scalar_tensor_tensor(
                out=num[:, :], in0=Et[:, :], scalar=wcol[:, 4:5],
                in1=Se[:, 0:TLOC], op0=ALU.mult, op1=ALU.add),
                deps=[tE, tacc, tz], tag=f"pb_n1{k}")
            td1 = P.step("vector", lambda v: v.scalar_tensor_tensor(
                out=den[:, :], in0=Et[:, :], scalar=wcol[:, 5:6],
                in1=Ze[:, 0:TLOC], op0=ALU.mult, op1=ALU.add),
                deps=[tE, tacc, tz], tag=f"pb_d1{k}")
            tb = P.step("gpsimd", lambda g: g.tensor_mul(
                Et[:, :], Pt[:, :], vt[:, :]), deps=[tE, tn1, td1, tl],
                tag=f"pb_b{k}")
            tn2 = P.step("vector", lambda v: v.scalar_tensor_tensor(
                out=num[:, :], in0=Et[:, :], scalar=wcol[:, 8:9],
                in1=num[:, :], op0=ALU.mult, op1=ALU.add),
                deps=[tb, tn1, ttd], tag=f"pb_n2{k}")
            td2 = P.step("vector", lambda v: v.scalar_tensor_tensor(
                out=den[:, :], in0=Pt[:, :], scalar=wcol[:, 8:9],
                in1=den[:, :], op0=ALU.mult, op1=ALU.add),
                deps=[td1, ttd, tl], tag=f"pb_d2{k}")
            trc = P.step("vector", lambda v: v.reciprocal(
                den[:, :], den[:, :]), deps=[td2], tag=f"pb_rc{k}")
            twk = P.step("vector", lambda v: v.tensor_mul(
                num[:, :], num[:, :], den[:, :]), deps=[trc, tn2],
                tag=f"pb_wk{k}")
            trr = P.step("vector", lambda v: v.tensor_mul(
                num[:, :], num[:, :], rt[:, :]), deps=[twk, tl],
                tag=f"pb_rr{k}")
            tst = P.dma("gpsimd", lambda eng, k=k: [
                eng.dma_start(out=rwkv_b[k], in_=num[:, :])], deps=[trr],
                tag=f"pb_st{k}")
            hg = P.red([tst, trr, tb, td2, tE, tacc])
    fPB = P.fence()
    with P.scope():
        sm_o = stream_matmul(P, tag="oproj", x_hbm=rwkv_b, w_hbm=w_o,
                             out_hbm=s1f, KB=8, M=1024, TLOC=TLOC, TP=TPX,
                             psb=psf[0:6], out_dt=F32, deps=fPB)
    f1 = P.fence()
    with P.scope():
        stream_matmul(P, tag="ff1", x_hbm=xn1_bf, w_hbm=w_1, out_hbm=t1b,
                      KB=8, M=4096, TLOC=TLOC, TP=TPX, psb=psf[0:6],
                      act=AF.Silu, scale_sb=sc1_sb, deps=f1)
    f2 = P.fence()
    with P.scope():
        stream_matmul(P, tag="ff2", x_hbm=xn1_bf, w_hbm=w_2, out_hbm=t2b,
                      KB=8, M=4096, TLOC=TLOC, TP=TPX, psb=psf[0:6],
                      act=AF.Identity, scale_sb=sc2_sb, deps=f2)
    f3 = P.fence()
    def em_mul(P_, k, bb, ob, dd):
        t = P_.step("vector", lambda v: v.tensor_mul(ob, bb[0], bb[1]),
                    deps=dd, tag=f"hmul_{k}")
        return [t]
    for g4 in range(4):
        with P.scope():
            ew_phase(P, tag=f"hm{g4}",
                     srcs=[(t1b[8 * g4:8 * g4 + 8], BF16),
                           (t2b[8 * g4:8 * g4 + 8], BF16)],
                     KB=8, TLOC=TLOC, emit=em_mul,
                     out_specs=[(hb[8 * g4:8 * g4 + 8], "gpsimd")],
                     deps=f3)
        f3 = P.fence()
    f4 = P.fence()
    with P.scope():
        stream_matmul(P, tag="fh1", x_hbm=hb, w_hbm=w_h1, out_hbm=h2b_,
                      KB=32, M=4096, TLOC=TLOC, TP=TPX, psb=psf[0:6],
                      act=AF.Gelu, scale_sb=sch1_sb, deps=f4)
    f5 = P.fence()
    with P.scope():
        stream_matmul(P, tag="fh2", x_hbm=h2b_, w_hbm=w_h2, out_hbm=h3b_,
                      KB=32, M=4096, TLOC=TLOC, TP=TPX, psb=psf[0:6],
                      act=AF.Gelu, scale_sb=sch2_sb, deps=f5)
    f6 = P.fence()
    with P.scope():
        stream_matmul(P, tag="ff3", x_hbm=h3b_, w_hbm=w_3, out_hbm=f_f,
                      KB=32, M=1024, TLOC=TLOC, TP=TPX, psb=psf[0:6],
                      act=AF.Identity, scale_sb=sc3_sb, out_dt=F32, deps=f6)
    f7 = P.fence()
    with P.scope():
        def em_x2(P_, k, bb, ob, dd):
            tg = P_.step("scalar", lambda sc: sc.activation(
                out=ob, in_=bb[2], func=AF.Silu), deps=dd, tag=f"x2_g{k}")
            tm = P_.step("vector", lambda v: v.tensor_mul(ob, ob, bb[1]),
                         deps=[tg], tag=f"x2_m{k}")
            ta = P_.step("vector", lambda v: v.tensor_add(ob, ob, bb[0]),
                         deps=[tm], tag=f"x2_a{k}")
            return [ta]
        ew_phase(P, tag="x2", srcs=[(x1, F32), (s1f, F32), (f_f, F32)],
                 KB=8, TLOC=TLOC, emit=em_x2, out_specs=[(x2, "sync")],
                 deps=f7)
    return dict(x2=x2)


def host_prep_l1b(inputs, TLOC=TL1):
    d = {}
    d["rampT"] = np.tile(np.arange(TLOC, dtype=np.float32), (128, 1))
    for nm, key in [("tdec", "L1_attn_time_decay"),
                    ("tfir", "L1_attn_time_first")]:
        d[nm] = np.ascontiguousarray(
            np.asarray(inputs[key]).reshape(8, 128).T)
    d["w_o"] = wblk(np.asarray(inputs["L1_attn_Wo"])).astype(BF)
    d["w_1"] = wblk(np.asarray(inputs["L1_ffn_w1_w"]).T).astype(BF)
    d["w_2"] = wblk(np.asarray(inputs["L1_ffn_w2_w"]).T).astype(BF)
    d["w_h1"] = wblk(np.asarray(inputs["L1_ffn_h1_w"]).T).astype(BF)
    d["w_h2"] = wblk(np.asarray(inputs["L1_ffn_h2_w"]).T).astype(BF)
    d["w_3"] = wblk(np.asarray(inputs["L1_ffn_w3_w"]).T).astype(BF)
    for nm, key in [("s_1", "L1_ffn_w1_s"), ("s_2", "L1_ffn_w2_s"),
                    ("s_h1", "L1_ffn_h1_s"), ("s_h2", "L1_ffn_h2_s"),
                    ("s_3", "L1_ffn_w3_s")]:
        d[nm] = np.ascontiguousarray(
            np.asarray(inputs[key]).reshape(-1, 128).T)
    return d



def bnd_norm3(P, *, tag, src_fn, kind, psb, ones_col_bf,
              ones1_f32, deps, ncols=3, eps=1e-5):
    bcol = P.sbuf(f"{tag}_c", [128, 8 * ncols], F32)
    bbf = P.sbuf(f"{tag}_b", [128, 8 * ncols], BF16)
    bsq = P.sbuf(f"{tag}_q", [128, 8 * ncols], BF16)
    brow = P.sbuf(f"{tag}_r", [1, 8], F32)
    bout = P.sbuf(f"{tag}_o", [128, 8 * ncols], F32)
    tld = P.dma("sync", lambda eng: src_fn(eng, bcol), deps=deps,
                tag=f"{tag}_ld")
    tbf = P.step("vector", lambda v: v.tensor_copy(bbf[:, :], bcol[:, :]),
                 deps=[tld], tag=f"{tag}_bf")
    tsq = P.step("scalar", lambda sc: sc.activation(
        out=bsq[:, :], in_=bcol[:, :], func=AF.Square), deps=[tld],
        tag=f"{tag}_sq")
    fin = []
    for j in range(ncols):
        bsel = bbf[:, j::ncols]
        qsel = bsq[:, j::ncols]
        tmm = P.step("tensor", lambda t, bsel=bsel, qsel=qsel: [
            t.matmul(psb[0][0:1, 0:8], ones_col_bf[:, 0:1], bsel,
                     start=True, stop=True),
            t.matmul(psb[1][0:1, 0:8], ones_col_bf[:, 0:1], qsel,
                     start=True, stop=True)][-1],
            deps=[tbf, tsq] + fin, tag=f"{tag}_mm{j}")
        trd = P.chain("vector", [
            lambda v: v.reduce_sum(brow[0:1, 0:1], psb[0][0:1, 0:8],
                                   axis=AXX),
            lambda v: v.reduce_sum(brow[0:1, 1:2], psb[1][0:1, 0:8],
                                   axis=AXX),
        ], deps=[tmm], tag=f"{tag}_rd{j}")
        if kind == "plain":
            tmu = P.chain("scalar", [
                lambda sc: sc.activation(out=brow[0:1, 2:3],
                                         in_=brow[0:1, 0:1],
                                         func=AF.Identity,
                                         scale=P.cval(-1.0 / 1024)[0:1, :]),
                lambda sc: sc.activation(out=brow[0:1, 3:4],
                                         in_=brow[0:1, 1:2],
                                         func=AF.Identity,
                                         scale=P.cval(1.0 / 1024)[0:1, :]),
                lambda sc: sc.activation(out=brow[0:1, 4:5],
                                         in_=brow[0:1, 2:3],
                                         func=AF.Square),
            ], deps=[trd], tag=f"{tag}_mu{j}")
            tvar = P.step("vector", lambda v: v.tensor_sub(
                brow[0:1, 3:4], brow[0:1, 3:4], brow[0:1, 4:5]),
                deps=[tmu], tag=f"{tag}_v{j}")
            tsd = P.step("scalar", lambda sc: sc.activation(
                out=brow[0:1, 5:6], in_=brow[0:1, 3:4], func=AF.Sqrt,
                bias=P.cval(float(eps))[0:1, :]), deps=[tvar],
                tag=f"{tag}_sd{j}")
            trc = P.step("vector", lambda v: v.reciprocal(
                brow[0:1, 6:7], brow[0:1, 5:6]), deps=[tsd],
                tag=f"{tag}_rc{j}")
            tbc = P.step("tensor", lambda t: [
                t.matmul(psb[2][:, 0:1], ones1_f32[0:1, :], brow[0:1, 2:3],
                         start=True, stop=True),
                t.matmul(psb[3][:, 0:1], ones1_f32[0:1, :], brow[0:1, 6:7],
                         start=True, stop=True)][-1], deps=[trc, tmu],
                tag=f"{tag}_bc{j}")
            for k in range(8):
                ci = k * ncols + j
                tn = P.chain("vector", [
                    lambda v, ci=ci: v.tensor_tensor(
                        out=bout[:, ci:ci + 1], in0=bcol[:, ci:ci + 1],
                        in1=psb[2][:, 0:1], op=ALU.add),
                    lambda v, ci=ci: v.tensor_tensor(
                        out=bout[:, ci:ci + 1], in0=bout[:, ci:ci + 1],
                        in1=psb[3][:, 0:1], op=ALU.mult),
                ], deps=[tbc, tld], tag=f"{tag}_n{j}_{k}")
                fin.append(tn)
        else:
            tsd = P.chain("scalar", [
                lambda sc: sc.activation(out=brow[0:1, 2:3],
                                         in_=brow[0:1, 1:2], func=AF.Sqrt),
            ], deps=[trd], tag=f"{tag}_sd{j}")
            trc = P.chain("vector", [
                lambda v: v.tensor_scalar_max(brow[0:1, 3:4],
                                              brow[0:1, 2:3], 1e-12),
                lambda v: v.reciprocal(brow[0:1, 4:5], brow[0:1, 3:4]),
                lambda v: v.tensor_scalar_mul(brow[0:1, 5:6],
                                              brow[0:1, 4:5],
                                              float(math.sqrt(1024))),
            ], deps=[tsd], tag=f"{tag}_rc{j}")
            tbc = P.step("tensor", lambda t: t.matmul(
                psb[3][:, 0:1], ones1_f32[0:1, :], brow[0:1, 5:6],
                start=True, stop=True), deps=[trc], tag=f"{tag}_bc{j}")
            for k in range(8):
                ci = k * ncols + j
                tn = P.step("vector", lambda v, ci=ci: v.tensor_tensor(
                    out=bout[:, ci:ci + 1], in0=bcol[:, ci:ci + 1],
                    in1=psb[3][:, 0:1], op=ALU.mult), deps=[tbc, tld],
                    tag=f"{tag}_n{j}_{k}")
                fin.append(tn)
    return bout, P.red(fin)


def mamba_A(P, *, tag, xm_ext, TLOC, w_ina, w_inb, w_x, w_dt_h, cwc, cbc,
            bdtc, alog_h, psf, onesb, ones1f, outs, deps):
    TPX = min(2048, TLOC)
    xin_ext = P.dram(f"{tag}_xin", [16, 128, TLOC + 3], F32)
    xmr_bf = P.dram(f"{tag}_xmr", [8, 128, TLOC], BF16)
    with P.scope():
        lb_, _, _ = ln_phase(P, tag=f"{tag}_rms", x_hbm=xm_ext[:, :, 3:],
                             KB=8, TLOC=TLOC, out_bf=xmr_bf, out_f32=None,
                             kind="rms", psb=psf[0:4], ones_col_bf=onesb,
                             ones1_f32=ones1f, deps=deps)
    f1 = P.fence()
    with P.scope():
        def srcf(eng, bcol):
            return [eng.dma_start(out=bcol[:, k * 3:k * 3 + 3],
                                  in_=xm_ext[k, :, 0:3]) for k in range(8)]
        bout, tb = bnd_norm3(P, tag=f"{tag}_bn", src_fn=srcf, kind="rms",
                             psb=psf[0:4], ones_col_bf=onesb,
                             ones1_f32=ones1f, deps=f1)
        bbf2 = P.sbuf(f"{tag}_bb2", [128, 24], BF16)
        whal = P.sbuf(f"{tag}_wh", [128, 8 * 128], BF16)
        hout = P.sbuf(f"{tag}_ho", [128, 3], F32)
        tbf2 = P.step("vector", lambda v: v.tensor_copy(bbf2[:, :],
                                                        bout[:, :]),
                      deps=tb, tag=f"{tag}_bc2")
        hsts = []
        prevw = None
        for m in range(16):
            twl = P.dma("gpsimd", lambda eng, m=m: [eng.dma_start(
                out=whal[:, :].rearrange("p (k x) -> p k x", k=8),
                in_=w_ina[m].rearrange("k p x -> p k x"))],
                deps=f1 + ([prevw] if prevw else []), tag=f"{tag}_whl{m}")

            def mmh(t, m=m):
                last = None
                for k in range(8):
                    last = t.matmul(psf[4][0:128, 0:3],
                                    whal[:, k * 128:(k + 1) * 128],
                                    bbf2[:, k * 3:k * 3 + 3],
                                    start=(k == 0), stop=(k == 7))
                return last
            tmm = P.step("tensor", mmh, deps=[twl, tbf2] + hsts[-1:],
                         tag=f"{tag}_hmm{m}")
            tev = P.step("vector", lambda v: v.tensor_copy(
                hout[:, :], psf[4][0:128, 0:3]), deps=[tmm],
                tag=f"{tag}_hev{m}")
            tst = P.dma("sync", lambda eng, m=m: [eng.dma_start(
                out=xin_ext[m, :, 0:3], in_=hout[:, :])], deps=[tev],
                tag=f"{tag}_hst{m}")
            hsts.append(tst)
            prevw = tmm
    f2 = P.fence()
    with P.scope():
        stream_matmul(P, tag=f"{tag}_wia", x_hbm=xmr_bf, w_hbm=w_ina,
                      out_hbm=xin_ext[:, :, 3:], KB=8, M=2048, TLOC=TLOC,
                      TP=TPX, psb=psf[0:6], out_dt=F32, deps=f2)
    f3 = P.fence()
    with P.scope():
        stream_matmul(P, tag=f"{tag}_wib", x_hbm=xmr_bf, w_hbm=w_inb,
                      out_hbm=outs["res_f"], KB=8, M=2048, TLOC=TLOC,
                      TP=TPX, psb=psf[0:6], out_dt=F32, deps=f3)
    f4 = P.fence()
    with P.scope():
        CH = 512
        NT = TLOC // CH
        NM = 16
        xt = [P.sbuf(f"{tag}_cx{i}", [128, 4 * (CH + 3)], F32)
              for i in range(2)]
        ut = [P.sbuf(f"{tag}_cu{i}", [128, 4 * CH], F32) for i in range(2)]
        ac = P.sbuf(f"{tag}_ca", [128, 4 * CH], F32)
        gate = [list(f4), list(f4)]
        for blk in range(NT * 4):
            t = blk % NT
            mg = blk // NT
            bi = blk % 2
            t0 = t * CH
            tl = P.dma("sync", lambda eng, mg=mg, t0=t0, bi=bi: [
                eng.dma_start(out=xt[bi][:, i * (CH + 3):(i + 1) * (CH + 3)],
                              in_=xin_ext[mg * 4 + i, :, t0:t0 + CH + 3])
                for i in range(4)], deps=gate[bi], tag=f"{tag}_cl{blk}")
            ks = []
            for i in range(4):
                m = mg * 4 + i
                xv = xt[bi][:, i * (CH + 3):(i + 1) * (CH + 3)]
                av = ac[:, i * CH:(i + 1) * CH]
                uv = ut[bi][:, i * CH:(i + 1) * CH]
                tc = P.chain("vector", [
                    lambda v, xv=xv, av=av, m=m: v.tensor_scalar_mul(
                        av, xv[:, 0:CH], cwc[:, m * 4:m * 4 + 1]),
                    lambda v, xv=xv, av=av, m=m: v.scalar_tensor_tensor(
                        out=av, in0=xv[:, 1:CH + 1],
                        scalar=cwc[:, m * 4 + 1:m * 4 + 2], in1=av,
                        op0=ALU.mult, op1=ALU.add),
                    lambda v, xv=xv, av=av, m=m: v.scalar_tensor_tensor(
                        out=av, in0=xv[:, 2:CH + 2],
                        scalar=cwc[:, m * 4 + 2:m * 4 + 3], in1=av,
                        op0=ALU.mult, op1=ALU.add),
                    lambda v, xv=xv, av=av, m=m: v.scalar_tensor_tensor(
                        out=av, in0=xv[:, 3:CH + 3],
                        scalar=cwc[:, m * 4 + 3:m * 4 + 4], in1=av,
                        op0=ALU.mult, op1=ALU.add),
                ], deps=[tl], tag=f"{tag}_cc{blk}_{i}")
                tu = P.step("scalar", lambda sc, av=av, uv=uv, m=m:
                            sc.activation(out=uv, in_=av, func=AF.Silu,
                                          bias=cbc[:, m:m + 1]),
                            deps=[tc], tag=f"{tag}_cs{blk}_{i}")
                ks.append(tu)
            st1 = P.dma("sync", lambda eng, mg=mg, t0=t0, bi=bi: [
                eng.dma_start(out=outs["u_f"][mg * 4 + i, :, t0:t0 + CH],
                              in_=ut[bi][:, i * CH:(i + 1) * CH])
                for i in range(4)], deps=ks, tag=f"{tag}_cs1{blk}")
            st2 = P.dma("gpsimd", lambda eng, mg=mg, t0=t0, bi=bi: [
                eng.dma_start(out=outs["u_bf"][mg * 4 + i, :, t0:t0 + CH],
                              in_=ut[bi][:, i * CH:(i + 1) * CH])
                for i in range(4)], deps=ks, tag=f"{tag}_cs2{blk}")
            gate[bi] = ks + [st1, st2]
    f5 = P.fence()
    with P.scope():
        stream_matmul(P, tag=f"{tag}_wx", x_hbm=outs["u_bf"], w_hbm=w_x,
                      out_hbm=outs["dblf"], KB=16, M=96, TLOC=TLOC, TP=TPX,
                      psb=psf[0:6], out_dt=F32, deps=f5)
    f6 = P.fence()
    with P.scope():
        wdt_sb = P.sbuf(f"{tag}_wdt", [64, 16 * 128], BF16)
        dt_sb = [P.sbuf(f"{tag}_dt{i}", [64, 512], BF16) for i in range(2)]
        ev1 = [P.sbuf(f"{tag}_de1{i}", [128, 512], F32) for i in range(2)]
        ev2 = [P.sbuf(f"{tag}_de2{i}", [128, 512], F32) for i in range(2)]
        twd = P.dma("gpsimd", lambda eng: [eng.dma_start(
            out=wdt_sb[:, :].rearrange("p (m x) -> p m x", m=16),
            in_=w_dt_h.rearrange("m p x -> p m x"))], deps=f6,
            tag=f"{tag}_wdl")
        gate = [list(f6), list(f6)]
        evg = [list(f6), list(f6)]
        for t in range(TLOC // 512):
            bi = t % 2
            t0 = t * 512
            tl = P.dma("gpsimd", lambda eng, t0=t0, bi=bi: [
                eng.dma_start(out=dt_sb[bi][:, :],
                              in_=outs["dblf"][0, 0:64, t0:t0 + 512])],
                deps=gate[bi], tag=f"{tag}_dl{t}")
            for m in range(16):
                tmm = P.step("tensor", lambda t_, m=m, bi=bi: t_.matmul(
                    psf[m % 2][0:128, :], wdt_sb[:, m * 128:(m + 1) * 128],
                    dt_sb[bi][:, :], start=True, stop=True),
                    deps=[twd, tl] + evg[m % 2], tag=f"{tag}_dmm{t}_{m}")
                te1 = P.step("scalar", lambda sc, m=m: sc.activation(
                    out=ev1[m % 2][:, :], in_=psf[m % 2][0:128, :],
                    func=AF.Exp, bias=bdtc[:, m:m + 1]),
                    deps=[tmm] + evg[m % 2], tag=f"{tag}_de1{t}_{m}")
                te2 = P.step("scalar", lambda sc, m=m: sc.activation(
                    out=ev2[m % 2][:, :], in_=ev1[m % 2][:, :], func=AF.Ln,
                    bias=1.0), deps=[te1], tag=f"{tag}_de2{t}_{m}")
                tst = P.dma("sync", lambda eng, m=m, t0=t0: [
                    eng.dma_start(out=outs["delta_f"][m, :, t0:t0 + 512],
                                  in_=ev2[m % 2][:, :])], deps=[te2],
                    tag=f"{tag}_dst{t}_{m}")
                evg[m % 2] = [te2, tst]
            gate[bi] = [tst]
    f7 = P.fence()
    def em_mul(P_, k, bb, ob, dd):
        return [P_.step("vector", lambda v: v.tensor_mul(ob, bb[0], bb[1]),
                        deps=dd, tag=f"{tag}_wm{k}")]
    for g2 in range(2):
        with P.scope():
            ew_phase(P, tag=f"{tag}_w{g2}",
                     srcs=[(outs["delta_f"][8 * g2:8 * g2 + 8], F32),
                           (outs["u_f"][8 * g2:8 * g2 + 8], F32)],
                     KB=8, TLOC=TLOC, emit=em_mul,
                     out_specs=[(outs["w_f"][8 * g2:8 * g2 + 8], "gpsimd")],
                     deps=P.fence())
    f8 = P.fence()
    with P.scope():
        bc_sb = P.sbuf(f"{tag}_bc", [128, TLOC], BF16)
        prev = f8
        for n in range(16):
            for src_row, dst in [(64 + n, outs["bb_b"])]:
                t1 = P.dma("gpsimd", lambda eng, src_row=src_row: [
                    eng.dma_start(out=bc_sb[:, :],
                                  in_=outs["dblf"][0, src_row:src_row + 1,
                                                   :].partition_broadcast(128))],
                    deps=prev, tag=f"{tag}_bb{n}")
                t2 = P.dma("sync", lambda eng, n=n, dst=dst: [
                    eng.dma_start(out=dst[n], in_=bc_sb[:, :])], deps=[t1],
                    tag=f"{tag}_bb2{n}")
                prev = [t2]
    f9 = P.fence()
    with P.scope():
        dt_t = [P.sbuf(f"{tag}_sd{i}", [128, TLOC], F32) for i in range(2)]
        w_t = [P.sbuf(f"{tag}_sw{i}", [128, TLOC], F32) for i in range(2)]
        a_t = [P.sbuf(f"{tag}_sa{i}", [128, TLOC], F32) for i in range(2)]
        bb_t = [P.sbuf(f"{tag}_sb{i}", [128, TLOC], BF16) for i in range(2)]
        b_t = [P.sbuf(f"{tag}_sbb{i}", [128, TLOC], BF16) for i in range(2)]
        h_t = [P.sbuf(f"{tag}_sh{i}", [128, TLOC], BF16) for i in range(2)]
        alog_t = P.sbuf(f"{tag}_al", [128, 16], F32)
        an_t = P.sbuf(f"{tag}_an", [128, 16], F32)
        cacc = P.sbuf(f"{tag}_cacc", [128, 8], F32)
        hend_t = P.sbuf(f"{tag}_he", [128, 16], BF16)
        hl3_t = P.sbuf(f"{tag}_h3", [128, 48], BF16)
        zcol = P.sbuf(f"{tag}_z", [128, 1], F32)
        tz = P.step("vector", lambda v: v.memset(zcol[:, :], 0.0), deps=f9,
                    tag=f"{tag}_z0")
        mg = list(f9) + [tz]
        for m in range(16):
            bi = m % 2
            tl = P.dma("sync", lambda eng, m=m, bi=bi: [
                eng.dma_start(out=dt_t[bi][:, :], in_=outs["delta_f"][m]),
                eng.dma_start(out=w_t[bi][:, :], in_=outs["w_f"][m]),
            ], deps=mg, tag=f"{tag}_sl{m}")
            tal = P.dma("gpsimd", lambda eng, m=m: [
                eng.dma_start(out=alog_t[:, :], in_=alog_h[m])], deps=mg,
                tag=f"{tag}_sal{m}")
            tan = P.chain("scalar", [
                lambda sc: sc.activation(out=an_t[:, :], in_=alog_t[:, :],
                                         func=AF.Exp),
                lambda sc: sc.mul(out=an_t[:, :], in_=an_t[:, :], mul=-1.0),
            ], deps=[tal], tag=f"{tag}_san{m}")
            tca = P.chain("vector", [
                lambda v, bi=bi: v.reduce_sum(cacc[:, 0:1], dt_t[bi][:, :],
                                              axis=AXX),
                lambda v, bi=bi: v.tensor_sub(
                    cacc[:, 1:2], cacc[:, 0:1],
                    dt_t[bi][:, TLOC - 1:TLOC]),
                lambda v, bi=bi: v.tensor_sub(
                    cacc[:, 2:3], cacc[:, 1:2],
                    dt_t[bi][:, TLOC - 2:TLOC - 1]),
            ], deps=[tl], tag=f"{tag}_sca{m}")
            tcst = P.dma("sync", lambda eng, m=m: [
                eng.dma_start(out=outs["cae"][m, :, 0:1], in_=cacc[:, 0:1],
                              allow_slow_non_contiguous=True),
                eng.dma_start(out=outs["cae"][m, :, 1:2], in_=cacc[:, 2:3],
                              allow_slow_non_contiguous=True),
                eng.dma_start(out=outs["cae"][m, :, 2:3], in_=cacc[:, 1:2],
                              allow_slow_non_contiguous=True),
                eng.dma_start(out=outs["cae"][m, :, 3:4], in_=cacc[:, 0:1],
                              allow_slow_non_contiguous=True),
            ], deps=[tca], tag=f"{tag}_scst{m}")
            ga = [[tan, tl], [tan, tl]]
            gb_ = [[tan, tl], [tan, tl]]
            gh = [[tan, tl], [tan, tl]]
            gbb = [[tan], [tan]]
            lastcp = tca
            ng = [tan, tl, tca]
            for n in range(16):
                p = n % 2
                tbl = P.dma("gpsimd", lambda eng, n=n, p=p: [
                    eng.dma_start(out=bb_t[p][:, :],
                                  in_=outs["bb_b"][n])], deps=gbb[p] + [tl],
                    tag=f"{tag}_sbl{m}_{n}")
                ta = P.step("scalar", lambda sc, n=n, bi=bi, p=p:
                            sc.activation(
                    out=a_t[p][:, :], in_=dt_t[bi][:, :], func=AF.Exp,
                    scale=an_t[:, n:n + 1]), deps=ga[p] + [tan],
                    tag=f"{tag}_sa{m}_{n}")
                tb = P.step("vector", lambda v, bi=bi, p=p: v.tensor_mul(
                    b_t[p][:, :], w_t[bi][:, :], bb_t[p][:, :]),
                    deps=[tbl] + gb_[p], tag=f"{tag}_sb{m}_{n}")
                th = P.step("vector", lambda v, p=p: v.tensor_tensor_scan(
                    out=h_t[p][:, :], data0=a_t[p][:, :], data1=b_t[p][:, :],
                    initial=zcol[:, 0:1], op0=ALU.mult, op1=ALU.add),
                    deps=[ta, tb] + gh[p] + [tz], tag=f"{tag}_ss{m}_{n}")
                tcp = P.chain("vector", [
                    lambda v, n=n, p=p: v.tensor_copy(hend_t[:, n:n + 1],
                                                      h_t[p][:, TLOC - 1:TLOC]),
                    lambda v, n=n, p=p: v.tensor_copy(
                        hl3_t[:, 3 * n:3 * n + 3], h_t[p][:, TLOC - 3:TLOC]),
                ], deps=[th, lastcp], tag=f"{tag}_scp{m}_{n}")
                lastcp = tcp
                ga[p] = [th]
                gb_[p] = [th]
                gh[p] = [tcp]
                gbb[p] = [tb]
                ng = [tcp, th, ta, tb]
            tst = P.dma("sync", lambda eng, m=m: [
                eng.dma_start(out=outs["hend"][m], in_=hend_t[:, :]),
                eng.dma_start(out=outs["hl3"][m], in_=hl3_t[:, :]),
            ], deps=ng, tag=f"{tag}_sst{m}")
            mg = P.red(ng + [tst, tcst] + ga[0] + ga[1] + gh[0] + gh[1])
    return P.fence()


def mamba_B(P, *, tag, TLOC, ins, carr4, alog_h, dpc, psf, onesb,
            ones1f, y_bf, deps):
    rjm = ins["rjm_sb"]
    csel = ins["caesel_sb"]
    with P.scope():
        dt_t = [P.sbuf(f"{tag}_d{i}", [128, TLOC], F32) for i in range(2)]
        w_t = [P.sbuf(f"{tag}_w{i}", [128, TLOC], BF16) for i in range(2)]
        u_t = P.sbuf(f"{tag}_u", [128, TLOC], F32)
        res_t = P.sbuf(f"{tag}_r", [128, TLOC], F32)
        a_t = [P.sbuf(f"{tag}_a{i}", [128, TLOC], F32) for i in range(2)]
        bb_t = [P.sbuf(f"{tag}_bb{i}", [128, TLOC], BF16) for i in range(2)]
        cb_t = [P.sbuf(f"{tag}_cb{i}", [128, TLOC], BF16) for i in range(2)]
        b_t = P.sbuf(f"{tag}_b", [128, TLOC], BF16)
        h_t = [P.sbuf(f"{tag}_h{i}", [128, TLOC], BF16) for i in range(2)]
        tmp_t = P.sbuf(f"{tag}_t", [128, TLOC], BF16)
        y_t = P.sbuf(f"{tag}_y", [128, TLOC], F32)
        alog_t = P.sbuf(f"{tag}_al", [128, 16], F32)
        an_t = P.sbuf(f"{tag}_an", [128, 16], F32)
        he_t = P.sbuf(f"{tag}_he", [128, 4 * 16], BF16)
        cae_t = P.sbuf(f"{tag}_ce", [128, 4 * 4], F32)
        hi_t = P.sbuf(f"{tag}_hi", [128, 16], F32)
        wc = P.sbuf(f"{tag}_wc", [128, 4], F32)
        wc2_t = P.sbuf(f"{tag}_wc2", [128, 16], F32)
        mg = list(deps)
        for m in range(16):
            bi = m % 2
            tl = P.dma("sync", lambda eng, m=m, bi=bi: [
                eng.dma_start(out=dt_t[bi][:, :], in_=ins["delta_f"][m]),
                eng.dma_start(out=u_t[:, :], in_=ins["u_f"][m]),
                eng.dma_start(out=res_t[:, :], in_=ins["res_f"][m]),
            ], deps=mg, tag=f"{tag}_l{m}")
            tlw = P.dma("gpsimd", lambda eng, m=m, bi=bi: [
                eng.dma_start(out=w_t[bi][:, :], in_=ins["w_f"][m]),
            ], deps=mg, tag=f"{tag}_lw{m}")
            tl2 = P.dma("gpsimd", lambda eng, m=m: [
                eng.dma_start(out=alog_t[:, :], in_=alog_h[m])] + [
                eng.dma_start(out=he_t[:, 16 * j:16 * j + 16],
                              in_=carr4["hend4"][j, m]) for j in range(4)] + [
                eng.dma_start(out=cae_t[:, 4 * j:4 * j + 4],
                              in_=carr4["cae4"][j, m]) for j in range(4)],
                deps=mg, tag=f"{tag}_l2{m}")
            tan = P.chain("scalar", [
                lambda sc: sc.activation(out=an_t[:, :], in_=alog_t[:, :],
                                         func=AF.Exp),
                lambda sc: sc.mul(out=an_t[:, :], in_=an_t[:, :], mul=-1.0),
            ], deps=[tl2], tag=f"{tag}_an{m}")
            tyz = P.step("vector", lambda v: v.memset(y_t[:, :], 0.0),
                         deps=mg, tag=f"{tag}_yz{m}")
            thz = P.step("vector", lambda v: v.memset(hi_t[:, :], 0.0),
                         deps=mg, tag=f"{tag}_hz{m}")
            tacc = thz
            for j in range(3):
                sub = [lambda v: v.memset(wc[:, 0:1], 0.0)]
                for l in range(4):
                    sub.append(lambda v, l=l, j=j: v.tensor_mul(
                        wc[:, 1:2], cae_t[:, 4 * l:4 * l + 1],
                        csel[:, j * 4 + l:j * 4 + l + 1]))
                    sub.append(lambda v: v.tensor_add(
                        wc[:, 0:1], wc[:, 0:1], wc[:, 1:2]))
                tw = P.chain("vector", sub, deps=[tacc, tl2],
                             tag=f"{tag}_hw{m}_{j}")
                te = P.step("scalar", lambda sc: sc.activation(
                    out=wc2_t[:, :], in_=an_t[:, :], func=AF.Exp,
                    scale=wc[:, 0:1]), deps=[tw, tan], tag=f"{tag}_he{m}_{j}")
                tm2 = P.chain("vector", [
                    lambda v, j=j: v.tensor_scalar_mul(
                        wc2_t[:, :], wc2_t[:, :], rjm[:, j:j + 1]),
                    lambda v, j=j: v.tensor_mul(
                        wc2_t[:, :], wc2_t[:, :],
                        he_t[:, 16 * j:16 * j + 16]),
                    lambda v: v.tensor_add(hi_t[:, :], hi_t[:, :],
                                           wc2_t[:, :]),
                ], deps=[te], tag=f"{tag}_hm{m}_{j}")
                tacc = tm2
            thi = tacc
            ga = [[thi, tl, tan], [thi, tl, tan]]
            gh = [[thi], [thi]]
            gt = [[thi], [thi]]
            gbc = [[thi], [thi]]
            lastya = tyz
            lastth = [thi, tl, tlw]
            ng = [tan, tl, tlw, tyz, tl2, thi]
            for n in range(16):
                p = n % 2
                tbl = P.dma("gpsimd", lambda eng, n=n, p=p: [
                    eng.dma_start(out=bb_t[p][:, :], in_=ins["bb_b"][n]),
                    eng.dma_start(out=cb_t[p][:, :], in_=ins["cb_b"][n]),
                ], deps=gbc[p] + [tl], tag=f"{tag}_bl{m}_{n}")
                ta = P.step("scalar", lambda sc, n=n, bi=bi, p=p:
                            sc.activation(
                    out=a_t[p][:, :], in_=dt_t[bi][:, :], func=AF.Exp,
                    scale=an_t[:, n:n + 1]), deps=ga[p] + [tan],
                    tag=f"{tag}_a{m}_{n}")
                tb = P.step("vector", lambda v, bi=bi: v.tensor_mul(
                    b_t[:, :], w_t[bi][:, :], bb_t[p][:, :]),
                    deps=[tbl] + lastth, tag=f"{tag}_b{m}_{n}")
                th = P.step("vector", lambda v, n=n, p=p:
                            v.tensor_tensor_scan(
                    out=h_t[p][:, :], data0=a_t[p][:, :], data1=b_t[:, :],
                    initial=hi_t[:, n:n + 1], op0=ALU.mult, op1=ALU.add),
                    deps=[ta, tb, thi] + gh[p], tag=f"{tag}_s{m}_{n}")
                tmu = P.step("gpsimd", lambda g, p=p: g.tensor_mul(
                    tmp_t[:, :], h_t[p][:, :], cb_t[p][:, :]),
                    deps=[th, tbl] + gt[p] + [lastya],
                    tag=f"{tag}_ym{m}_{n}")
                tya = P.step("gpsimd", lambda g: g.tensor_add(
                    y_t[:, :], y_t[:, :], tmp_t[:, :]),
                    deps=[tmu, lastya], tag=f"{tag}_ya{m}_{n}")
                lastya = tya
                ga[p] = [th]
                gh[p] = [tmu]
                gt[p] = []
                gbc[p] = [tb, tmu]
                lastth = [th]
                ng = [tya, th, ta, tb, tmu]
            tud = P.step("vector", lambda v, m=m: v.scalar_tensor_tensor(
                out=y_t[:, :], in0=u_t[:, :], scalar=dpc[:, m:m + 1],
                in1=y_t[:, :], op0=ALU.mult, op1=ALU.add), deps=ng + [tl],
                tag=f"{tag}_ud{m}")
            tsr = P.step("scalar", lambda sc: sc.activation(
                out=res_t[:, :], in_=res_t[:, :], func=AF.Silu),
                deps=[tl] + mg, tag=f"{tag}_sr{m}")
            tyf = P.step("vector", lambda v: v.tensor_mul(
                y_t[:, :], y_t[:, :], res_t[:, :]), deps=[tud, tsr],
                tag=f"{tag}_yf{m}")
            tst = P.dma("gpsimd", lambda eng, m=m: [
                eng.dma_start(out=y_bf[m], in_=y_t[:, :])], deps=[tyf],
                tag=f"{tag}_st{m}")
            mg = P.red(ng + [tst, tyf, tud, tsr])
    return P.fence()


def bcast_bc(P, *, tag, dblf, dsts, TLOC, deps):
    bc_sb = P.sbuf(f"{tag}_sb", [128, TLOC], BF16)
    prev = list(deps)
    for which, dst in dsts:
        base = 64 if which == "B" else 80
        for n in range(16):
            t1 = P.dma("gpsimd", lambda eng, n=n, base=base: [
                eng.dma_start(out=bc_sb[:, :],
                              in_=dblf[0, base + n:base + n + 1,
                                       :].partition_broadcast(128))],
                deps=prev, tag=f"{tag}_{which}{n}")
            t2 = P.dma("sync", lambda eng, n=n, dst=dst: [
                eng.dma_start(out=dst[n], in_=bc_sb[:, :])], deps=[t1],
                tag=f"{tag}2_{which}{n}")
            prev = [t2]
    return P.fence()


def mamba_halo(P, *, tag, TLOC, hl3_nb, cae_nb, hend4, cae4, rjm_nb,
               csel_nb, Cl3, unl3, resnl3, alog_h, dpc, bflag, w_out,
               xm_ext_next, onesb, ones1f, psf, deps):
    with P.scope():
        al_t = P.sbuf(f"{tag}_al", [128, 16], F32)
        an_t = P.sbuf(f"{tag}_an", [128, 16], F32)
        he_t = P.sbuf(f"{tag}_he", [128, 64], BF16)
        cae_t = P.sbuf(f"{tag}_ce", [128, 16], F32)
        cnb_t = P.sbuf(f"{tag}_cn", [128, 4], F32)
        h3_t = P.sbuf(f"{tag}_h3", [128, 48], BF16)
        hi_t = P.sbuf(f"{tag}_hi", [128, 16], F32)
        wc = P.sbuf(f"{tag}_wc", [128, 4], F32)
        wc2 = P.sbuf(f"{tag}_w2", [128, 16], F32)
        P3 = P.sbuf(f"{tag}_p3", [128, 3], F32)
        hh = P.sbuf(f"{tag}_hh", [128, 3], F32)
        y3 = P.sbuf(f"{tag}_y3", [128, 3], F32)
        u3 = P.sbuf(f"{tag}_u3", [128, 3], F32)
        r3 = P.sbuf(f"{tag}_r3", [128, 3], F32)
        crow = P.sbuf(f"{tag}_cr", [1, 48], F32)
        cb48 = P.sbuf(f"{tag}_cb", [128, 48], F32)
        yall = P.sbuf(f"{tag}_ya", [128, 48], BF16)
        wo_sb = P.sbuf(f"{tag}_wo", [128, 16 * 128], BF16)
        xo = P.sbuf(f"{tag}_xo", [128, 3], F32)
        bfl = P.sbuf(f"{tag}_bf", [128, 1], F32)
        tc = P.dma("sync", lambda eng: [
            eng.dma_start(out=crow[:, :], in_=Cl3),
            eng.dma_start(out=bfl[:, :], in_=bflag),
        ], deps=deps, tag=f"{tag}_c")
        tcb = P.step("tensor", lambda t: t.matmul(
            psf[5][:, 0:48], ones1f[0:1, :], crow[0:1, :], start=True,
            stop=True), deps=[tc], tag=f"{tag}_cbm")
        tcb2 = P.step("vector", lambda v: v.tensor_copy(cb48[:, :],
                                                        psf[5][:, 0:48]),
                      deps=[tcb], tag=f"{tag}_cbe")
        mg = list(deps) + [tcb2]
        for m in range(16):
            tl = P.dma("gpsimd", lambda eng, m=m: [
                eng.dma_start(out=al_t[:, :], in_=alog_h[m]),
                eng.dma_start(out=h3_t[:, :], in_=hl3_nb[m]),
                eng.dma_start(out=cnb_t[:, :], in_=cae_nb[m]),
                eng.dma_start(out=u3[:, :], in_=unl3[m]),
                eng.dma_start(out=r3[:, :], in_=resnl3[m]),
            ] + [eng.dma_start(out=he_t[:, 16 * j:16 * j + 16],
                               in_=hend4[j, m]) for j in range(4)]
              + [eng.dma_start(out=cae_t[:, 4 * j:4 * j + 4],
                               in_=cae4[j, m]) for j in range(4)],
                deps=mg, tag=f"{tag}_l{m}")
            tan = P.chain("scalar", [
                lambda sc: sc.activation(out=an_t[:, :], in_=al_t[:, :],
                                         func=AF.Exp),
                lambda sc: sc.mul(out=an_t[:, :], in_=an_t[:, :], mul=-1.0),
            ], deps=[tl], tag=f"{tag}_an{m}")
            thz = P.step("vector", lambda v: v.memset(hi_t[:, :], 0.0),
                         deps=mg, tag=f"{tag}_hz{m}")
            tacc = thz
            for j in range(3):
                sub = [lambda v: v.memset(wc[:, 0:1], 0.0)]
                for l in range(4):
                    sub.append(lambda v, l=l, j=j: v.tensor_mul(
                        wc[:, 1:2], cae_t[:, 4 * l:4 * l + 1],
                        csel_nb[:, j * 4 + l:j * 4 + l + 1]))
                    sub.append(lambda v: v.tensor_add(
                        wc[:, 0:1], wc[:, 0:1], wc[:, 1:2]))
                tw = P.chain("vector", sub, deps=[tacc, tl],
                             tag=f"{tag}_hw{m}_{j}")
                te = P.step("scalar", lambda sc: sc.activation(
                    out=wc2[:, :], in_=an_t[:, :], func=AF.Exp,
                    scale=wc[:, 0:1]), deps=[tw, tan], tag=f"{tag}_pe{m}_{j}")
                tm2 = P.chain("vector", [
                    lambda v, j=j: v.tensor_scalar_mul(
                        wc2[:, :], wc2[:, :], rjm_nb[:, j:j + 1]),
                    lambda v, j=j: v.tensor_mul(
                        wc2[:, :], wc2[:, :], he_t[:, 16 * j:16 * j + 16]),
                    lambda v: v.tensor_add(hi_t[:, :], hi_t[:, :],
                                           wc2[:, :]),
                ], deps=[te], tag=f"{tag}_hm{m}_{j}")
                tacc = tm2
            tyz = P.step("vector", lambda v: v.memset(y3[:, :], 0.0),
                         deps=mg, tag=f"{tag}_y0{m}")
            ng = [tacc, tyz, tan]
            for n in range(16):
                tp3 = P.step("scalar", lambda sc, n=n: sc.activation(
                    out=P3[:, :], in_=cnb_t[:, 1:4], func=AF.Exp,
                    scale=an_t[:, n:n + 1]), deps=ng, tag=f"{tag}_p{m}_{n}")
                thh = P.step("vector", lambda v, n=n: v.scalar_tensor_tensor(
                    out=hh[:, :], in0=P3[:, :], scalar=hi_t[:, n:n + 1],
                    in1=h3_t[:, 3 * n:3 * n + 3], op0=ALU.mult, op1=ALU.add),
                    deps=[tp3], tag=f"{tag}_hh{m}_{n}")
                tya = P.chain("vector", [
                    lambda v, n=n: v.tensor_mul(hh[:, :], hh[:, :],
                                                cb48[:, 3 * n:3 * n + 3]),
                    lambda v: v.tensor_add(y3[:, :], y3[:, :], hh[:, :]),
                ], deps=[thh], tag=f"{tag}_yy{m}_{n}")
                ng = [tya]
            tud = P.step("vector", lambda v, m=m: v.scalar_tensor_tensor(
                out=y3[:, :], in0=u3[:, :], scalar=dpc[:, m:m + 1],
                in1=y3[:, :], op0=ALU.mult, op1=ALU.add), deps=ng + [tl],
                tag=f"{tag}_ud{m}")
            tsr = P.step("scalar", lambda sc: sc.activation(
                out=r3[:, :], in_=r3[:, :], func=AF.Silu), deps=[tl],
                tag=f"{tag}_sr{m}")
            tyf = P.chain("vector", [
                lambda v: v.tensor_mul(y3[:, :], y3[:, :], r3[:, :]),
                lambda v, m=m: v.tensor_copy(yall[:, 3 * m:3 * m + 3],
                                             y3[:, :]),
            ], deps=[tud, tsr], tag=f"{tag}_yf{m}")
            mg = P.red(ng + [tyf, tl])
        prevw = None
        for mo in range(8):
            twl = P.dma("gpsimd", lambda eng, mo=mo: [eng.dma_start(
                out=wo_sb[:, :].rearrange("p (k x) -> p k x", k=16),
                in_=w_out[mo].rearrange("k p x -> p k x"))],
                deps=mg + ([prevw] if prevw else []), tag=f"{tag}_wl{mo}")

            def mmh(t, mo=mo):
                last = None
                for k in range(16):
                    last = t.matmul(psf[4][0:128, 0:3],
                                    wo_sb[:, k * 128:(k + 1) * 128],
                                    yall[:, k * 3:k * 3 + 3],
                                    start=(k == 0), stop=(k == 15))
                return last
            tmm = P.step("tensor", mmh, deps=[twl] + mg, tag=f"{tag}_mm{mo}")
            tev = P.step("scalar", lambda sc: sc.activation(
                out=xo[:, :], in_=psf[4][0:128, 0:3], func=AF.Identity,
                scale=bfl[:, 0:1]), deps=[tmm, tc], tag=f"{tag}_ev{mo}")
            tst = P.dma("sync", lambda eng, mo=mo: [eng.dma_start(
                out=xm_ext_next[mo, :, 0:3], in_=xo[:, :])], deps=[tev],
                tag=f"{tag}_st{mo}")
            prevw = tev
            mg = mg + [tst, tev]
    return P.fence()


def _mamba_weight_drams(P):
    d = {}
    d["w_ina"] = P.dram("w_ina", [16, 8, 128, 128], BF16, kind="ExternalInput")
    d["w_inb"] = P.dram("w_inb", [16, 8, 128, 128], BF16, kind="ExternalInput")
    d["w_x"] = P.dram("w_x", [128, 16 * 96], BF16, kind="ExternalInput")
    d["w_dt"] = P.dram("w_dt", [16, 64, 128], BF16, kind="ExternalInput")
    d["cwc"] = P.dram("cwc", [128, 64], F32, kind="ExternalInput")
    d["cbc"] = P.dram("cbc", [128, 16], F32, kind="ExternalInput")
    d["bdtc"] = P.dram("bdtc", [128, 16], F32, kind="ExternalInput")
    d["alog"] = P.dram("alog", [16, 128, 16], F32, kind="ExternalInput")
    d["dpc"] = P.dram("dpc", [128, 16], F32, kind="ExternalInput")
    d["w_out"] = P.dram("w_out", [8, 16, 128, 128], BF16,
                        kind="ExternalInput")
    return d


def _mamba_A_outs(P, TLOC):
    o = {}
    o["res_f"] = P.dram("res_f", [16, 128, TLOC], F32)
    o["u_f"] = P.dram("u_f", [16, 128, TLOC], F32)
    o["u_bf"] = P.dram("u_bf", [16, 128, TLOC], BF16)
    o["delta_f"] = P.dram("delta_f", [16, 128, TLOC], F32)
    o["w_f"] = P.dram("w_f", [16, 128, TLOC], F32)
    o["dblf"] = P.dram("dblf", [1, 96, TLOC], F32)
    o["bb_b"] = P.dram("bb_b", [16, 128, TLOC], BF16)
    o["hend"] = P.dram("hend", [16, 128, 16], BF16)
    o["hl3"] = P.dram("hl3", [16, 128, 48], BF16)
    o["cae"] = P.dram("cae", [16, 128, 4], F32)
    return o


def emit_m0(P, TLOC, ext, wd):
    x2 = ext["x2"]
    x2bnd = ext["x2bnd"]
    outs = _mamba_A_outs(P, TLOC)
    xn2_bf = P.dram("xn2_bf", [8, 128, TLOC], BF16)
    xm_ext = P.dram("xm_ext", [8, 128, TLOC + 3], F32)
    psf = [P.psum(f"ps{i}", [128, 512], F32) for i in range(6)]
    onesb = P.sbuf("onesb", [128, 1], BF16)
    ones1f = P.sbuf("ones1f", [1, 128], F32)
    cwc_sb = P.sbuf("cwc_sb", [128, 64], F32)
    cbc_sb = P.sbuf("cbc_sb", [128, 16], F32)
    bdtc_sb = P.sbuf("bdtc_sb", [128, 16], F32)
    tco = P.step("vector", lambda v: v.memset(onesb[:, :], 1.0), tag="c0")
    tco2 = P.step("vector", lambda v: v.memset(ones1f[:, :], 1.0), tag="c1")
    tcl = P.dma("sync", lambda eng: [
        eng.dma_start(out=cwc_sb[:, :], in_=wd["cwc"]),
        eng.dma_start(out=cbc_sb[:, :], in_=wd["cbc"]),
        eng.dma_start(out=bdtc_sb[:, :], in_=wd["bdtc"]),
    ], tag="consts")
    c0 = [tco, tco2, tcl]
    with P.scope():
        ln_phase(P, tag="ln2", x_hbm=x2, KB=8, TLOC=TLOC, out_bf=xn2_bf,
                 out_f32=xm_ext[:, :, 3:], kind="plain", psb=psf[0:4],
                 ones_col_bf=onesb, ones1_f32=ones1f, deps=c0)
    f1 = P.fence()
    with P.scope():
        def srcf(eng, bcol):
            return [eng.dma_start(out=bcol[:, :], in_=x2bnd)]
        bout, tb = bnd_norm3(P, tag="bn2", src_fn=srcf, kind="plain",
                             psb=psf[0:4], ones_col_bf=onesb,
                             ones1_f32=ones1f, deps=f1)
        tst = P.dma("sync", lambda eng: [
            eng.dma_start(out=xm_ext[k, :, 0:3], in_=bout[:, 3 * k:3 * k + 3])
            for k in range(8)], deps=tb, tag="bn2_st")
    f2 = P.fence()
    mamba_A(P, tag="b0", xm_ext=xm_ext, TLOC=TLOC, w_ina=wd["w_ina"],
            w_inb=wd["w_inb"], w_x=wd["w_x"], w_dt_h=wd["w_dt"],
            cwc=cwc_sb, cbc=cbc_sb, bdtc=bdtc_sb, alog_h=wd["alog"],
            psf=psf, onesb=onesb, ones1f=ones1f, outs=outs, deps=f2)
    return dict(outs, xn2_bf=xn2_bf)


def emit_mk(P, TLOC, ext, wd):
    TPX = min(2048, TLOC)
    ins = {nm: ext[nm] for nm in
           ("delta_p", "w_p", "u_p", "res_p", "dblf_p", "hend4", "cae4",
            "hl3_nb", "cae_nb", "Cl3", "unl3", "resnl3", "rjm", "caesel",
            "rjm_nb", "caesel_nb", "bflag")}
    outs = _mamba_A_outs(P, TLOC)
    cb_b = P.dram("cb_b", [16, 128, TLOC], BF16)
    y_bf = P.dram("y_bf", [16, 128, TLOC], BF16)
    xm_ext = P.dram("xm_ext2", [8, 128, TLOC + 3], F32)
    psf = [P.psum(f"ps{i}", [128, 512], F32) for i in range(6)]
    onesb = P.sbuf("onesb", [128, 1], BF16)
    ones1f = P.sbuf("ones1f", [1, 128], F32)
    cwc_sb = P.sbuf("cwc_sb", [128, 64], F32)
    cbc_sb = P.sbuf("cbc_sb", [128, 16], F32)
    bdtc_sb = P.sbuf("bdtc_sb", [128, 16], F32)
    dpc_sb = P.sbuf("dpc_sb", [128, 16], F32)
    rjm_sb = P.sbuf("rjm_sb", [128, 4], F32)
    csel_sb = P.sbuf("csel_sb", [128, 16], F32)
    rjmn_sb = P.sbuf("rjmn_sb", [128, 4], F32)
    cseln_sb = P.sbuf("cseln_sb", [128, 16], F32)
    tco = P.step("vector", lambda v: v.memset(onesb[:, :], 1.0), tag="c0")
    tco2 = P.step("vector", lambda v: v.memset(ones1f[:, :], 1.0), tag="c1")
    tcl = P.dma("sync", lambda eng: [
        eng.dma_start(out=cwc_sb[:, :], in_=wd["cwc"]),
        eng.dma_start(out=cbc_sb[:, :], in_=wd["cbc"]),
        eng.dma_start(out=bdtc_sb[:, :], in_=wd["bdtc"]),
        eng.dma_start(out=dpc_sb[:, :], in_=wd["dpc"]),
        eng.dma_start(out=rjm_sb[:, :], in_=ins["rjm"]),
        eng.dma_start(out=csel_sb[:, :], in_=ins["caesel"]),
        eng.dma_start(out=rjmn_sb[:, :], in_=ins["rjm_nb"]),
        eng.dma_start(out=cseln_sb[:, :], in_=ins["caesel_nb"]),
    ], tag="consts")
    c0 = [tco, tco2, tcl]
    f0 = bcast_bc(P, tag="bc", dblf=ins["dblf_p"],
                  dsts=[("B", outs["bb_b"]), ("C", cb_b)], TLOC=TLOC,
                  deps=c0)
    bins = dict(delta_f=ins["delta_p"], w_f=ins["w_p"], u_f=ins["u_p"],
                res_f=ins["res_p"], bb_b=outs["bb_b"], cb_b=cb_b,
                rjm_sb=rjm_sb, caesel_sb=csel_sb)
    carr4 = dict(hend4=ins["hend4"], cae4=ins["cae4"])
    fB = mamba_B(P, tag="pb", TLOC=TLOC, ins=bins, carr4=carr4,
                 alog_h=wd["alog"], dpc=dpc_sb, psf=psf, onesb=onesb,
                 ones1f=ones1f, y_bf=y_bf, deps=f0)
    with P.scope():
        stream_matmul(P, tag="wo", x_hbm=y_bf, w_hbm=wd["w_out"],
                      out_hbm=xm_ext[:, :, 3:], KB=16, M=1024, TLOC=TLOC,
                      TP=TPX, psb=psf[0:6], out_dt=F32, deps=fB)
    fW = P.fence()
    fH = mamba_halo(P, tag="hl", TLOC=TLOC, hl3_nb=ins["hl3_nb"],
                    cae_nb=ins["cae_nb"], hend4=ins["hend4"],
                    cae4=ins["cae4"], rjm_nb=rjmn_sb, csel_nb=cseln_sb,
                    Cl3=ins["Cl3"], unl3=ins["unl3"], resnl3=ins["resnl3"],
                    alog_h=wd["alog"], dpc=dpc_sb, bflag=ins["bflag"],
                    w_out=wd["w_out"], xm_ext_next=xm_ext, onesb=onesb,
                    ones1f=ones1f, psf=psf, deps=fW)
    mamba_A(P, tag="bk", xm_ext=xm_ext, TLOC=TLOC, w_ina=wd["w_ina"],
            w_inb=wd["w_inb"], w_x=wd["w_x"], w_dt_h=wd["w_dt"],
            cwc=cwc_sb, cbc=cbc_sb, bdtc=bdtc_sb, alog_h=wd["alog"],
            psf=psf, onesb=onesb, ones1f=ones1f, outs=outs, deps=fH)
    return outs


def emit_mfin(P, TLOC, ext, wd):
    TPX = min(2048, TLOC)
    ins = {nm: ext[nm] for nm in
           ("delta_p", "w_p", "u_p", "res_p", "dblf_p", "hend4", "cae4",
            "rjm", "caesel", "x2", "xn2_bf")}
    alog = wd["alog"]
    dpc_d = wd["dpc"]
    w_out = wd["w_out"]
    w_g = P.dram("w_g", [8, 128, 8], F32, kind="ExternalInput")
    bg_t = P.dram("bg_t", [128, 8], F32, kind="ExternalInput")
    w_1e = P.dram("w_1e", [8, 16, 8, 128, 128], BF16, kind="ExternalInput")
    b1c = P.dram("b1c", [128, 8 * 16], F32, kind="ExternalInput")
    w_2c = P.dram("w_2c", [8, 128, 128, 128], BF16, kind="ExternalInput")
    b2r = P.dram("b2r", [8, 1024], F32, kind="ExternalInput")
    identd = P.dram("identd", [128, 128], BF16, kind="ExternalInput")
    x3 = P.dram("x3", [8, 128, TLOC], F32)
    s2f = P.dram("s2f", [8, 128, TLOC], F32)
    cb_b = P.dram("cb_b", [16, 128, TLOC], BF16)
    y_bf = P.dram("y_bf", [16, 128, TLOC], BF16)
    bb_b = P.dram("bb_b", [16, 128, TLOC], BF16)
    geh = P.dram("geh", [TLOC // 128, 128, 8], F32)
    gefm = P.dram("gefm", [1, 8, TLOC], BF16)
    heb = P.dram("heb", [16, 128, TLOC], BF16)
    hega = P.dram("hega", [128, 128, TLOC], BF16)
    b2g = P.dram("b2g", [8, 128, TLOC], F32)
    g2f = P.dram("g2f", [8, 128, TLOC], F32)
    psf = [P.psum(f"ps{i}", [128, 512], F32) for i in range(6)]
    psbf_g = P.psum("psbf_g", [128, 512], BF16)
    onesb = P.sbuf("onesb", [128, 1], BF16)
    ones1f = P.sbuf("ones1f", [1, 128], F32)
    dpc_sb = P.sbuf("dpc_sb", [128, 16], F32)
    rjm_sb = P.sbuf("rjm_sb", [128, 4], F32)
    csel_sb = P.sbuf("csel_sb", [128, 16], F32)
    bg_sb = P.sbuf("bg_sb", [128, 8], F32)
    b1_sb = P.sbuf("b1_sb", [128, 8 * 16], F32)
    b2_sb = P.sbuf("b2_sb", [8, 1024], BF16)
    idb = P.sbuf("idb", [128, 128], BF16)
    tco = P.step("vector", lambda v: v.memset(onesb[:, :], 1.0), tag="c0")
    tco2 = P.step("vector", lambda v: v.memset(ones1f[:, :], 1.0), tag="c1")
    tcl = P.dma("sync", lambda eng: [
        eng.dma_start(out=dpc_sb[:, :], in_=dpc_d),
        eng.dma_start(out=rjm_sb[:, :], in_=ins["rjm"]),
        eng.dma_start(out=csel_sb[:, :], in_=ins["caesel"]),
        eng.dma_start(out=bg_sb[:, :], in_=bg_t),
        eng.dma_start(out=b1_sb[:, :], in_=b1c),
        eng.dma_start(out=idb[:, :], in_=identd),
    ], tag="consts")
    tcl2 = P.dma("gpsimd", lambda eng: [
        eng.dma_start(out=b2_sb[:, :], in_=b2r)], tag="consts2")
    c0 = [tco, tco2, tcl, tcl2]
    f0 = bcast_bc(P, tag="bc", dblf=ins["dblf_p"],
                  dsts=[("B", bb_b), ("C", cb_b)], TLOC=TLOC, deps=c0)
    bins = dict(delta_f=ins["delta_p"], w_f=ins["w_p"], u_f=ins["u_p"],
                res_f=ins["res_p"], bb_b=bb_b, cb_b=cb_b,
                rjm_sb=rjm_sb, caesel_sb=csel_sb)
    carr4 = dict(hend4=ins["hend4"], cae4=ins["cae4"])
    fB = mamba_B(P, tag="pb", TLOC=TLOC, ins=bins, carr4=carr4,
                 alog_h=alog, dpc=dpc_sb, psf=psf, onesb=onesb,
                 ones1f=ones1f, y_bf=y_bf, deps=f0)
    with P.scope():
        stream_matmul(P, tag="wo", x_hbm=y_bf, w_hbm=w_out, out_hbm=s2f,
                      KB=16, M=1024, TLOC=TLOC, TP=TPX, psb=psf[0:6],
                      out_dt=F32, deps=fB)
    fW = P.fence()
    with P.scope():
        xsb = P.sbuf("gt_x", [128, 8 * 128], BF16)
        wg_sb = P.sbuf("gt_w", [128, 8 * 8], BF16)
        gt = P.sbuf("gt_g", [128, 8], F32)
        gtmp = P.sbuf("gt_t", [128, 8], F32)
        grow = P.sbuf("gt_r", [128, 6], F32)
        twg = P.dma("gpsimd", lambda eng: [eng.dma_start(
            out=wg_sb[:, :].rearrange("p (k x) -> p k x", k=8),
            in_=w_g.rearrange("k p x -> p k x"))], deps=fW, tag="gt_wl")
        gate = list(fW)
        for tt in range(TLOC // 128):
            t0 = tt * 128
            tl = P.dma("sync", lambda eng, t0=t0: [
                eng.dma_start(out=xsb[:, k * 128:(k + 1) * 128],
                              in_=ins["xn2_bf"][k, :, t0:t0 + 128])
                for k in range(8)], deps=gate, tag=f"gt_l{tt}")

            def mmg(t_, tt=tt):
                last = None
                for k in range(8):
                    last = t_.matmul(psf[4][0:128, 0:8],
                                     xsb[:, k * 128:(k + 1) * 128],
                                     wg_sb[:, k * 8:(k + 1) * 8],
                                     start=(k == 0), stop=(k == 7))
                return last
            tmm = P.step("tensor", mmg, deps=[tl, twg] + gate,
                         tag=f"gt_mm{tt}")
            tlg = P.chain("vector", [
                lambda v: v.tensor_add(gt[:, :], psf[4][0:128, 0:8],
                                       bg_sb[:, :]),
                lambda v: v.tensor_scalar_mul(gt[:, :], gt[:, :], 0.2),
                lambda v: v.reduce_max(grow[:, 0:1], gt[:, :], axis=AXX),
                lambda v: v.tensor_scalar_mul(grow[:, 0:1], grow[:, 0:1],
                                              -1.0),
            ], deps=[tmm, tcl], tag=f"gt_lg{tt}")
            tex = P.step("scalar", lambda sc: sc.activation(
                out=gt[:, :], in_=gt[:, :], func=AF.Exp,
                bias=grow[:, 0:1], accum_out=grow[:, 1:2]), deps=[tlg],
                tag=f"gt_ex{tt}")
            tpr = P.chain("vector", [
                lambda v: v.reciprocal(grow[:, 2:3], grow[:, 1:2]),
                lambda v: v.tensor_scalar_mul(gt[:, :], gt[:, :],
                                              grow[:, 2:3]),
                lambda v: v.reduce_max(grow[:, 0:1], gt[:, :], axis=AXX),
                lambda v: v.tensor_single_scalar(
                    out=gtmp[:, :], in_=gt[:, :], scalar=grow[:, 0:1],
                    op=ALU.is_ge),
                lambda v: v.scalar_tensor_tensor(
                    out=gtmp[:, :], in0=gtmp[:, :], scalar=-1e30,
                    in1=gt[:, :], op0=ALU.mult, op1=ALU.add),
                lambda v: v.reduce_max(grow[:, 3:4], gtmp[:, :], axis=AXX),
                lambda v: v.tensor_single_scalar(
                    out=gtmp[:, :], in_=gt[:, :], scalar=grow[:, 3:4],
                    op=ALU.is_ge),
                lambda v: v.tensor_add(grow[:, 4:5], grow[:, 0:1],
                                       grow[:, 3:4]),
                lambda v: v.tensor_scalar_add(grow[:, 4:5], grow[:, 4:5],
                                              1e-9),
                lambda v: v.reciprocal(grow[:, 5:6], grow[:, 4:5]),
                lambda v: v.tensor_mul(gt[:, :], gt[:, :], gtmp[:, :]),
                lambda v: v.tensor_scalar_mul(gt[:, :], gt[:, :],
                                              grow[:, 5:6]),
            ], deps=[tex], tag=f"gt_pr{tt}")
            tst = P.dma("sync", lambda eng, tt=tt: [
                eng.dma_start(out=geh[tt], in_=gt[:, :])], deps=[tpr],
                tag=f"gt_st{tt}")
            gate = [tst, tpr]
    fG = P.fence()
    with P.scope():
        gtb = P.sbuf("gtr_b", [128, 8], BF16)
        gte = P.sbuf("gtr_e", [8, 128], BF16)
        prev = list(fG)
        for tt in range(TLOC // 128):
            tl = P.dma("gpsimd", lambda eng, tt=tt: [
                eng.dma_start(out=gtb[:, :], in_=geh[tt])], deps=prev,
                tag=f"gtr_l{tt}")
            ttr = P.step("tensor", lambda t, tt=tt: t.transpose(
                psbf_g[0:8, 0:128], gtb[:, :], idb[:, :]),
                deps=[tl, tcl] + prev, tag=f"gtr_t{tt}")
            tev = P.step("vector", lambda v: v.tensor_copy(
                gte[:, :], psbf_g[0:8, 0:128]), deps=[ttr],
                tag=f"gtr_e{tt}")
            tst = P.dma("sync", lambda eng, tt=tt: [
                eng.dma_start(out=gefm[0, :, tt * 128:(tt + 1) * 128],
                              in_=gte[:, :])], deps=[tev], tag=f"gtr_s{tt}")
            prev = [tst, tev]
    fT = P.fence()
    for e in range(8):
        with P.scope():
            stream_matmul(P, tag=f"ex{e}", x_hbm=ins["xn2_bf"],
                          w_hbm=w_1e[e], out_hbm=heb, KB=8, M=2048,
                          TLOC=TLOC, TP=TPX, psb=psf[0:6], act=AF.Gelu,
                          bias_sb=b1_sb[:, 16 * e:16 * e + 16], deps=fT)
        fT = P.fence()
        with P.scope():
            geb = P.sbuf(f"geb{e}", [128, TLOC], BF16)
            het = P.sbuf(f"het{e}", [128, TLOC], BF16)
            hot = P.sbuf(f"hot{e}", [128, TLOC], BF16)
            tgb = P.dma("gpsimd", lambda eng, e=e: [
                eng.dma_start(out=geb[:, :],
                              in_=gefm[0, e:e + 1,
                                       :].partition_broadcast(128))],
                deps=fT, tag=f"geb{e}")
            prev = list(fT) + [tgb]
            for m in range(16):
                tl = P.dma("sync", lambda eng, m=m: [
                    eng.dma_start(out=het[:, :], in_=heb[m])], deps=prev,
                    tag=f"egl{e}_{m}")
                tm = P.step("vector", lambda v: v.tensor_mul(
                    hot[:, :], het[:, :], geb[:, :]), deps=[tl, tgb],
                    tag=f"egm{e}_{m}")
                tst = P.dma("gpsimd", lambda eng, m=m, e=e: [
                    eng.dma_start(out=hega[16 * e + m], in_=hot[:, :])],
                    deps=[tm], tag=f"egs{e}_{m}")
                prev = [tm, tst]
        fT = P.fence()
    with P.scope():
        ger = P.sbuf("b2_ge", [8, TLOC], BF16)
        evb = P.sbuf("b2_ev", [128, 512], F32)
        tgl = P.dma("sync", lambda eng: [eng.dma_start(out=ger[:, :],
                                                       in_=gefm[0])],
                    deps=fT, tag="b2_gl")
        prev = list(fT) + [tgl]
        for mo in range(8):
            for tch in range(TLOC // 512):
                tmm = P.step("tensor", lambda t, mo=mo, tch=tch: t.matmul(
                    psf[4][0:128, :], b2_sb[:, mo * 128:(mo + 1) * 128],
                    ger[:, tch * 512:(tch + 1) * 512], start=True,
                    stop=True), deps=prev, tag=f"b2_mm{mo}_{tch}")
                tev = P.step("vector", lambda v: v.tensor_copy(
                    evb[:, :], psf[4][0:128, :]), deps=[tmm],
                    tag=f"b2_ev{mo}_{tch}")
                tst = P.dma("sync", lambda eng, mo=mo, tch=tch: [
                    eng.dma_start(out=b2g[mo, :, tch * 512:(tch + 1) * 512],
                                  in_=evb[:, :])], deps=[tev],
                    tag=f"b2_st{mo}_{tch}")
                prev = [tev, tst]
    fb2 = P.fence()
    with P.scope():
        badd = P.sbuf("w2_badd", [128, 512], F32)
        fadd = P.sbuf("w2_fout", [128, 512], F32)
        evx_prev = [list(fb2)]

        def ev_extra(P_, t_base, m, t, ev_tile, ms, dd):
            t0 = t_base + t * 512
            tlb = P_.dma("sync", lambda eng, m=m, t0=t0: [
                eng.dma_start(out=badd[:, :],
                              in_=b2g[m, :, t0:t0 + 512])],
                deps=dd + evx_prev[0], tag=f"w2x_l{m}_{t0}")
            ta = P_.step("vector", lambda v: v.tensor_add(
                fadd[:, :], ev_tile[0:ms, :], badd[:, :]),
                deps=[tlb] + dd + evx_prev[0], tag=f"w2x_a{m}_{t0}")
            ts = P_.dma("sync", lambda eng, m=m, t0=t0: [
                eng.dma_start(out=g2f[m, :, t0:t0 + 512], in_=fadd[:, :])],
                deps=[ta], tag=f"w2x_s{m}_{t0}")
            evx_prev[0] = [ta, ts]
            return [ta, ts]
        stream_matmul(P, tag="w2c", x_hbm=hega, w_hbm=w_2c, out_hbm=None,
                      KB=128, M=1024, TLOC=TLOC, TP=TPX, psb=psf[0:6],
                      out_dt=F32, store_out=False, extra_evac=ev_extra,
                      deps=fb2)
    fw2 = P.fence()
    with P.scope():
        def em_x3(P_, k, bb, ob, dd):
            tg = P_.step("scalar", lambda sc: sc.activation(
                out=ob, in_=bb[2], func=AF.Silu), deps=dd, tag=f"x3_g{k}")
            tm = P_.step("vector", lambda v: v.tensor_mul(ob, ob, bb[1]),
                         deps=[tg], tag=f"x3_m{k}")
            ta = P_.step("vector", lambda v: v.tensor_add(ob, ob, bb[0]),
                         deps=[tm], tag=f"x3_a{k}")
            return [ta]
        ew_phase(P, tag="x3", srcs=[(ins["x2"], F32), (s2f, F32),
                                    (g2f, F32)], KB=8, TLOC=TLOC,
                 emit=em_x3, out_specs=[(x3, "sync")], deps=fw2)
    return dict(x3=x3)


def host_prep_mamba_w(inputs):
    W_in = np.asarray(inputs["L2_attn_W_in"])
    d = {}
    d["w_ina"] = wblk(W_in[:, :2048]).astype(BF)
    d["w_inb"] = wblk(W_in[:, 2048:]).astype(BF)
    d["w_x"] = wblk(np.asarray(inputs["L2_attn_W_x"])).astype(BF)
    d["w_dt"] = np.ascontiguousarray(
        np.asarray(inputs["L2_attn_W_dt"]).reshape(64, 16, 128)
        .transpose(1, 0, 2)).astype(BF)
    cw = np.asarray(inputs["L2_attn_conv_w"])[:, 0, :]
    d["cwc"] = np.ascontiguousarray(
        cw.reshape(16, 128, 4).transpose(1, 0, 2).reshape(128, 64))
    d["cbc"] = np.ascontiguousarray(
        np.asarray(inputs["L2_attn_conv_b"]).reshape(16, 128).T)
    d["bdtc"] = np.ascontiguousarray(
        np.asarray(inputs["L2_attn_b_dt"]).reshape(16, 128).T)
    d["alog"] = np.ascontiguousarray(
        np.asarray(inputs["L2_attn_A_log"]).reshape(16, 128, 16))
    d["dpc"] = np.ascontiguousarray(
        np.asarray(inputs["L2_attn_Dp"]).reshape(16, 128).T)
    d["w_out"] = wblk(np.asarray(inputs["L2_attn_W_out"])).astype(BF)
    return d


def _bnd3_cols(tokens3):
    z = np.zeros((128, 24), np.float32)
    for j in range(3):
        b = tokens3[j].reshape(8, 128)
        for k in range(8):
            z[:, k * 3 + j] = b[k]
    return z


def _mk_masks(myi):
    rjm = np.zeros(4, np.float32)
    csel = np.zeros((4, 4), np.float32)
    for j in range(4):
        if j < myi:
            rjm[j] = 1.0
        for l in range(4):
            if j < l < myi:
                csel[j, l] = 1.0
    return (np.tile(rjm, (128, 1)).astype(np.float32),
            np.tile(csel.reshape(-1), (128, 1)).astype(np.float32))


def host_prep_mfin(inputs):
    d = {}
    d["w_g"] = np.ascontiguousarray(
        np.asarray(inputs["L2_ffn_Wg"]).reshape(8, 128, 8))
    d["bg_t"] = np.tile(np.asarray(inputs["L2_ffn_bg"]),
                        (128, 1)).astype(np.float32)
    d["w_1e"] = np.stack([wblk(np.asarray(inputs["L2_ffn_W1"])[e])
                          for e in range(8)]).astype(BF)
    b1 = np.asarray(inputs["L2_ffn_b1"])
    d["b1c"] = np.ascontiguousarray(np.concatenate(
        [b1[e].reshape(16, 128).T for e in range(8)], axis=1))
    W2 = np.asarray(inputs["L2_ffn_W2"])
    d["w_2c"] = wblk(W2.reshape(16384, 1024)).astype(BF)
    d["b2r"] = np.ascontiguousarray(np.asarray(inputs["L2_ffn_b2"]))
    d["identd"] = np.eye(128).astype(BF)
    return d



def emit_l3a(P, TLOC, ext):
    TPX = min(2048, TLOC)
    x3 = ext["x3"]
    w_qkv = P.dram("w_qkv", [24, 8, 128, 128], BF16, kind="ExternalInput")
    w_gate = P.dram("w_gate", [8, 128, 8], F32, kind="ExternalInput")
    bg_t = P.dram("bg_t", [128, 8], F32, kind="ExternalInput")
    xn3_bf = P.dram("xn3_bf", [8, 128, TLOC], BF16)
    qkv = P.dram("qkv", [24, 128, TLOC], BF16)
    el = P.dram("el", [TLOC // 128, 128, 8], F32)
    expd = P.dram("expd", [1, 8], F32)
    psf = [P.psum(f"ps{i}", [128, 512], F32) for i in range(6)]
    onesb = P.sbuf("onesb", [128, 1], BF16)
    ones1f = P.sbuf("ones1f", [1, 128], F32)
    bg_sb = P.sbuf("bg_sb", [128, 8], F32)
    tco = P.step("vector", lambda v: v.memset(onesb[:, :], 1.0), tag="c0")
    tco2 = P.step("vector", lambda v: v.memset(ones1f[:, :], 1.0), tag="c1")
    tcl = P.dma("sync", lambda eng: [eng.dma_start(out=bg_sb[:, :],
                                                   in_=bg_t)], tag="cl")
    c0 = [tco, tco2, tcl]
    with P.scope():
        ln_phase(P, tag="ln3", x_hbm=x3, KB=8, TLOC=TLOC, out_bf=xn3_bf,
                 out_f32=None, kind="plain", psb=psf[0:4],
                 ones_col_bf=onesb, ones1_f32=ones1f, deps=c0)
    f1 = P.fence()
    with P.scope():
        stream_matmul(P, tag="qkv", x_hbm=xn3_bf, w_hbm=w_qkv, out_hbm=qkv,
                      KB=8, M=3072, TLOC=TLOC, TP=TPX, psb=psf[0:6],
                      deps=f1)
    f2 = P.fence()
    with P.scope():
        xsb = P.sbuf("g_x", [128, 8 * 128], BF16)
        wg_sb = P.sbuf("g_w", [128, 64], BF16)
        gt = P.sbuf("g_g", [128, 8], F32)
        gbf = P.sbuf("g_b", [128, 8], BF16)
        erow = P.sbuf("g_e", [1, 8], F32)
        twg = P.dma("gpsimd", lambda eng: [eng.dma_start(
            out=wg_sb[:, :].rearrange("p (k x) -> p k x", k=8),
            in_=w_gate.rearrange("k p x -> p k x"))], deps=f2, tag="g_wl")
        gate = list(f2)
        accg = []
        for tt in range(TLOC // 128):
            t0 = tt * 128
            tl = P.dma("sync", lambda eng, t0=t0: [
                eng.dma_start(out=xsb[:, k * 128:(k + 1) * 128],
                              in_=xn3_bf[k, :, t0:t0 + 128])
                for k in range(8)], deps=gate, tag=f"g_l{tt}")

            def mmg(t_, tt=tt):
                last = None
                for k in range(8):
                    last = t_.matmul(psf[4][0:128, 0:8],
                                     xsb[:, k * 128:(k + 1) * 128],
                                     wg_sb[:, k * 8:(k + 1) * 8],
                                     start=(k == 0), stop=(k == 7))
                return last
            tmm = P.step("tensor", mmg, deps=[tl, twg] + gate,
                         tag=f"g_mm{tt}")
            tad = P.step("vector", lambda v: v.tensor_add(
                gt[:, :], psf[4][0:128, 0:8], bg_sb[:, :]),
                deps=[tmm, tcl], tag=f"g_ad{tt}")
            tex = P.step("scalar", lambda sc: sc.activation(
                out=gt[:, :], in_=gt[:, :], func=AF.Exp), deps=[tad],
                tag=f"g_ex{tt}")
            tst = P.dma("sync", lambda eng, tt=tt: [
                eng.dma_start(out=el[tt], in_=gt[:, :])], deps=[tex],
                tag=f"g_st{tt}")
            tbf = P.step("vector", lambda v: v.tensor_copy(gbf[:, :],
                                                           gt[:, :]),
                         deps=[tex], tag=f"g_bf{tt}")
            tps = P.step("tensor", lambda t_, tt=tt: t_.matmul(
                psf[5][0:1, 0:8], onesb[:, 0:1], gbf[:, :],
                start=(tt == 0), stop=(tt == TLOC // 128 - 1)),
                deps=[tbf] + accg, tag=f"g_ps{tt}")
            accg = [tps]
            gate = [tst, tbf, tps]
        tev = P.step("vector", lambda v: v.tensor_copy(
            erow[0:1, :], psf[5][0:1, 0:8]), deps=accg, tag="g_ev")
        tse = P.dma("sync", lambda eng: [eng.dma_start(out=expd,
                                                       in_=erow[0:1, :])],
                    deps=[tev], tag="g_se")
    return dict(xn3_bf=xn3_bf, qkv=qkv, el=el, expd=expd)


def host_prep_l3a(inputs):
    d = {}
    wq = np.asarray(inputs["L3_attn_wq"])
    wk = np.asarray(inputs["L3_attn_wk"])
    wv = np.asarray(inputs["L3_attn_wv"])
    d["w_qkv"] = wblk(np.concatenate([wq, wk, wv], axis=1)).astype(BF)
    d["w_gate"] = np.ascontiguousarray(
        np.asarray(inputs["L3_ffn_gate_w"]).reshape(8, 128, 8))
    d["bg_t"] = np.tile(np.asarray(inputs["L3_ffn_gate_b"]),
                        (128, 1)).astype(np.float32)
    return d


def emit_l3b(P, TLOC, ext):
    TPX = min(2048, TLOC)
    NW = TLOC // 1024
    qkv = ext["qkv"]
    xn3_bf = ext["xn3_bf"]
    gk_b = ext["gk_b"]
    gv_b = ext["gv_b"]
    expd4 = ext["expd4"]
    el = ext["el"]
    maskd = P.dram("maskd", [4, 128, 512], F32, kind="ExternalInput")
    identd = P.dram("identd", [128, 128], BF16, kind="ExternalInput")
    w_o = P.dram("w_o", [8, 8, 128, 128], BF16, kind="ExternalInput")
    w_1 = P.dram("w_1", [8, 8, 8, 128, 128], BF16, kind="ExternalInput")
    w_2 = P.dram("w_2", [8, 8, 8, 128, 128], BF16, kind="ExternalInput")
    w_3 = P.dram("w_3", [8, 8, 8, 128, 128], BF16, kind="ExternalInput")
    b_1 = P.dram("b_1", [128, 64], F32, kind="ExternalInput")
    b_2 = P.dram("b_2", [128, 64], F32, kind="ExternalInput")
    b_3 = P.dram("b_3", [128, 64], F32, kind="ExternalInput")
    s3f = P.dram("s3f", [8, 128, TLOC], F32)
    vpart = P.dram("vpart", [128, 64], F32)
    indfm = P.dram("indfm", [1, 8, TLOC], BF16)
    att = P.dram("att", [8, 128, TLOC], BF16)
    wtk_h = P.dram("wtk_h", [TLOC // 128, 128, 8], F32)
    wtfm = P.dram("wtfm", [1, 8, TLOC], BF16)
    ye = P.dram("ye", [8, 128, TLOC], BF16)
    h1_ = P.dram("h1_", [8, 128, TLOC], BF16)
    h2_ = P.dram("h2_", [8, 128, TLOC], BF16)
    psf = [P.psum(f"ps{i}", [128, 512], F32) for i in range(6)]
    psbf = P.psum("psbf", [128, 512], BF16)
    onesb = P.sbuf("onesb", [128, 1], BF16)
    ones1f = P.sbuf("ones1f", [1, 128], F32)
    idb = P.sbuf("idb", [128, 128], BF16)
    b1_sb = P.sbuf("b1_sb", [128, 64], F32)
    b2_sb = P.sbuf("b2_sb", [128, 64], F32)
    b3_sb = P.sbuf("b3_sb", [128, 64], F32)
    tco = P.step("vector", lambda v: v.memset(onesb[:, :], 1.0), tag="c0")
    tco2 = P.step("vector", lambda v: v.memset(ones1f[:, :], 1.0), tag="c1")
    tcl = P.dma("sync", lambda eng: [
        eng.dma_start(out=idb[:, :], in_=identd),
        eng.dma_start(out=b1_sb[:, :], in_=b_1),
        eng.dma_start(out=b2_sb[:, :], in_=b_2),
        eng.dma_start(out=b3_sb[:, :], in_=b_3),
    ], tag="cl")
    c0 = [tco, tco2, tcl]
    with P.scope():
        qh = P.sbuf("at_q", [64, 1024], BF16)
        kh = P.sbuf("at_k", [64, 1024], BF16)
        vh = P.sbuf("at_v", [64, 1024], BF16)
        gkh = P.sbuf("at_gk", [64, 256], BF16)
        gvh = P.sbuf("at_gv", [64, 256], BF16)
        vt = P.sbuf("at_vt", [128, 512], BF16)
        gvt = P.sbuf("at_gvt", [128, 128], BF16)
        msk = P.sbuf("at_m", [128, 4 * 512], F32)
        pl = P.sbuf("at_p", [128, 1024], BF16)
        pg = P.sbuf("at_pg", [128, 256], BF16)
        pt = P.sbuf("at_pt", [128, 128], BF16)
        rows = P.sbuf("at_r", [128, 8], F32)
        ao = P.sbuf("at_o", [64, TLOC], BF16)
        tml = P.dma("sync", lambda eng: [
            eng.dma_start(out=msk[:, j * 512:(j + 1) * 512], in_=maskd[j])
            for j in range(4)], deps=c0, tag="at_ml")
        hg = list(c0) + [tml]
        for h in range(16):
            ktile, krow = 8 + h // 2, (h % 2) * 64
            vtile = 16 + h // 2
            tgl = P.dma("gpsimd", lambda eng, h=h: [
                eng.dma_start(out=gkh[:, :],
                              in_=gk_b[h // 2, (h % 2) * 64:(h % 2) * 64 + 64, :]),
                eng.dma_start(out=gvh[:, :],
                              in_=gv_b[h // 2, (h % 2) * 64:(h % 2) * 64 + 64, :]),
            ], deps=hg, tag=f"at_gl{h}")
            tgt = []
            for c in range(2):
                t1 = P.step("tensor", lambda t, c=c: t.transpose(
                    psbf[0:128, 0:64], gvh[:, c * 128:(c + 1) * 128],
                    idb[0:64, 0:64]), deps=[tgl] + hg + tgt,
                    tag=f"at_gt{h}_{c}")
                t2 = P.step("vector", lambda v, c=c: v.tensor_copy(
                    gvt[:, c * 64:(c + 1) * 64], psbf[0:128, 0:64]),
                    deps=[t1], tag=f"at_gte{h}_{c}")
                tgt = [t2]
            wg = hg + tgt + [tgl]
            for w in range(NW):
                w0 = w * 1024
                tql = P.dma("sync", lambda eng, h=h, w0=w0: [
                    eng.dma_start(out=qh[:, :],
                                  in_=qkv[h // 2, (h % 2) * 64:(h % 2) * 64 + 64,
                                          w0:w0 + 1024]),
                    eng.dma_start(out=kh[:, :],
                                  in_=qkv[8 + h // 2, (h % 2) * 64:(h % 2) * 64 + 64,
                                          w0:w0 + 1024]),
                    eng.dma_start(out=vh[:, :],
                                  in_=qkv[16 + h // 2, (h % 2) * 64:(h % 2) * 64 + 64,
                                          w0:w0 + 1024]),
                ], deps=wg, tag=f"at_l{h}_{w}")
                tvt = []
                for c in range(8):
                    t1 = P.step("tensor", lambda t, c=c: t.transpose(
                        psbf[0:128, 64:128], vh[:, c * 128:(c + 1) * 128],
                        idb[0:64, 0:64]), deps=[tql] + wg + tvt,
                        tag=f"at_vt{h}_{w}_{c}")
                    t2 = P.step("vector", lambda v, c=c: v.tensor_copy(
                        vt[:, c * 64:(c + 1) * 64], psbf[0:128, 64:128]),
                        deps=[t1], tag=f"at_vte{h}_{w}_{c}")
                    tvt = [t2]
                qg = [tql] + tvt
                for qi in range(8):
                    qsl = slice(qi * 128, (qi + 1) * 128)
                    nkb = qi // 4 + 1
                    tS = P.step("tensor", lambda t, qsl=qsl, nkb=nkb: [
                        t.matmul(psf[kb][0:128, :], qh[:, qsl],
                                 kh[:, kb * 512:(kb + 1) * 512],
                                 start=True, stop=True)
                        for kb in range(nkb)][-1], deps=qg,
                        tag=f"at_S{h}_{w}_{qi}")
                    dkb = qi // 4
                    tM = P.step("vector", lambda v, dkb=dkb, qi=qi:
                                v.tensor_tensor(
                                    out=psf[dkb][0:128, :],
                                    in0=psf[dkb][0:128, :],
                                    in1=msk[:, (qi % 4) * 512:
                                             (qi % 4 + 1) * 512],
                                    op=ALU.add),
                                deps=[tS, tml], tag=f"at_M{h}_{w}_{qi}")
                    tSg = P.step("tensor", lambda t, qsl=qsl: t.matmul(
                        psf[2][0:128, 0:256], qh[:, qsl], gkh[:, :],
                        start=True, stop=True), deps=qg + [tgl, tM],
                        tag=f"at_Sg{h}_{w}_{qi}")
                    rmax = P.chain("vector", ([
                        lambda v: v.reduce_max(rows[:, 0:1],
                                               psf[0][0:128, :], axis=AXX)]
                        if nkb == 1 else [
                        lambda v: v.reduce_max(rows[:, 0:1],
                                               psf[0][0:128, :], axis=AXX),
                        lambda v: v.reduce_max(rows[:, 1:2],
                                               psf[1][0:128, :], axis=AXX),
                        lambda v: v.tensor_max(rows[:, 0:1], rows[:, 0:1],
                                               rows[:, 1:2])]) + [
                        lambda v: v.tensor_scalar_mul(rows[:, 2:3],
                                                      rows[:, 0:1], -0.125),
                    ], deps=[tM], tag=f"at_rm{h}_{w}_{qi}")
                    exps = []
                    for kb in range(nkb):
                        te = P.step("scalar", lambda sc, kb=kb: sc.activation(
                            out=pl[:, kb * 512:(kb + 1) * 512],
                            in_=psf[kb][0:128, :], func=AF.Exp,
                            scale=P.cval(0.125), bias=rows[:, 2:3],
                            accum_out=rows[:, 3 + kb:4 + kb]),
                            deps=[rmax], tag=f"at_e{h}_{w}_{qi}_{kb}")
                        exps.append(te)
                    rg = P.chain("vector", [
                        lambda v: v.reduce_max(rows[:, 5:6],
                                               psf[2][0:128, 0:256],
                                               axis=AXX),
                        lambda v: v.tensor_scalar_mul(rows[:, 6:7],
                                                      rows[:, 5:6], -0.125),
                    ], deps=[tSg], tag=f"at_rg{h}_{w}_{qi}")
                    teg = P.step("scalar", lambda sc: sc.activation(
                        out=pg[:, :], in_=psf[2][0:128, 0:256], func=AF.Exp,
                        scale=P.cval(0.125), bias=rows[:, 6:7],
                        accum_out=rows[:, 7:8]), deps=[rg],
                        tag=f"at_eg{h}_{w}_{qi}")
                    tnl = P.chain("vector", ([
                        lambda v: v.tensor_add(rows[:, 3:4], rows[:, 3:4],
                                               rows[:, 4:5])]
                        if nkb == 2 else []) + [
                        lambda v: v.reciprocal(rows[:, 3:4], rows[:, 3:4]),
                        lambda v, qi=qi: v.tensor_scalar_mul(
                            pl[:, 0:(qi + 1) * 128], pl[:, 0:(qi + 1) * 128],
                            rows[:, 3:4]),
                        lambda v: v.reciprocal(rows[:, 7:8], rows[:, 7:8]),
                        lambda v: v.tensor_scalar_mul(pg[:, :], pg[:, :],
                                                      rows[:, 7:8]),
                    ], deps=exps + [teg], tag=f"at_n{h}_{w}_{qi}")
                    pvg = []
                    for kc in range(qi + 1):
                        tt1 = P.step("tensor", lambda t, qsl=qsl, kc=kc:
                                     t.transpose(
                                         psbf[0:128, 128:256],
                                         pl[:, kc * 128:(kc + 1) * 128],
                                         idb[:, :]), deps=[tnl] + pvg,
                                     tag=f"at_pt{h}_{w}_{qi}_{kc}")
                        tt2 = P.step("vector", lambda v: v.tensor_copy(
                            pt[:, :], psbf[0:128, 128:256]), deps=[tt1],
                            tag=f"at_pte{h}_{w}_{qi}_{kc}")
                        tt3 = P.step("tensor", lambda t, kc=kc, qi=qi:
                                     t.matmul(
                                         psf[3][0:64, 0:128],
                                         vt[:, kc * 64:(kc + 1) * 64],
                                         pt[:, :], start=(kc == 0),
                                         stop=False), deps=[tt2] + tvt,
                                     tag=f"at_pv{h}_{w}_{qi}_{kc}")
                        pvg = [tt3]
                    for gc in range(2):
                        tt1 = P.step("tensor", lambda t, gc=gc: t.transpose(
                            psbf[0:128, 128:256],
                            pg[:, gc * 128:(gc + 1) * 128], idb[:, :]),
                            deps=[tnl] + pvg, tag=f"at_gt2{h}_{w}_{qi}_{gc}")
                        tt2 = P.step("vector", lambda v: v.tensor_copy(
                            pt[:, :], psbf[0:128, 128:256]), deps=[tt1],
                            tag=f"at_gte2{h}_{w}_{qi}_{gc}")
                        tt3 = P.step("tensor", lambda t, gc=gc: t.matmul(
                            psf[3][0:64, 0:128],
                            gvt[:, gc * 64:(gc + 1) * 64], pt[:, :],
                            start=False, stop=(gc == 1)),
                            deps=[tt2] + tgt, tag=f"at_gv2{h}_{w}_{qi}_{gc}")
                        pvg = [tt3]
                    tev = P.step("vector", lambda v, w0=w0, qsl=qsl, qi=qi:
                                 v.tensor_copy(
                                     ao[:, w0 + qi * 128:w0 + (qi + 1) * 128],
                                     psf[3][0:64, 0:128]), deps=pvg,
                                 tag=f"at_ev{h}_{w}_{qi}")
                    qg = [tev, tnl] + pvg
                wg = qg
            tsa = P.dma("sync", lambda eng, h=h: [
                eng.dma_start(out=att[h // 2, (h % 2) * 64:(h % 2) * 64 + 64, :],
                              in_=ao[:, :])], deps=wg, tag=f"at_sa{h}")
            hg = P.red(wg + [tsa])
    fA = P.fence()
    with P.scope():
        stream_matmul(P, tag="op", x_hbm=att, w_hbm=w_o, out_hbm=s3f,
                      KB=8, M=1024, TLOC=TLOC, TP=TPX, psb=psf[0:6],
                      out_dt=F32, deps=fA)
    fO = P.fence()
    with P.scope():
        dsum = P.sbuf("wt_d", [1, 8], F32)
        dtmp = P.sbuf("wt_dt", [1, 8], F32)
        drb = P.sbuf("wt_db", [128, 8], F32)
        elt = P.sbuf("wt_el", [128, 8], F32)
        wtt = P.sbuf("wt_w", [128, 8], F32)
        wtb = P.sbuf("wt_wb", [128, 8], BF16)
        ind = P.sbuf("wt_i", [128, 8], BF16)
        rows2 = P.sbuf("wt_r", [128, 4], F32)
        gte2 = P.sbuf("wt_t", [8, 128], BF16)
        tdl = P.dma("sync", lambda eng: [
            eng.dma_start(out=dsum[0:1, :], in_=expd4[0])], deps=fO,
            tag="wt_dl")
        tac = tdl
        for j in range(1, 4):
            tl_ = P.dma("gpsimd", lambda eng, j=j: [
                eng.dma_start(out=dtmp[0:1, :], in_=expd4[j])], deps=[tac],
                tag=f"wt_d{j}")
            tac = P.step("vector", lambda v: v.tensor_add(
                dsum[0:1, :], dsum[0:1, :], dtmp[0:1, :]), deps=[tl_, tac],
                tag=f"wt_a{j}")
        trc = P.step("vector", lambda v: v.reciprocal(dsum[0:1, :],
                                                      dsum[0:1, :]),
                     deps=[tac], tag="wt_rc")
        tbc = P.step("tensor", lambda t: t.matmul(
            psf[4][0:128, 0:8], ones1f[0:1, :], dsum[0:1, :], start=True,
            stop=True), deps=[trc], tag="wt_bc")
        tbe = P.step("vector", lambda v: v.tensor_copy(drb[:, :],
                                                       psf[4][0:128, 0:8]),
                     deps=[tbc], tag="wt_be")
        prev = [tbe]
        for tt in range(TLOC // 128):
            tl = P.dma("sync", lambda eng, tt=tt: [
                eng.dma_start(out=elt[:, :], in_=el[tt])], deps=prev,
                tag=f"wt_l{tt}")
            tw = P.chain("vector", [
                lambda v: v.tensor_mul(wtt[:, :], elt[:, :], drb[:, :]),
                lambda v: v.reduce_max(rows2[:, 0:1], wtt[:, :], axis=AXX),
                lambda v: v.tensor_single_scalar(
                    out=ind[:, :].bitcast(BF16) if False else wtb[:, :],
                    in_=wtt[:, :], scalar=rows2[:, 0:1], op=ALU.is_ge),
                lambda v: v.scalar_tensor_tensor(
                    out=elt[:, :], in0=wtb[:, :], scalar=-1e30,
                    in1=wtt[:, :], op0=ALU.mult, op1=ALU.add),
                lambda v: v.reduce_max(rows2[:, 1:2], elt[:, :], axis=AXX),
                lambda v: v.tensor_single_scalar(
                    out=ind[:, :], in_=wtt[:, :], scalar=rows2[:, 1:2],
                    op=ALU.is_ge),
                lambda v: v.tensor_mul(wtb[:, :], wtt[:, :], ind[:, :]),
            ], deps=[tl, tbe], tag=f"wt_c{tt}")
            tt1 = P.step("tensor", lambda t: t.transpose(
                psbf[0:8, 0:128], wtb[:, :], idb[:, :]), deps=[tw],
                tag=f"wt_t1{tt}")
            tt2 = P.step("vector", lambda v: v.tensor_copy(
                gte2[:, :], psbf[0:8, 0:128]), deps=[tt1], tag=f"wt_t2{tt}")
            ts1 = P.dma("sync", lambda eng, tt=tt: [
                eng.dma_start(out=wtfm[0, :, tt * 128:(tt + 1) * 128],
                              in_=gte2[:, :])], deps=[tt2], tag=f"wt_s1{tt}")
            tt3 = P.step("tensor", lambda t: t.transpose(
                psbf[0:8, 0:128], ind[:, :], idb[:, :]), deps=[tw, tt2],
                tag=f"wt_t3{tt}")
            tt4 = P.step("vector", lambda v: v.tensor_copy(
                gte2[:, :], psbf[0:8, 0:128]), deps=[tt3, ts1],
                tag=f"wt_t4{tt}")
            ts2 = P.dma("sync", lambda eng, tt=tt: [
                eng.dma_start(out=indfm[0, :, tt * 128:(tt + 1) * 128],
                              in_=gte2[:, :])], deps=[tt4], tag=f"wt_s2{tt}")
            prev = [ts2, tt4, tw]
    fW = P.fence()
    vp_sb = P.sbuf("vp", [128, 64], F32)
    tvz = P.step("vector", lambda v: v.memset(vp_sb[:, :], 0.0), deps=fW,
                 tag="vp0")
    fE = fW + [tvz]
    for e in range(8):
        with P.scope():
            stream_matmul(P, tag=f"y1_{e}", x_hbm=xn3_bf, w_hbm=w_1[e],
                          out_hbm=h1_, KB=8, M=1024, TLOC=TLOC, TP=TPX,
                          psb=psf[0:6], act=AF.Relu,
                          bias_sb=b1_sb[:, 8 * e:8 * e + 8], deps=fE)
        fE = P.fence()
        with P.scope():
            stream_matmul(P, tag=f"y2_{e}", x_hbm=h1_, w_hbm=w_2[e],
                          out_hbm=h2_, KB=8, M=1024, TLOC=TLOC, TP=TPX,
                          psb=psf[0:6], act=AF.Relu,
                          bias_sb=b2_sb[:, 8 * e:8 * e + 8], deps=fE)
        fE = P.fence()
        with P.scope():
            stream_matmul(P, tag=f"y3_{e}", x_hbm=h2_, w_hbm=w_3[e],
                          out_hbm=ye, KB=8, M=1024, TLOC=TLOC, TP=TPX,
                          psb=psf[0:6], act=AF.Relu,
                          bias_sb=b3_sb[:, 8 * e:8 * e + 8], deps=fE)
        fE = P.fence()
        with P.scope():
            wte = P.sbuf(f"vp_w{e}", [128, TLOC], BF16)
            yet = P.sbuf(f"vp_y{e}", [128, TLOC], BF16)
            tmpe = P.sbuf(f"vp_t{e}", [128, TLOC], F32)
            red = P.sbuf(f"vp_r{e}", [128, 1], F32)
            twl = P.dma("gpsimd", lambda eng, e=e: [
                eng.dma_start(out=wte[:, :],
                              in_=wtfm[0, e:e + 1, :].partition_broadcast(128))],
                deps=fE, tag=f"vp_wl{e}")
            prev = list(fE) + [twl]
            for k in range(8):
                tl = P.dma("sync", lambda eng, k=k: [
                    eng.dma_start(out=yet[:, :], in_=ye[k])], deps=prev,
                    tag=f"vp_l{e}_{k}")
                tc = P.chain("vector", [
                    lambda v: v.tensor_mul(tmpe[:, :], yet[:, :], wte[:, :]),
                    lambda v: v.reduce_sum(red[:, 0:1], tmpe[:, :],
                                           axis=AXX),
                    lambda v, e=e, k=k: v.tensor_add(
                        vp_sb[:, e * 8 + k:e * 8 + k + 1],
                        vp_sb[:, e * 8 + k:e * 8 + k + 1], red[:, 0:1]),
                ], deps=[tl, twl, tvz], tag=f"vp_c{e}_{k}")
                prev = [tc]
        fE = P.fence()
    tvs = P.dma("sync", lambda eng: [eng.dma_start(out=vpart,
                                                   in_=vp_sb[:, :])],
                deps=fE, tag="vp_st")
    return dict(s3f=s3f, vpart=vpart, indfm=indfm)


def host_prep_l3b(inputs, TLOC=TL1):
    d = {}
    maskd = np.zeros((4, 128, 512), np.float32)
    for j in range(4):
        qoff = j * 128
        for q in range(128):
            maskd[j, q, qoff + q + 1:] = -1e30
    d["maskd"] = maskd
    d["identd"] = np.eye(128).astype(BF)
    d["w_o"] = wblk(np.asarray(inputs["L3_attn_wo"])).astype(BF)
    for i, nm in [(1, "w_1"), (2, "w_2"), (3, "w_3")]:
        W = np.asarray(inputs[f"L3_ffn_w{i}"])
        d[nm] = np.stack([wblk(W[e]) for e in range(8)]).astype(BF)
        bb_ = np.asarray(inputs[f"L3_ffn_b{i}"])
        d[f"b_{i}"] = np.ascontiguousarray(np.concatenate(
            [bb_[e].reshape(8, 128).T for e in range(8)], axis=1))
    return d


def emit_l3c(P, TLOC, ext):
    x3 = ext["x3"]
    s3f = ext["s3f"]
    indfm = ext["indfm"]
    vpart8 = ext["vpart8"]
    identd = P.dram("identd", [128, 128], BF16, kind="ExternalInput")
    x4 = P.dram("x4", [8, 128, TLOC], F32, kind="ExternalOutput")
    psf = [P.psum(f"ps{i}", [128, 512], F32) for i in range(6)]
    psbf = P.psum("psbf", [128, 512], BF16)
    idb = P.sbuf("idb", [128, 128], BF16)
    vp = P.sbuf("vp", [128, 64], F32)
    vp2 = P.sbuf("vp2", [128, 64], F32)
    vT = P.sbuf("vT", [8, 8 * 128], BF16)
    vk = P.sbuf("vk", [128, 8], BF16)
    ge = P.sbuf("ge", [8, TLOC], BF16)
    tcl = P.dma("sync", lambda eng: [
        eng.dma_start(out=idb[:, :], in_=identd),
        eng.dma_start(out=vp[:, :], in_=vpart8[0]),
        eng.dma_start(out=ge[:, :], in_=indfm[0]),
    ], tag="cl")
    tac = tcl
    for j in range(1, 8):
        tl_ = P.dma("gpsimd", lambda eng, j=j: [
            eng.dma_start(out=vp2[:, :], in_=vpart8[j])], deps=[tac],
            tag=f"va{j}")
        tac = P.step("vector", lambda v: v.tensor_add(
            vp[:, :], vp[:, :], vp2[:, :]), deps=[tl_, tac], tag=f"vs{j}")
    tvs = []
    for k in range(8):
        tb = P.step("vector", lambda v, k=k: v.tensor_copy(
            vk[:, :], vp[:, k::8]), deps=[tac] + tvs, tag=f"vb{k}")
        t1 = P.step("tensor", lambda t: t.transpose(
            psbf[0:8, 0:128], vk[:, :], idb[:, :]), deps=[tb] + tvs,
            tag=f"vt{k}")
        t2 = P.step("vector", lambda v, k=k: v.tensor_copy(
            vT[:, k * 128:(k + 1) * 128], psbf[0:8, 0:128]), deps=[t1],
            tag=f"ve{k}")
        tvs = [t2]
    fV = P.fence()
    with P.scope():
        xb = P.sbuf("xb", [128, 512], F32)
        sb_ = P.sbuf("sb_", [128, 512], F32)
        gb = P.sbuf("gb", [128, 512], F32)
        prev = list(fV)
        for k in range(8):
            for tch in range(TLOC // 512):
                t0 = tch * 512
                tl = P.dma("sync", lambda eng, k=k, t0=t0: [
                    eng.dma_start(out=xb[:, :], in_=x3[k, :, t0:t0 + 512]),
                    eng.dma_start(out=sb_[:, :], in_=s3f[k, :, t0:t0 + 512]),
                ], deps=prev, tag=f"xl{k}_{tch}")
                tmm = P.step("tensor", lambda t, k=k, t0=t0: t.matmul(
                    psf[0][0:128, :], vT[0:8, k * 128:(k + 1) * 128],
                    ge[0:8, t0:t0 + 512], start=True, stop=True),
                    deps=prev + tvs + [tcl], tag=f"gm{k}_{tch}")
                tg = P.step("scalar", lambda sc: sc.activation(
                    out=gb[:, :], in_=psf[0][0:128, :], func=AF.Silu),
                    deps=[tmm], tag=f"gs{k}_{tch}")
                tx = P.chain("vector", [
                    lambda v: v.tensor_mul(gb[:, :], gb[:, :], sb_[:, :]),
                    lambda v: v.tensor_add(gb[:, :], gb[:, :], xb[:, :]),
                ], deps=[tg, tl], tag=f"gx{k}_{tch}")
                tst = P.dma("sync", lambda eng, k=k, t0=t0: [
                    eng.dma_start(out=x4[k, :, t0:t0 + 512],
                                  in_=gb[:, :])], deps=[tx],
                    tag=f"xs{k}_{tch}")
                prev = [tst, tx, tmm]
    return dict(x4=x4)



RG4 = [[0, 1, 2, 3], [4, 5, 6, 7]]
RG8 = [[0, 1, 2, 3, 4, 5, 6, 7]]


def build_full(TLOC=TL1):
    P = Prog("full")
    P.start_block()
    selprev = P.dram("selprev", [128, 4], F32, kind="ExternalInput")
    mb_rjm = P.dram("mb_rjm", [128, 4], F32, kind="ExternalInput")
    mb_caesel = P.dram("mb_caesel", [128, 16], F32, kind="ExternalInput")
    mb_rjm_nb = P.dram("mb_rjm_nb", [128, 4], F32, kind="ExternalInput")
    mb_caesel_nb = P.dram("mb_caesel_nb", [128, 16], F32,
                          kind="ExternalInput")
    mb_bflag = P.dram("mb_bflag", [128, 1], F32, kind="ExternalInput")
    selp_sb = P.sbuf("selp_sb", [128, 4], F32)
    zcol = P.sbuf("zcol", [128, 8], F32)
    tsel = P.dma("sync", lambda eng: [
        eng.dma_start(out=selp_sb[:, :], in_=selprev)], tag="selld")
    tz = P.step("vector", lambda v: v.memset(zcol[:, :], 0.0), tag="z0")

    def bar():
        P.barrier_all(P.red(P.fence()))

    def ag(nm, src, shape, dt, rg=RG4):
        g = P.dram(nm, [len(rg[0])] + list(shape), dt)
        P.step("gpsimd", lambda e: e.collective_compute(
            "AllGather", ALU.bypass, replica_groups=rg,
            ins=[src], outs=[g]), deps=P.red(P.fence()), tag=nm)
        return g

    def sel_combine(nm, g, items, f32cast=True):
        deps = P.red(P.fence())
        with P.scope():
            for i, (srcf, stores, Pp, C) in enumerate(items):
                a4 = P.sbuf(f"{nm}_a{i}", [Pp, 4 * C], F32)
                o = P.sbuf(f"{nm}_o{i}", [Pp, C], F32)
                tl = P.dma("gpsimd", lambda eng, srcf=srcf, a4=a4, C=C: [
                    eng.dma_start(out=a4[:, j * C:(j + 1) * C],
                                  in_=srcf(j)) for j in range(4)],
                    deps=deps, tag=f"{nm}_l{i}")
                fns = [lambda v, a4=a4, o=o, C=C, Pp=Pp: v.tensor_scalar_mul(
                    o[:, :], a4[:, 0:C], selp_sb[0:Pp, 0:1])]
                for j in range(1, 4):
                    fns.append(lambda v, j=j, a4=a4, o=o, C=C, Pp=Pp:
                               v.scalar_tensor_tensor(
                                   out=o[:, :], in0=a4[:, j * C:(j + 1) * C],
                                   scalar=selp_sb[0:Pp, j:j + 1], in1=o[:, :],
                                   op0=ALU.mult, op1=ALU.add))
                tc = P.chain("vector", fns, deps=[tl, tsel],
                             tag=f"{nm}_c{i}")
                for si, (oap, c0, c1) in enumerate(stores):
                    P.dma("gpsimd", lambda eng, oap=oap, o=o, c0=c0, c1=c1:
                          [eng.dma_start(out=oap, in_=o[:, c0:c1])],
                          deps=[tc], tag=f"{nm}_s{i}_{si}")

    P.pfx = "A_"
    with P.scope():
        ra = emit_l0a(P, TLOC)
    P.pfx = "g0_"
    s_all = ag("s_all", ra["send"], [16, 16, 64], F32)
    meanp_all = ag("meanp_all", ra["meanp"], [128, 8], F32)
    bar()
    P.pfx = "B_"
    with P.scope():
        rb = emit_l0b(P, TLOC, dict(x=ra["x"], xn_bf=ra["xn_bf"],
                                    xn_f=ra["xn_f"], qr=ra["qr"],
                                    y_loc=ra["y_loc"], s_all=s_all,
                                    meanp_all=meanp_all))
    P.pfx = "g1_"
    x1 = rb["x1"]
    x1l = P.dram("x1l", [128, 8], F32)
    P.dma("gpsimd", lambda eng: [
        eng.dma_start(out=x1l[:, kb:kb + 1], in_=x1[kb, :, TLOC - 1:TLOC],
                      allow_slow_non_contiguous=True) for kb in range(8)],
        deps=P.red(P.fence()), tag="x1l_pk")
    x1l_g = ag("x1l_g", x1l, [128, 8], F32)
    x1bnd = P.dram("x1bnd", [128, 8], F32)
    sel_combine("x1b", x1l_g,
                [(lambda j: x1l_g[j], [(x1bnd, 0, 8)], 128, 8)])
    bar()
    P.pfx = "C_"
    with P.scope():
        rc = emit_l1a(P, TLOC, dict(x1=x1, x1bnd=x1bnd))
    P.pfx = "g2_"
    carry = rc["carry"]
    mui = P.dram("mui", [128, 8], F32)
    P.dma("gpsimd", lambda eng: [
        eng.dma_start(out=carry[kb, :, 3:4], in_=zcol[:, 0:1],
                      allow_slow_non_contiguous=True) for kb in range(8)],
        deps=P.red(P.fence()) + [tz], tag="cz")
    P.dma("sync", lambda eng: [
        eng.dma_start(out=mui[:, kb:kb + 1], in_=carry[kb, :, 2:3],
                      allow_slow_non_contiguous=True) for kb in range(8)],
        deps=P.red(P.fence()), tag="mui_pk")
    carr = ag("carr", carry, [8, 128, 4], F32)
    bar()
    P.pfx = "D_"
    with P.scope():
        rd = emit_l1b(P, TLOC, dict(x1=x1, xn1_bf=rc["xn1_bf"],
                                    v_f=rc["v_f"], r_f=rc["r_f"],
                                    S_loc=rc["S_loc"], Z_loc=rc["Z_loc"],
                                    P_loc=rc["P_loc"], carr=carr, mui=mui))
    P.pfx = "g3_"
    x2 = rd["x2"]
    x2t = P.dram("x2t", [128, 24], F32)
    P.dma("gpsimd", lambda eng: [
        eng.dma_start(out=x2t[:, kb * 3:kb * 3 + 3],
                      in_=x2[kb, :, TLOC - 3:TLOC],
                      allow_slow_non_contiguous=True) for kb in range(8)],
        deps=P.red(P.fence()), tag="x2t_pk")
    x2t_g = ag("x2t_g", x2t, [128, 24], F32)
    x2bnd = P.dram("x2bnd", [128, 24], F32)
    sel_combine("x2b", x2t_g,
                [(lambda j: x2t_g[j], [(x2bnd, 0, 24)], 128, 24)])
    bar()
    P.pfx = "M_"
    wd = _mamba_weight_drams(P)
    P.pfx = "M0_"
    with P.scope():
        rm = emit_m0(P, TLOC, dict(x2=x2, x2bnd=x2bnd), wd)
    xn2_bf = rm["xn2_bf"]

    mbm = dict(rjm=mb_rjm, caesel=mb_caesel, rjm_nb=mb_rjm_nb,
               caesel_nb=mb_caesel_nb, bflag=mb_bflag)

    def mamba_boundary(pfx, po):
        P.pfx = pfx
        hend4 = ag("hend_g", po["hend"], [16, 128, 16], BF16)
        cae4 = ag("cae_g", po["cae"], [16, 128, 4], F32)
        hl3_g = ag("hl3_g", po["hl3"], [16, 128, 48], BF16)
        nbt = P.dram("nbt", [16, 128, 6], F32)
        ctl = P.dram("ctl", [1, 48], F32)
        P.dma("gpsimd", lambda eng: [
            eng.dma_start(out=nbt[:, :, 0:3],
                          in_=po["u_f"][:, :, TLOC - 3:TLOC],
                          allow_slow_non_contiguous=True),
            eng.dma_start(out=nbt[:, :, 3:6],
                          in_=po["res_f"][:, :, TLOC - 3:TLOC],
                          allow_slow_non_contiguous=True),
            eng.dma_start(out=ctl[0:1, :],
                          in_=po["dblf"][0:1, 80:96, TLOC - 3:TLOC],
                          allow_slow_non_contiguous=True),
        ], deps=P.red(P.fence()), tag="nbt_pk")
        nbt_g = ag("nbt_g", nbt, [16, 128, 6], F32)
        ctl_g = ag("ctl_g", ctl, [1, 48], F32)
        hl3_nb = P.dram("hl3_nb", [16, 128, 48], BF16)
        cae_nb = P.dram("cae_nb", [16, 128, 4], F32)
        unl3 = P.dram("unl3", [16, 128, 3], F32)
        resnl3 = P.dram("resnl3", [16, 128, 3], F32)
        Cl3 = P.dram("Cl3", [1, 48], F32)
        items = []
        for i in range(16):
            items.append((lambda j, i=i: hl3_g[j, i],
                          [(hl3_nb[i], 0, 48)], 128, 48))
            items.append((lambda j, i=i: cae4[j, i],
                          [(cae_nb[i], 0, 4)], 128, 4))
            items.append((lambda j, i=i: nbt_g[j, i],
                          [(unl3[i], 0, 3), (resnl3[i], 3, 6)], 128, 6))
        items.append((lambda j: ctl_g[j], [(Cl3, 0, 48)], 1, 48))
        sel_combine("nbc", hl3_g, items)
        bar()
        return dict(delta_p=po["delta_f"], w_p=po["w_f"], u_p=po["u_f"],
                    res_p=po["res_f"], dblf_p=po["dblf"], hend4=hend4,
                    cae4=cae4, hl3_nb=hl3_nb, cae_nb=cae_nb, Cl3=Cl3,
                    unl3=unl3, resnl3=resnl3, **mbm)

    for k in (1, 2, 3):
        extk = mamba_boundary(f"m{k}g_", rm)
        P.pfx = f"M{k}_"
        with P.scope():
            rm = emit_mk(P, TLOC, extk, wd)
    P.pfx = "mfg_"
    hend4 = ag("hend_g", rm["hend"], [16, 128, 16], BF16)
    cae4 = ag("cae_g", rm["cae"], [16, 128, 4], F32)
    bar()
    P.pfx = "MF_"
    with P.scope():
        rmf = emit_mfin(P, TLOC, dict(
            delta_p=rm["delta_f"], w_p=rm["w_f"], u_p=rm["u_f"],
            res_p=rm["res_f"], dblf_p=rm["dblf"], hend4=hend4, cae4=cae4,
            rjm=mb_rjm, caesel=mb_caesel, x2=x2, xn2_bf=xn2_bf), wd)
    x3 = rmf["x3"]
    bar()
    P.pfx = "E_"
    with P.scope():
        re_ = emit_l3a(P, TLOC, dict(x3=x3))
    P.pfx = "g4_"
    qkv = re_["qkv"]
    gkvb = P.dram("gkvb", [16, 128, 256], BF16)
    P.dma("gpsimd", lambda eng: [
        eng.dma_start(out=gkvb[:, :, :], in_=qkv[8:24, :, 0:256],
                      allow_slow_non_contiguous=True)],
        deps=P.red(P.fence()), tag="gkv_pk")
    gkv_g = ag("gkv_g", gkvb, [16, 128, 256], BF16)
    expd4 = ag("expd_g", re_["expd"], [1, 8], F32)
    gk_b = P.dram("gk_b", [8, 128, 256], BF16)
    gv_b = P.dram("gv_b", [8, 128, 256], BF16)
    P.dma("gpsimd", lambda eng: [
        eng.dma_start(out=gk_b, in_=gkv_g[0, 0:8]),
        eng.dma_start(out=gv_b, in_=gkv_g[0, 8:16])],
        deps=P.red(P.fence()), tag="gkv_x")
    bar()
    P.pfx = "F_"
    with P.scope():
        rf = emit_l3b(P, TLOC, dict(qkv=qkv, xn3_bf=re_["xn3_bf"],
                                    gk_b=gk_b, gv_b=gv_b, expd4=expd4,
                                    el=re_["el"]))
    P.pfx = "g5_"
    vpart8 = ag("vpart8", rf["vpart"], [128, 64], F32, rg=RG8)
    bar()
    P.pfx = "G_"
    with P.scope():
        rg_ = emit_l3c(P, TLOC, dict(x3=x3, s3f=rf["s3f"],
                                     indfm=rf["indfm"], vpart8=vpart8))
    P.pfx = ""
    P.step("sync", lambda e: e.nop(), deps=P.red(P.fence()), tag="fin")
    P.close()
    build_full.cst = P.cst_array()
    return P.nc



_PROG_CACHE = {}


def _get_prog(name, builder):
    if name not in _PROG_CACHE:
        _PROG_CACHE[name] = builder()
    return _PROG_CACHE[name]


def _assemble(res_list, key, TLOC=TL1):
    out = np.empty((B, 4 * TLOC, 1024), np.float32)
    for b in range(B):
        for j in range(4):
            r = np.asarray(res_list[4 * b + j][key], np.float32)
            out[b, j * TLOC:(j + 1) * TLOC] = r.reshape(1024, TLOC).T
    return out


def host_prep_full_shared(inputs, TLOC=TL1):
    d = {}

    def put(pfx, dd):
        for k, v in dd.items():
            d[pfx + k] = v
    put("A_", host_prep_l0a_w(inputs, TLOC))
    put("B_", host_prep_l0b(inputs, TLOC))
    put("C_", host_prep_l1a(inputs, TLOC))
    put("D_", host_prep_l1b(inputs, TLOC))
    put("M_", host_prep_mamba_w(inputs))
    put("MF_", host_prep_mfin(inputs))
    put("E_", host_prep_l3a(inputs))
    put("F_", host_prep_l3b(inputs, TLOC))
    d["G_identd"] = np.eye(128).astype(BF)
    d["cst"] = build_full.cst
    return d


def host_prep_full_core(inputs, core, TLOC=TL1):
    mych = core % 4
    d = {}
    for k, v in host_prep_l0a_core(inputs, core, TLOC).items():
        d["A_" + k] = v
    rj = np.zeros(4, np.float32)
    mk = np.zeros(4, np.float32)
    for j in range(4):
        if j < mych:
            rj[j] = TLOC * (mych - 1 - j)
            mk[j] = 1.0
    rjs = np.tile(rj, (128, 1)).astype(np.float32)
    rjmm = np.tile(mk, (128, 1)).astype(np.float32)
    d["B_rjs"] = rjs
    d["B_rjm"] = rjmm
    d["D_rjs"] = rjs
    d["D_rjm"] = rjmm
    sel = np.zeros((128, 4), np.float32)
    if mych > 0:
        sel[:, mych - 1] = 1.0
    d["selprev"] = sel
    rjm, caesel = _mk_masks(mych)
    rjmn, caeseln = _mk_masks(max(mych - 1, 0))
    d["mb_rjm"] = rjm
    d["mb_caesel"] = caesel
    d["mb_rjm_nb"] = rjmn
    d["mb_caesel_nb"] = caeseln
    d["mb_bflag"] = np.full((128, 1), 1.0 if mych > 0 else 0.0,
                            np.float32)
    return d


def kernel(**inputs):
    TLOC = TL1
    cores = list(range(NCORES))
    nc = _get_prog("full", build_full)
    shared = host_prep_full_shared(inputs, TLOC)
    im = [dict(shared, **host_prep_full_core(inputs, c, TLOC))
          for c in cores]
    res = run_bass_kernel_spmd(nc, im, core_ids=cores).results
    return _assemble(res, "G_x4", TLOC)
